# version 24
# baseline (speedup 1.0000x reference)
import math
from contextlib import ExitStack

import numpy as np

import concourse.bass as bass
import concourse.mybir as mybir
from concourse.bass_core_rust import IndirectOffsetOnAxis
from concourse.bass_utils import run_bass_kernel_spmd

F32 = mybir.dt.float32
BF16 = mybir.dt.bfloat16
U8 = mybir.dt.uint8
I32 = mybir.dt.int32
AF = mybir.ActivationFunctionType
ALU = mybir.AluOpType
AX = mybir.AxisListType

NCORES = 8
T, DM, H, HD, FF, E = 2048, 1024, 16, 64, 2048, 8
NT, ND, NF = T // 128, DM // 128, FF // 128
CAP = float(T * NCORES // E)
CEXP = 16.0
TWO_PI = 2.0 * math.pi
BIS_ITERS = 40
LN_EPS = 1e-5

DBG = []


class Ctr:
    def __init__(self, h, step=1):
        self.h = h
        self.n = 0
        self.step = step

    def inc(self, instr, k=None):
        k = self.step if k is None else k
        instr.then_inc(self.h, k)
        self.n += k
        return instr



class AutoSync:

    _PASS = ("wait_ge", "dma_start", "dma_start_transpose", "collective_compute",
             "indirect_dma_start", "sem_inc", "wait_op")

    def __init__(self, eng, ctr):
        self._eng = eng
        self._ctr = ctr

    def __getattr__(self, name):
        f = getattr(self._eng, name)
        if name.startswith("_") or name in AutoSync._PASS or not callable(f):
            return f
        eng, ctr = self._eng, self._ctr

        def wrapped(*a, **k):
            eng.wait_ge(ctr.h, ctr.n)
            ins = f(*a, **k)
            ctr.inc(ins)
            return ins

        return wrapped


class BlockWrap:
    def __init__(self, block, cV, cS, cP):
        self._b = block
        self._cV, self._cS, self._cP = cV, cS, cP

    def vector(self, f):
        return self._b.vector(lambda v: f(AutoSync(v, self._cV)))

    def scalar(self, f):
        return self._b.scalar(lambda s: f(AutoSync(s, self._cS)))

    def gpsimd(self, f):
        return self._b.gpsimd(lambda g: f(AutoSync(g, self._cP)))

    def tensor(self, f):
        return self._b.tensor(f)

    def sync(self, f):
        return self._b.sync(f)


def build_nc():
    nc = bass.Bass("TRN2", target_bir_lowering=False)

    x_hbm = nc.dram_tensor("x", [T, DM], F32, kind="ExternalInput").ap()
    wq_hbm = nc.dram_tensor("wq", [DM, DM], F32, kind="ExternalInput").ap()
    bq_hbm = nc.dram_tensor("bq", [1, DM], F32, kind="ExternalInput").ap()
    wk_hbm = nc.dram_tensor("wk", [DM, HD], F32, kind="ExternalInput").ap()
    bk_hbm = nc.dram_tensor("bk", [1, HD], F32, kind="ExternalInput").ap()
    wv_hbm = nc.dram_tensor("wv", [DM, HD], F32, kind="ExternalInput").ap()
    bv_hbm = nc.dram_tensor("bv", [1, HD], F32, kind="ExternalInput").ap()
    wo_hbm = nc.dram_tensor("wo", [DM, DM], F32, kind="ExternalInput").ap()
    bo_hbm = nc.dram_tensor("bo", [1, DM], F32, kind="ExternalInput").ap()
    wg_hbm = nc.dram_tensor("wg", [DM, E], F32, kind="ExternalInput").ap()
    w1_hbm = nc.dram_tensor("w1", [E, DM, FF], F32, kind="ExternalInput").ap()
    w2_hbm = nc.dram_tensor("w2", [E, FF, DM], F32, kind="ExternalInput").ap()
    out_hbm = nc.dram_tensor("out", [T, DM], F32, kind="ExternalOutput").ap()

    ag_in = nc.dram_tensor("ag_in", [128, E, NT], F32)
    wq_dram = nc.dram_tensor("wq_dram", [128, 8192], BF16)
    wo_dram = nc.dram_tensor("wo_dram", [128, 8192], BF16)
    xn_dram = nc.dram_tensor("xn_dram", [T, DM], BF16)
    xg_dram = nc.dram_tensor("xg_dram", [E * 512, DM], BF16)
    fg_dram = nc.dram_tensor("fg_dram", [E * 512, DM], BF16)
    ag_out = nc.dram_tensor("ag_out", [NCORES, 128, E, NT], F32, addr_space="Shared")

    dbg_specs = {
        "xnT": ([128, ND, T], BF16), "kT": ([128, T], BF16),
        "vstat": ([128, NT, 65], BF16), "probs": ([128, NT, E], F32),
        "tau": ([128, E], F32), "gates": ([128, NT, E], F32),
        "qT0": ([128, ND * 512], BF16), "attnT0": ([128, ND * 512], BF16),
        "acc0": ([128, NT // 2, DM], F32),
        "probs_all": ([128, E, 128], F32), "kvtm": ([128, NT, 128], F32),
    }
    dbg_hbm = {}
    for name in DBG:
        shp, dt = dbg_specs[name]
        dbg_hbm[name] = nc.dram_tensor("dbg_" + name, shp, dt, kind="ExternalOutput").ap()

    ctx = ExitStack()
    with ctx:
        def sb(name, shape, dt=F32):
            return ctx.enter_context(nc.sbuf_tensor(name, shape, dt))

        def psum_t(name, shape, dt=F32):
            return ctx.enter_context(nc.psum_tensor(name, shape, dt))

        def sem(name, step=1):
            return Ctr(ctx.enter_context(nc.semaphore(name)), step)

        xnT = sb("xnT", [128, ND, T], BF16)
        acc = sb("acc", [128, NT // 2, DM], F32)
        costab = sb("costab", [128, NT, 32], F32)
        sintab = sb("sintab", [128, NT, 32], F32)
        costab_bf = sb("costab_bf", [128, NT, 32], BF16)
        sintab_bf = sb("sintab_bf", [128, NT, 32], BF16)
        trimask = sb("trimask", [128, 128], BF16)
        idf32 = sb("idf32", [128, 128], F32)
        idbf = sb("idbf", [128, 128], BF16)
        ones_1x128_bf = sb("ones_1x128_bf", [1, 128], BF16)
        ones_1x128_f = sb("ones_1x128_f", [1, 128], F32)
        ones_128_bf = sb("ones_128_bf", [128, 128], BF16)
        negc = sb("negc", [128, 1], F32)
        kT = sb("kT", [128, T], BF16)
        vstat = sb("vstat", [128, NT, 65], BF16)
        kvtm = sb("kvtm", [128, NT, 128], F32)
        probs = sb("probs", [128, NT, E], F32)
        probs_all = sb("probs_all", [128, E, 128], F32)
        probs_em = sb("probs_em", [128, E, NT], F32)
        cmpb = sb("cmpb", [128, E, 128], BF16)
        cntpp = sb("cntpp", [128, E], BF16)
        tau = sb("tau", [128, E], F32)
        lo = sb("lo", [128, E], F32)
        hi = sb("hi", [128, E], F32)
        geF = sb("geF", [128, E], F32)
        giF = sb("giF", [128, E], F32)
        bis_d = sb("bis_d", [128, E], F32)
        geb = sb("geb", [128, E], U8)
        geb_inv = sb("geb_inv", [128, E], U8)
        gsel = sb("gsel", [128, NT, E], F32)
        gmax = sb("gmax", [128, NT], F32)
        gates = sb("gates", [128, NT, E], F32)
        wkv = sb("wkv", [128, ND, 128], BF16)
        wg_sb = sb("wg_sb", [128, ND, E], BF16)
        bq_row = sb("bq_row", [1, DM], BF16)
        bkv_row = sb("bkv_row", [1, 128], BF16)
        bo_row = sb("bo_row", [1, DM], BF16)
        rz_pair = sb("rz_pair", [1, 1024], BF16)
        zi_sb = sb("zi_sb", [64, 512], BF16)
        lmax = sb("lmax", [128, NT], F32)
        zrow = sb("zrow", [128, NT], F32)
        st_ssq = sb("st_ssq", [128, 4], F32)
        st_m = sb("st_m", [128, 4], F32)
        st_nm = sb("st_nm", [128, 4], F32)
        st_r = sb("st_r", [128, 4], F32)
        st_rs = sb("st_rs", [128, 4], F32)
        st_nmr = sb("st_nmr", [128, 4], F32)
        st_ln = sb("st_ln", [128, 4], F32)
        tgen = sb("tgen", [128, NT, 32], F32)
        tgen3 = sb("tgen3", [128, NT, 32], F32)
        tgen2 = sb("tgen2", [128, NT, 32], F32)

        tt_iota = sb("tt_iota", [128, NT], F32)
        tgen_i = tgen3[:, :, :].bitcast(I32)
        zln_sb = tgen[0:1, :, :].rearrange("p a b -> p (a b)")
        j_iota = sb("j_iota", [1, 32], F32)
        theta = sb("theta", [1, 32], F32)
        thetaR = sb("thetaR", [128, 32], F32)

        kvflat = kvtm[:, :, :].rearrange("p a b -> p (a b)")
        inv_i32 = kvflat[:, 0:16].bitcast(I32)
        g_tok = kvflat[:, 16:32]
        inv_f = kvflat[:, 32:48]
        slot_f = kvflat[:, 48:64]
        e_f = kvflat[:, 64:80]
        e512 = kvflat[:, 80:88]
        flagsE = kvflat[:, 128:192].bitcast(BF16)
        colpfx_sb = kvflat[:, 192:320]
        slotsel = kvflat[:, 320:448]
        flagsF = kvflat[:, 448:576]
        eidx8 = kvflat[:, 576:584]
        stricL = kvflat[:, 584:648].bitcast(BF16)
        xsc_sb = [kvflat[:, 768 + k * 512:1280 + k * 512].bitcast(BF16)
                  for k in range(2)]

        regA = sb("regA", [128, 16384], BF16)

        def carveA(off_bytes, n_elems, dt):
            if dt == BF16:
                return regA[:, off_bytes // 2: off_bytes // 2 + n_elems]
            return regA[:, off_bytes // 2:(off_bytes + n_elems * 4) // 2].bitcast(dt)

        xbuf = [carveA(0, 1024, F32), carveA(4096, 1024, F32),
                carveA(16384, 1024, F32), carveA(20480, 1024, F32)]
        dLx4 = None
        sq_scr = carveA(8192, 1024, F32)
        xnstage = [carveA(12288 + k * 2048, 1024, BF16) for k in range(2)]

        qTq = carveA(0, ND * 512, BF16)
        attnTq = carveA(8192, ND * 512, BF16)
        q_tm = [carveA(16384 + k * 2048, 1024, BF16) for k in range(2)]
        q_tm3 = [a.rearrange("p (h j) -> p h j", h=16) for a in q_tm]
        rope_scr = carveA(20480, 1024, BF16).rearrange("p (h j) -> p h j", h=16)
        krope_scr = carveA(0, 1024, F32).rearrange("p (h j) -> p h j", h=16)
        pbuf = [carveA(22528 + k * 2048, 1024, BF16) for k in range(4)]

        hT = carveA(0, NF * 1024, BF16)
        xg_sb = carveA(0, 4096, BF16)
        xgT = carveA(8192, 4096, BF16)
        hG = carveA(16384, 8192, BF16)

        ffg2 = [tgen2[:, :, :].rearrange("p a b -> p (a b)").bitcast(BF16),
                tgen3[:, :, :].rearrange("p a b -> p (a b)").bitcast(BF16)]
        fo_sb = ffg2
        xrel = [probs_all[:, :, :].rearrange("p e t -> p (e t)"),
                kvflat[:, 1024:2048]]

        regB = sb("regB", [128, 16384], BF16)
        w1_sb = regB.rearrange("p (d f) -> p d f", d=ND)

        regR = sb("regR", [128, 16384], BF16)
        wq_sb = regR[:, 0:8192].rearrange("p (d f) -> p d f", d=ND)
        wq_sb2 = regB[:, 0:8192].rearrange("p (d f) -> p d f", d=ND)
        wo_sb = regR[:, 8192:16384].rearrange("p (d f) -> p d f", d=ND)
        w2_sb = regR.rearrange("p (f d) -> p f d", f=NF)

        PA = psum_t("PA", [128, 1024], F32)
        PB = psum_t("PB", [128, 1024], F32)
        PC = psum_t("PC", [128, 1024], F32)
        PD = psum_t("PD", [128, 1024], F32)

        dLx = [sem("dLx%d" % i, 16) for i in range(4)]
        dWa = sem("dWa", 16)
        dWq = sem("dWq", 16)
        dWo = sem("dWo", 16)
        dW1 = sem("dW1", 16)
        dW2 = sem("dW2", 16)
        dPr = sem("dPr", 16)
        dXr = [sem("dXr0", 16), sem("dXr1", 16)]
        dOutT = [sem("dOutT%d" % i, 16) for i in range(8)]
        dDbg = sem("dDbg", 16)
        dStash = sem("dStash", 16)
        dWq2 = sem("dWq2", 16)
        dWo2 = sem("dWo2", 16)
        dXn = [sem("dXn0", 16), sem("dXn1", 16)]
        dPre = sem("dPre", 16)
        dXs = [sem("dXs0", 16), sem("dXs1", 16)]
        dSc = [sem("dSc0", 16), sem("dSc1", 16)]
        dXgA = [sem("dXg%d" % a, 16) for a in range(4)]
        dFgP = [sem("dFg0", 16), sem("dFg1", 16)]
        dFo = [sem("dFo0", 16), sem("dFo1", 16)]
        cT = sem("cT")
        cS = sem("cS")
        cV = sem("cV")
        cP = sem("cP")
        ccc = sem("ccc")

        block = BlockWrap(ctx.enter_context(nc.Block()), cV, cS, cP)

        @block.gpsimd
        def _(g):
            g.memset(ones_1x128_bf[:, :], 1.0)
            g.memset(ones_1x128_f[:, :], 1.0)
            g.memset(ones_128_bf[:, :], 1.0)
            g.memset(negc[:, :], -CEXP)
            g.memset(vstat[:, :, :], 1.0)
            g.memset(lo[:, :], 0.0)
            g.memset(hi[:, :], 1.0)
            g.memset(tau[:, :], 0.5)
            g.memset(idf32[:, :], 1.0)
            g.memset(idbf[:, :], 1.0)
            g.memset(trimask[:, :], 1.0)
            g.wait_ge(cP.h, cP.n)
            g.affine_select(out=idf32[:, :], in_=idf32[:, :], compare_op=ALU.is_equal,
                            fill=0.0, base=0, pattern=[[-1, 128]], channel_multiplier=1)
            g.affine_select(out=idbf[:, :], in_=idbf[:, :], compare_op=ALU.is_equal,
                            fill=0.0, base=0, pattern=[[-1, 128]], channel_multiplier=1)
            g.affine_select(out=trimask[:, :], in_=trimask[:, :], compare_op=ALU.is_ge,
                            fill=0.0, base=0, pattern=[[1, 128]], channel_multiplier=-1)
            g.iota(tt_iota[:, :], [[128, NT]], channel_multiplier=1,
                   allow_small_or_imprecise_dtypes=True)
            g.iota(j_iota[:, :], [[1, 32]], channel_multiplier=0,
                          allow_small_or_imprecise_dtypes=True)
            dWa.inc(g.dma_start(out=wkv[:, :, 0:64],
                                in_=wk_hbm.rearrange("(d p) f -> p d f", p=128)))
            dWa.inc(g.dma_start(out=wkv[:, :, 64:128],
                                in_=wv_hbm.rearrange("(d p) f -> p d f", p=128)))
            dWa.inc(g.dma_start(out=wg_sb[:, :, :],
                                in_=wg_hbm.rearrange("(d p) e -> p d e", p=128)))
            dWa.inc(g.dma_start(out=bkv_row[:, 0:64], in_=bk_hbm))
            dWa.inc(g.dma_start(out=bkv_row[:, 64:128], in_=bv_hbm))
        w_kv_done = [dWa.n]

        w_q_done = [None]
        w_o_done = [None]
        w1_arr = {}

        @block.scalar
        def _(s):
            s.wait_ge(cP.h, cP.n)
            s.activation(theta[:, :], j_iota[:, :], AF.Exp,
                                scale=-math.log(10000.0) / 32.0)

        @block.tensor
        def _(t):
            t.wait_ge(cS.h, cS.n)
            cT.inc(t.matmul(PA[:, 0:32], ones_1x128_f[:, :], theta[:, :],
                            start=True, stop=True))

        @block.vector
        def _(v):
            v.wait_ge(cT.h, cT.n)
            v.tensor_copy(thetaR[:, :], PA[:, 0:32])
            for a in range(NT):
                v.tensor_scalar_mul(tgen[:, a, :], thetaR[:, :], tt_iota[:, a:a + 1])
            v.tensor_scalar_mul(tgen[:, :, :], tgen[:, :, :], 1.0 / TWO_PI)
            v.tensor_copy(tgen_i, tgen[:, :, :])
            v.tensor_copy(tgen2[:, :, :], tgen_i)
            v.tensor_sub(tgen2[:, :, :], tgen[:, :, :], tgen2[:, :, :])
            v.tensor_single_scalar(tgen3[:, :, :], tgen2[:, :, :], 0.5, ALU.is_gt)
            v.tensor_sub(tgen2[:, :, :], tgen2[:, :, :], tgen3[:, :, :])

        @block.scalar
        def _(s):
            s.wait_ge(cV.h, cV.n)
            s.activation(sintab[:, :, :], tgen2[:, :, :], AF.Sin, scale=TWO_PI)

        @block.vector
        def _(v):
            v.wait_ge(cS.h, cS.n)
            v.tensor_scalar_add(tgen[:, :, :], tgen[:, :, :], 0.25)
            v.tensor_copy(tgen_i, tgen[:, :, :])
            v.tensor_copy(tgen2[:, :, :], tgen_i)
            v.tensor_sub(tgen2[:, :, :], tgen[:, :, :], tgen2[:, :, :])
            v.tensor_single_scalar(tgen3[:, :, :], tgen2[:, :, :], 0.5, ALU.is_gt)
            v.tensor_sub(tgen2[:, :, :], tgen2[:, :, :], tgen3[:, :, :])

        @block.scalar
        def _(s):
            s.wait_ge(cV.h, cV.n)
            s.activation(costab[:, :, :], tgen2[:, :, :], AF.Sin, scale=TWO_PI)
        tab_s = cS.n

        @block.vector
        def _(v):
            v.wait_ge(cS.h, tab_s)
            with nc.allow_low_precision(reason="bf16 rope tables; q is bf16"):
                v.tensor_copy(costab_bf[:, :, :], costab[:, :, :])
                v.tensor_copy(sintab_bf[:, :, :], sintab[:, :, :])

        xn_sca = [0] * NT
        xn_done_v = [0] * NT
        rec_done = [0] * NT
        xn_st_d = [0] * NT
        kv_copy = [0] * NT
        trp_t = [0] * NT
        lg_mm_t = [0] * NT
        ktr_t = [0] * NT
        neg_done_v = [0] * NT
        lg_sca = [0] * NT
        kt_v = [0] * NT

        krs = probs_em[:, :, :].rearrange("p e a -> p (e a)")[:, 0:64]
        kv_slots = [PA[:, 0:128], PA[:, 512:640]]
        lg_slots = [PB[:, 0:8], PB[:, 512:520]]

        for r in range(NT + 4):
            if r < NT:
                b = r % 4

                @block.sync
                def _(sy, r=r, b=b):
                    if r >= 4:
                        sy.wait_ge(cS.h, xn_sca[r - 4])
                    for q in range(4):
                        dLx[b].inc(sy.dma_start(
                            out=xbuf[b][:, 256 * q:256 * (q + 1)],
                            in_=x_hbm[128 * r:128 * (r + 1), 256 * q:256 * (q + 1)]))

            if 4 <= r < 8:
                k = r - 4
                @block.gpsimd
                def _(g, k=k):
                    if k == 0:
                        dWq.inc(g.dma_start(out=bq_row[:, :], in_=bq_hbm))
                    dWq.inc(g.dma_start(
                        out=wq_sb[:, 2 * k:2 * k + 2, :],
                        in_=wq_hbm.rearrange("(d p) f -> p d f",
                                             p=128)[:, 2 * k:2 * k + 2, :]))
                if r == 7:
                    w_q_done[0] = (dWq, dWq.n)
            if 8 <= r < 12:
                k = r - 8
                @block.gpsimd
                def _(g, k=k):
                    if k == 0:
                        dWo.inc(g.dma_start(out=bo_row[:, :], in_=bo_hbm))
                    dWo.inc(g.dma_start(
                        out=wo_sb[:, 2 * k:2 * k + 2, :],
                        in_=wo_hbm.rearrange("(d p) f -> p d f",
                                             p=128)[:, 2 * k:2 * k + 2, :]))
                if r == 11:
                    w_o_done[0] = (dWo, dWo.n)
            if 12 <= r < 16:
                k = r - 12
                @block.gpsimd
                def _(g, k=k):
                    for d in (2 * k, 2 * k + 1):
                        dW1.inc(g.dma_start(out=w1_sb[:, d, :],
                                            in_=w1_hbm[0, 128 * d:128 * (d + 1), :]))
                if r == 15:
                    w1_arr[(0, 0)] = dW1.n

            sq_plan = cS.n + 1 if r < NT else cS.n

            @block.scalar
            def _(s, r=r):
                if r < NT:
                    b = r % 4
                    bs = r % 4
                    s.wait_ge(dLx[b].h, 64 * (r // 4 + 1))
                    if r >= 4:
                        s.wait_ge(cV.h, rec_done[r - 4])
                    s.activation(sq_scr, xbuf[b], AF.Square,
                                 accum_out=st_ssq[:, bs:bs + 1])
                if 1 <= r <= NT:
                    t = r - 1
                    bb = t % 2
                    tb = t % 4
                    if t >= 2:
                        s.wait_ge(dXn[bb].h, xn_st_d[t - 2])
                        s.wait_ge(cT.h, trp_t[t - 2])
                    s.wait_ge(cV.h, rec_done[t])
                    s.activation(st_ln[:, tb:tb + 1], st_r[:, tb:tb + 1], AF.Ln)
                    s.activation(st_rs[:, tb:tb + 1], st_ln[:, tb:tb + 1], AF.Exp,
                                 scale=-0.5)
                    s.activation(st_nmr[:, tb:tb + 1], st_nm[:, tb:tb + 1], AF.Identity,
                                 scale=st_rs[:, tb:tb + 1])
                    s.activation(xnstage[bb], xbuf[t % 4], AF.Identity,
                                 bias=st_nmr[:, tb:tb + 1], scale=st_rs[:, tb:tb + 1])
                    xn_sca[t] = cS.n
                if 4 <= r < NT + 4:
                    te = r - 4
                    lgps_e = lg_slots[te % 2]
                    s.wait_ge(cV.h, neg_done_v[te])
                    s.activation(probs[:, te, :], lgps_e, AF.Exp,
                                 bias=lmax[:, te:te + 1], accum_out=zrow[:, te:te + 1])
                    lg_sca[te] = cS.n

            if r < NT:
                b = r % 4
                bs = r % 4

                @block.vector
                def _(v, r=r, b=b, bs=bs, sq_done=sq_plan):
                    v.wait_ge(dLx[b].h, 64 * (r // 4 + 1))
                    if r >= 4:
                        v.wait_ge(cS.h, xn_sca[r - 4])
                    v.reduce_sum(st_m[:, bs:bs + 1], xbuf[b], axis=AX.X)
                    v.tensor_scalar_mul(st_m[:, bs:bs + 1], st_m[:, bs:bs + 1], 1.0 / DM)
                    v.tensor_scalar_mul(st_nm[:, bs:bs + 1], st_m[:, bs:bs + 1], -1.0)
                    v.wait_ge(cS.h, sq_done)
                    v.tensor_mul(st_r[:, bs:bs + 1], st_m[:, bs:bs + 1], st_m[:, bs:bs + 1])
                    v.scalar_tensor_tensor(st_r[:, bs:bs + 1], st_ssq[:, bs:bs + 1], 1.0 / DM,
                                           st_r[:, bs:bs + 1], op0=ALU.mult, op1=ALU.subtract)
                    v.tensor_scalar_add(st_r[:, bs:bs + 1], st_r[:, bs:bs + 1], LN_EPS)
                    rec_done[r] = cV.n

            if 1 <= r <= NT:
                t = r - 1
                bb = t % 2

                @block.gpsimd
                def _(g, t=t, bb=bb):
                    g.wait_ge(cS.h, xn_sca[t])
                    dXn[bb].inc(g.dma_start(out=xn_dram.ap()[128 * t:128 * (t + 1), :],
                                            in_=xnstage[bb]))
                xn_st_d[t] = dXn[bb].n

                @block.tensor
                def _(tn, t=t, bb=bb):
                    tn.wait_ge(cS.h, xn_sca[t])
                    if t >= 2:
                        tn.wait_ge(cV.h, xn_done_v[t - 2])
                    pc = PC.bitcast(BF16)[:, 1024 * bb:1024 * (bb + 1)]
                    for c in range(ND):
                        ins = tn.transpose(pc[:, 128 * c:128 * (c + 1)],
                                           xnstage[bb][:, 128 * c:128 * (c + 1)],
                                           idbf[:, :])
                    cT.inc(ins)
                trp_t[t] = cT.n

            if 2 <= r <= NT + 1:
                t = r - 2
                bb = t % 2

                @block.vector
                def _(v, t=t, bb=bb):
                    v.wait_ge(cT.h, trp_t[t])
                    pc = PC.bitcast(BF16)[:, 1024 * bb:1024 * (bb + 1)]
                    v.tensor_copy(
                        xnT[:, :, 128 * t:128 * (t + 1)],
                        pc.rearrange("p (c t) -> p c t", c=ND))
                    xn_done_v[t] = cV.n

                kvps = kv_slots[t % 2]
                lgps = lg_slots[t % 2]

                @block.tensor
                def _(tn, t=t, kvps=kvps, lgps=lgps):
                    tn.wait_ge(dWa.h, w_kv_done[0])
                    tn.wait_ge(cV.h, xn_done_v[t])
                    if t >= 2:
                        tn.wait_ge(cV.h, kv_copy[t - 2])
                        tn.wait_ge(cS.h, lg_sca[t - 2])
                    ts = slice(128 * t, 128 * (t + 1))
                    tn.matmul(kvps, ones_1x128_bf[:, :], bkv_row[:, :],
                              start=True, stop=False)
                    for d in range(ND):
                        ins = tn.matmul(kvps, xnT[:, d, ts], wkv[:, d, :],
                                        start=False, stop=(d == ND - 1))
                    cT.inc(ins)
                    for d in range(ND):
                        ins = tn.matmul(lgps, xnT[:, d, ts], wg_sb[:, d, :],
                                        start=(d == 0), stop=(d == ND - 1))
                    cT.inc(ins)
                lg_mm_t[t] = cT.n

            if 3 <= r <= NT + 2:
                t = r - 3
                kvps = kv_slots[t % 2]
                lgps = lg_slots[t % 2]

                @block.vector
                def _(v, t=t, kvps=kvps, lgps=lgps):
                    v.wait_ge(cT.h, lg_mm_t[t])
                    v.tensor_copy(kvtm[:, t, :], kvps)
                    v.reduce_max(lmax[:, t:t + 1], lgps, axis=AX.X, negate=True)
                neg_done_v[t] = cV.n
                kv_copy[t] = cV.n

                @block.gpsimd
                def _(g, t=t):
                    g.wait_ge(cV.h, kv_copy[t])
                    k2 = kvtm[:, t, 0:64]
                    cA, cB = costab[:, t, 0:16], costab[:, t, 16:32]
                    sA, sB = sintab[:, t, 0:16], sintab[:, t, 16:32]
                    g.tensor_tensor(krs[:, 32:48], k2[:, 16:32], sA, ALU.mult)
                    g.tensor_tensor(krs[:, 0:16], k2[:, 0:16], cA, ALU.mult)
                    g.tensor_sub(krs[:, 0:16], krs[:, 0:16], krs[:, 32:48])
                    g.tensor_tensor(krs[:, 48:64], k2[:, 0:16], sB, ALU.mult)
                    g.tensor_tensor(krs[:, 16:32], k2[:, 16:32], cB, ALU.mult)
                    g.tensor_add(krs[:, 16:32], krs[:, 16:32], krs[:, 48:64])
                    g.tensor_copy(k2[:, 0:32], krs[:, 0:32])
                krope_p = cP.n

                @block.tensor
                def _(tn, t=t, krope_p=krope_p):
                    tn.wait_ge(cP.h, krope_p)
                    if t >= 2:
                        tn.wait_ge(cV.h, kt_v[t - 2])
                    ktp = PD[0:64, 0:128] if t % 2 == 0 else PD[0:64, 512:640]
                    cT.inc(tn.transpose(ktp, kvtm[:, t, 0:64], idf32[:, :]))
                ktr_t[t] = cT.n

            if 4 <= r <= NT + 3:
                t = r - 4

                @block.vector
                def _(v, t=t):
                    v.wait_ge(cT.h, ktr_t[t])
                    ktp = PD[0:64, 0:128] if t % 2 == 0 else PD[0:64, 512:640]
                    v.tensor_copy(kT[0:64, 128 * t:128 * (t + 1)], ktp)
                    kt_v[t] = cV.n

        ln_last_v = cV.n

        @block.sync
        def _(sy):
            for bq4 in range(4):
                sy.wait_ge(dLx[bq4].h, dLx[bq4].n)
            sy.wait_ge(dWq.h, dWq.n)
            dStash.inc(sy.dma_start(out=wq_dram.ap(), in_=regR[:, 0:8192]))
            sy.wait_ge(dWo.h, dWo.n)
            dStash.inc(sy.dma_start(out=wo_dram.ap(), in_=regR[:, 8192:16384]))
            for k in range(E * 512 // 128):
                dPre.inc(sy.dma_start(
                    out=xg_dram.ap()[128 * k:128 * (k + 1), :],
                    in_=regR[:, 0:1024]))

        if "xnT" in DBG:
            @block.sync
            def _(sy):
                sy.wait_ge(cV.h, ln_last_v)
                dDbg.inc(sy.dma_start(out=dbg_hbm["xnT"], in_=xnT[:, :, :]))

        def emit_rope(v, view3, nh, t):
            cosA = costab_bf[:, t:t + 1, 0:16].broadcast_to([128, nh, 16])
            cosB = costab_bf[:, t:t + 1, 16:32].broadcast_to([128, nh, 16])
            sinA = sintab_bf[:, t:t + 1, 0:16].broadcast_to([128, nh, 16])
            sinB = sintab_bf[:, t:t + 1, 16:32].broadcast_to([128, nh, 16])
            scr = rope_scr[:, 0:nh, :]
            v.tensor_tensor(scr[:, :, 32:48], view3[:, :, 16:32], sinA, ALU.mult)
            v.tensor_tensor(scr[:, :, 0:16], view3[:, :, 0:16], cosA, ALU.mult)
            v.tensor_sub(scr[:, :, 0:16], scr[:, :, 0:16], scr[:, :, 32:48])
            v.tensor_tensor(scr[:, :, 48:64], view3[:, :, 0:16], sinB, ALU.mult)
            v.tensor_tensor(scr[:, :, 16:32], view3[:, :, 16:32], cosB, ALU.mult)
            v.tensor_add(scr[:, :, 16:32], scr[:, :, 16:32], scr[:, :, 48:64])
            v.tensor_copy(view3[:, :, 0:32], scr[:, :, 0:32])

        @block.vector
        def _(v):
            v.wait_ge(cS.h, lg_sca[NT - 1])
            v.reciprocal(zrow[:, :], zrow[:, :])
            zB = zrow[:, :].rearrange("p (a e) -> p a e", e=1).broadcast_to([128, NT, E])
            v.tensor_tensor(probs[:, :, :], probs[:, :, :], zB, ALU.mult)
            v.tensor_copy(vstat[:, :, 0:64], kvtm[:, :, 64:128])

        @block.vector
        def _(v):
            v.wait_ge(cV.h, kt_v[NT - 1])
            v.tensor_copy(kT[64:128, :], kT[0:64, :])
            v.tensor_copy(probs_em[:, :, :].rearrange("p e a -> p a e"),
                          probs[:, :, :])

        kv_last_v = cV.n
        probs_last_s = cS.n

        @block.gpsimd
        def _(g):
            g.wait_ge(cV.h, kv_last_v)
            g.memset(stricL, 1.0)
            g.wait_ge(cP.h, cP.n)
            g.affine_select(out=stricL, in_=stricL, compare_op=ALU.is_gt,
                            fill=0.0, base=0, pattern=[[1, 128]], channel_multiplier=-1)
            g.iota(eidx8, [[1, E]], channel_multiplier=0,
                   allow_small_or_imprecise_dtypes=True)
        sidx_p = cP.n

        @block.sync
        def _(sy):
            sy.wait_ge(cV.h, kv_last_v)
            dPr.inc(sy.dma_start(out=ag_in.ap(), in_=probs_em[:, :, :]))

        @block.gpsimd
        def _(g):
            g.wait_ge(dPr.h, 16)
            ccc.inc(g.collective_compute(
                "AllGather", ALU.bypass,
                replica_groups=[list(range(NCORES))],
                ins=[ag_in.ap()], outs=[ag_out.ap()]))

        @block.sync
        def _(sy):
            sy.wait_ge(ccc.h, 1)
            for e in range(E):
                dPr.inc(sy.dma_start(
                    out=probs_all[:, e, :].rearrange("p (r a) -> p r a", r=NCORES),
                    in_=ag_out.ap()[:, :, e:e + 1, :]
                    .rearrange("r p e a -> p r (e a)")))
        pall_done = dPr.n

        for name, gate in (("kT", kv_last_v), ("vstat", kv_last_v),
                           ("kvtm", kv_last_v), ("probs", kv_last_v)):
            if name in DBG:
                src = {"kT": kT[:, :], "vstat": vstat[:, :, :],
                       "kvtm": kvtm[:, :, :], "probs": probs[:, :, :]}[name]

                @block.sync
                def _(sy, name=name, src=src, gate=gate):
                    sy.wait_ge(cV.h, gate)
                    dDbg.inc(sy.dma_start(out=dbg_hbm[name], in_=src))
        if "probs_all" in DBG:
            @block.sync
            def _(sy):
                sy.wait_ge(dPr.h, pall_done)
                dDbg.inc(sy.dma_start(out=dbg_hbm["probs_all"], in_=probs_all[:, :, :]))

        cnt_ps = PD[:, 512:520]

        class Bisect:
            def __init__(self):
                self.started = 0
                self.done = 0
                self.stage = "idle"
                self.red_cv = 0
                self.red_vv = 0
                self.cmp_vv = 0
                self.mm_ct = 0
                self.first = True
                self.enabled = False

            def pre(self, g):
                if not self.enabled:
                    return
                if self.stage != "idle" or self.started >= BIS_ITERS:
                    return
                if self.first:
                    g.wait_ge(dPr.h, pall_done)
                    self.first = False
                with nc.allow_low_precision(reason="counts <=128 exact in bf16"):
                    for e in range(E):
                        g.tensor_scalar(cmpb[:, e, :], probs_all[:, e, :],
                                        tau[:, e:e + 1], 0.0, op0=ALU.is_ge,
                                        op1=ALU.bypass)
                self.red_cv = cP.n
                self.started += 1
                self.stage = "cmp"

            def red(self, v):
                if not self.enabled or self.stage != "cmp":
                    return
                v.wait_ge(cP.h, self.red_cv)
                with nc.allow_low_precision(reason="counts <=128 exact in bf16"):
                    v.reduce_sum(cntpp[:, :], cmpb[:, :, :], axis=AX.X)
                self.red_vv = cV.n
                self.stage = "red"

            def mm(self, tn):
                if not self.enabled or self.stage != "red":
                    return
                tn.wait_ge(cV.h, self.red_vv)
                cT.inc(tn.matmul(cnt_ps, ones_128_bf[:, :], cntpp[:, :],
                                 start=True, stop=True))
                self.mm_ct = cT.n
                self.stage = "mm"

            def cmp2(self, v):
                if not self.enabled or self.stage != "mm":
                    return
                v.wait_ge(cT.h, self.mm_ct)
                v.tensor_single_scalar(geF[:, :], cnt_ps, CAP, ALU.is_ge)
                v.tensor_single_scalar(giF[:, :], cnt_ps, CAP, ALU.is_lt)
                self.cmp_vv = cV.n
                self.stage = "cmpd"

            def upd(self, g):
                if not self.enabled or self.stage != "cmpd":
                    return
                g.wait_ge(cV.h, self.cmp_vv)
                g.tensor_sub(bis_d[:, :], tau[:, :], lo[:, :])
                g.tensor_mul(bis_d[:, :], geF[:, :], bis_d[:, :])
                g.tensor_add(lo[:, :], lo[:, :], bis_d[:, :])
                g.tensor_sub(bis_d[:, :], tau[:, :], hi[:, :])
                g.tensor_mul(bis_d[:, :], giF[:, :], bis_d[:, :])
                g.tensor_add(hi[:, :], hi[:, :], bis_d[:, :])
                g.tensor_add(tau[:, :], lo[:, :], hi[:, :])
                g.tensor_scalar_mul(tau[:, :], tau[:, :], 0.5)
                self.done += 1
                self.stage = "idle"

            def active(self):
                return (self.enabled
                        and (self.stage == "cmpd"
                             or (self.stage == "idle" and self.started < BIS_ITERS)))

        bis = Bisect()
        gates_done_v = [0]

        def emit_gates(v):
            loB = lo[:, :].rearrange("p (a e) -> p a e", a=1).broadcast_to([128, NT, E])
            v.tensor_tensor(gsel[:, :, :], probs[:, :, :], loB, ALU.is_ge)
            v.scalar_tensor_tensor(gsel[:, :, :], gsel[:, :, :], 2.0, probs[:, :, :],
                                   op0=ALU.mult, op1=ALU.add)
            v.reduce_max(gmax[:, :], gsel[:, :, :], axis=AX.X)
            gmB = gmax[:, :].rearrange("p (a e) -> p a e", e=1).broadcast_to([128, NT, E])
            v.tensor_tensor(gsel[:, :, :], gsel[:, :, :], gmB, ALU.is_equal)
            v.tensor_mul(gates[:, :, :], gsel[:, :, :], probs[:, :, :])
            gates_done_v[0] = cV.n

        qev_s = {}
        qtr_t = {}
        last_exp_pair = [0]
        qevq_v = {}
        oev_v = {}
        last_S_mm = [0]
        attn_tt_v = {}
        zi_evac_v = {}
        zi_mm_last = [0]
        out_store_d = {}
        w2_arr = {}
        w1_last_ct = {}
        w2_last_ct = {}
        stt_v = {}

        wq_view = [None]

        def emit_qproj(w):
            wqv = wq_view[0] if wq_view[0] is not None else wq_sb
            for t in range(4 * w, 4 * w + 4):
                b = t % 2
                qps = PA if b == 0 else PB
                tps = PC if b == 0 else PD

                @block.tensor
                def _(tn, t=t, qps=qps, w=w):
                    tn.wait_ge(w_q_done[0][0].h, w_q_done[0][1])
                    if w == 0 and t < 2:
                        tn.wait_ge(cS.h, probs_last_s)
                    prev = qev_s.get(t - 2)
                    if prev is not None:
                        tn.wait_ge(cS.h, prev)
                    prevo = oev_v.get(t - 2)
                    if prevo is not None:
                        tn.wait_ge(cV.h, prevo)
                    ts = slice(128 * t, 128 * (t + 1))
                    tn.matmul(qps[:, 0:512], ones_1x128_bf[:, :], bq_row[:, 0:512],
                              start=True, stop=False)
                    tn.matmul(qps[:, 512:1024], ones_1x128_bf[:, :], bq_row[:, 512:1024],
                              start=True, stop=False)
                    for d in range(ND):
                        i0 = tn.matmul(qps[:, 0:512], xnT[:, d, ts], wqv[:, d, 0:512],
                                       start=False, stop=(d == ND - 1))
                        i1 = tn.matmul(qps[:, 512:1024], xnT[:, d, ts],
                                       wqv[:, d, 512:1024],
                                       start=False, stop=(d == ND - 1))
                    cT.inc(i0)
                    cT.inc(i1)
                qmm = cT.n

                @block.scalar
                def _(s, t=t, b=b, qps=qps, qmm=qmm, w=w):
                    s.wait_ge(cT.h, qmm)
                    prev = qtr_t.get(t - 2)
                    if prev is not None:
                        s.wait_ge(cT.h, prev)
                    if w == 0 and t < 2:
                        s.wait_ge(cV.h, ln_last_v)
                    s.activation(q_tm[b], qps[:, :], AF.Identity)
                    qev_s[t] = cS.n

                @block.vector
                def _(v, t=t, b=b, w=w):
                    v.wait_ge(cS.h, qev_s[t])
                    emit_rope(v, q_tm3[b], 16, t)
                    v.engine_nop()
                qrope = cV.n

                @block.tensor
                def _(tn, t=t, b=b, tps=tps, qrope=qrope, w=w):
                    tn.wait_ge(cV.h, qrope)
                    prev = qevq_v.get(t - 2)
                    if prev is not None:
                        tn.wait_ge(cV.h, prev)
                    if w == 0 and t < 2:
                        tn.wait_ge(cV.h, kt_v[NT - 1])
                    tpsb = tps[:, :].bitcast(BF16)[:, 0:1024]
                    for c in range(ND):
                        ins = tn.transpose(tpsb[:, 128 * c:128 * (c + 1)],
                                           q_tm[b][:, 128 * c:128 * (c + 1)],
                                           idbf[:, :])
                    cT.inc(ins)
                    qtr_t[t] = cT.n

                @block.vector
                def _(v, t=t, b=b, tps=tps, w=w):
                    v.wait_ge(cT.h, qtr_t[t])
                    if last_S_mm[0]:
                        v.wait_ge(cT.h, last_S_mm[0])
                    toff = 128 * (t - 4 * w)
                    tpsb = tps[:, :].bitcast(BF16)[:, 0:1024]
                    v.tensor_copy(
                        qTq[:, :].rearrange("p (c u) -> p c u", c=ND)[:, :, toff:toff + 128],
                        tpsb.rearrange("p (c u) -> p c u", c=ND))
                    qevq_v[t] = cV.n

        def emit_heads(w, pair_hook=None):
            nj = 4 * w + 4
            pend_mm = []
            pend_fin = []
            rz_free_ct = [0]

            def c0_of(j):
                k = j - 4 * w
                return 128 * k if k > 0 else 0

            def inject_znorm(tn):
                if pend_mm:
                    pw, ph, hh, rzc = pend_mm.pop(0)
                    tn.wait_ge(cS.h, rzc)
                    prev_zi = zi_evac_v.get(("z", pw, ph - 1)) or zi_evac_v.get("last")
                    if prev_zi:
                        tn.wait_ge(cV.h, prev_zi)
                    cT.inc(tn.matmul(PD[0:64, 0:512], ones_1x128_bf[:, 0:64],
                                     rz_pair[:, 512 * hh:512 * (hh + 1)],
                                     start=True, stop=True))
                    pend_fin.append((pw, ph, hh, cT.n))
                    rz_free_ct[0] = cT.n

            def finish_znorm(v):
                if pend_fin:
                    pw, ph, hh, zic = pend_fin.pop(0)
                    v.wait_ge(cT.h, zic)
                    v.tensor_copy(zi_sb[:, :], PD[0:64, 0:512])
                    po = PC[0:65, 512 * hh:512 * (hh + 1)]
                    v.tensor_tensor(
                        attnTq[(ph % 2) * 64:(ph % 2) * 64 + 64,
                               512 * (ph // 2):512 * (ph // 2) + 512],
                        po[0:64, :], zi_sb[:, :], ALU.mult)
                    attn_tt_v[(pw, ph)] = cV.n
                    zi_evac_v["last"] = cV.n

            jlist = list(range(nj - 1, -1, -1))

            prev_pvl = [0]
            for hp in range(H // 2):
                h0, h1 = 2 * hp, 2 * hp + 1
                ops = [PC[0:65, 0:512], PC[0:65, 512:1024]]
                spair = [(PA[:, 0:512], PA[:, 512:1024]),
                         (PB[:, 0:512], PB[:, 512:1024])]
                exp_cs = {}
                mask_cv = {}
                pv_ct = {}

                @block.vector
                def _(v, hp=hp, w=w):
                    if w == 0 and hp == 0:
                        for bb in range(2):
                            v.wait_ge(dXn[bb].h, dXn[bb].n)
                    if prev_pvl[0]:
                        v.wait_ge(cT.h, prev_pvl[0])
                    for si, cz in ((0, 384), (1, 256), (2, 128)):
                        v.memset(pbuf[si][:, :].rearrange(
                            "p (h c) -> p h c", h=2)[:, :, 0:cz], 0.0)
                    finish_znorm(v)
                    bis.red(v)
                pb_zero = cV.n

                if bis.enabled and bis.stage == "mm":
                    @block.vector
                    def _(v):
                        bis.cmp2(v)

                if bis.active():
                    @block.gpsimd
                    def _(g):
                        bis.upd(g)
                        bis.pre(g)

                planned = {}
                pn = cS.n
                for i in range(nj):
                    j = jlist[i]
                    pn += 1 if c0_of(j) == 0 else 2
                    planned[j] = pn
                rzc_planned = pn + 4

                for i0 in range(0, nj, 2):
                    @block.tensor
                    def _(tn, i0=i0, hp=hp, h0=h0, h1=h1, w=w):
                        tn.wait_ge(cV.h, qevq_v[4 * w + 3])
                        if hp == 0 and i0 == 0:
                            tn.wait_ge(cS.h, qev_s[4 * w + 3])
                        if i0 >= 4:
                            for i in (i0 - 4, i0 - 3):
                                jp = jlist[i]
                                wg_ = mask_cv.get(jp)
                                if wg_ is not None:
                                    tn.wait_ge(cV.h, wg_)
                                else:
                                    tn.wait_ge(cS.h, planned[jp])
                                if i == 0 and hp > 0:
                                    pa = attn_tt_v.get((w, 2 * hp - 1))
                                    if pa is not None:
                                        tn.wait_ge(cV.h, pa)
                                for hh in (0, 1):
                                    cT.inc(tn.matmul(
                                        ops[hh],
                                        vstat[:, jp, :],
                                        pbuf[i % 4][:, 512 * hh:512 * (hh + 1)],
                                        start=(jp == jlist[0]), stop=(jp == 0)))
                                pv_ct[jp] = cT.n
                        for i in (i0, i0 + 1):
                            j = jlist[i]
                            if i >= 2:
                                tn.wait_ge(cS.h, planned[jlist[i - 2]])
                            elif hp > 0:
                                tn.wait_ge(cS.h, last_exp_pair[0])
                            c0 = c0_of(j)
                            for hh, h in ((0, h0), (1, h1)):
                                kbase = (h % 2) * 64
                                mv = qTq[(h % 2) * 64:(h % 2) * 64 + 64,
                                         512 * (h // 2):512 * (h // 2) + 512]
                                cT.inc(tn.matmul(
                                    spair[j % 2][hh][:, c0:512] if c0 else spair[j % 2][hh],
                                    kT[kbase:kbase + 64, 128 * j:128 * (j + 1)],
                                    mv[:, c0:512] if c0 else mv,
                                    start=True, stop=True))
                            exp_cs[("smm", j)] = cT.n
                        inject_znorm(tn)
                        bis.mm(tn)

                    if bis.enabled and bis.stage == "mm":
                        @block.vector
                        def _(v):
                            bis.cmp2(v)

                    if bis.active():
                        @block.gpsimd
                        def _(g):
                            bis.upd(g)
                            bis.pre(g)

                    @block.vector
                    def _(v, i0=i0, hp=hp, w=w):
                        finish_znorm(v)
                        bis.red(v)
                        for i in (i0, i0 + 1):
                            j = jlist[i]
                            if j - 4 * w >= 0:
                                c0 = c0_of(j)
                                v.wait_ge(cS.h, planned[j])
                                pv2 = pbuf[i % 4][:, :].rearrange(
                                    "p (h c) -> p h c", h=2)[:, :, c0:c0 + 128]
                                tm = trimask[:, :].rearrange(
                                    "p (a c) -> p a c", a=1).broadcast_to([128, 2, 128])
                                v.tensor_tensor(pv2, pv2, tm, ALU.mult)
                                mask_cv[j] = cV.n

                for t0 in (nj, nj + 2):
                    @block.tensor
                    def _(tn, t0=t0, hp=hp, w=w):
                        for i in (t0 - 4, t0 - 3):
                            jp = jlist[i]
                            wg_ = mask_cv.get(jp)
                            if wg_ is not None:
                                tn.wait_ge(cV.h, wg_)
                            else:
                                tn.wait_ge(cS.h, planned[jp])
                            if i == 0 and hp > 0:
                                pa = attn_tt_v.get((w, 2 * hp - 1))
                                if pa is not None:
                                    tn.wait_ge(cV.h, pa)
                            for hh in (0, 1):
                                cT.inc(tn.matmul(
                                    ops[hh],
                                    vstat[:, jp, :],
                                    pbuf[i % 4][:, 512 * hh:512 * (hh + 1)],
                                    start=(jp == jlist[0]), stop=(jp == 0)))
                            pv_ct[jp] = cT.n
                pvl = cT.n
                prev_pvl[0] = cT.n
                last_exp_pair[0] = planned[0]

                @block.scalar
                def _(s, hp=hp, w=w, pb_zero=pb_zero, pvl=pvl, nj=nj):
                    for i in range(nj):
                        j = jlist[i]
                        c0 = c0_of(j)
                        if i < 2:
                            s.wait_ge(cV.h, pb_zero)
                        s.wait_ge(cT.h, exp_cs[("smm", j)])
                        if i >= 4:
                            pc = pv_ct.get(jlist[i - 4])
                            if pc is not None:
                                s.wait_ge(cT.h, pc)
                        if c0 == 0:
                            s.activation(pbuf[i % 4][:, :],
                                         PA[:, :] if j % 2 == 0 else PB[:, :],
                                         AF.Exp, scale=0.125, bias=negc[:, :])
                        else:
                            for hh in (0, 1):
                                s.activation(pbuf[i % 4][:, 512 * hh + c0:512 * (hh + 1)],
                                             spair[j % 2][hh][:, c0:512],
                                             AF.Exp, scale=0.125, bias=negc[:, :])
                    s.wait_ge(cT.h, pvl)
                    if rz_free_ct[0]:
                        s.wait_ge(cT.h, rz_free_ct[0])
                    for hh in (0, 1):
                        s.activation(zln_sb[:, :], PC[64:65, 512 * hh:512 * (hh + 1)],
                                     AF.Ln)
                        with nc.allow_low_precision(reason="1/Z bf16; attnT is bf16"):
                            s.activation(rz_pair[:, 512 * hh:512 * (hh + 1)],
                                         zln_sb[:, :], AF.Exp, scale=-1.0)
                assert cS.n == rzc_planned, (cS.n, rzc_planned)
                rzc = cS.n
                pend_mm.append((w, h0, 0, rzc))
                pend_mm.append((w, h1, 1, rzc))
                if pair_hook:
                    pair_hook()

            for _ in range(2):
                @block.tensor
                def _(tn):
                    inject_znorm(tn)

                @block.vector
                def _(v):
                    finish_znorm(v)
            last_S_mm[0] = cT.n

        def emit_oproj(w):
            half = w // 2
            att_done = cV.n
            for t in range(4 * w, 4 * w + 4):
                b = t % 2
                ops = PA if b == 0 else PB
                th = t - 8 * half

                @block.tensor
                def _(tn, t=t, ops=ops, w=w, att_done=att_done):
                    tn.wait_ge(w_o_done[0][0].h, w_o_done[0][1])
                    tn.wait_ge(cV.h, att_done)
                    prev = oev_v.get(t - 2)
                    if prev is not None:
                        tn.wait_ge(cV.h, prev)
                    toff = 128 * (t - 4 * w)
                    tn.matmul(ops[:, 0:512], ones_1x128_bf[:, :], bo_row[:, 0:512],
                              start=True, stop=False)
                    tn.matmul(ops[:, 512:1024], ones_1x128_bf[:, :], bo_row[:, 512:1024],
                              start=True, stop=False)
                    for c in range(ND):
                        stat = attnTq[:, 512 * c + toff:512 * c + toff + 128]
                        i0 = tn.matmul(ops[:, 0:512], stat, wo_sb[:, c, 0:512],
                                       start=False, stop=(c == ND - 1))
                        i1 = tn.matmul(ops[:, 512:1024], stat, wo_sb[:, c, 512:1024],
                                       start=False, stop=(c == ND - 1))
                    cT.inc(i0)
                    cT.inc(i1)
                omm = cT.n

                @block.vector
                def _(v, t=t, ops=ops, omm=omm, th=th, half=half):
                    v.wait_ge(cT.h, omm)
                    if half == 1:
                        prev_store = out_store_d.get((0, th))
                        if prev_store is not None:
                            v.wait_ge(dOutT[th].h, prev_store)
                    v.tensor_copy(acc[:, th, :], ops[:, :])
                    oev_v[t] = cV.n

        sp = {}

        def emit_indices_and_scatter():
            @block.vector
            def _(v):
                v.tensor_copy(flagsF.rearrange("p (e a) -> p e a", e=E),
                              gsel[:, :, :].rearrange("p a e -> p e a"))
                with nc.allow_low_precision(reason="0/1 flags exact in bf16"):
                    v.tensor_copy(flagsE, flagsF)
                v.tensor_scalar_mul(e512[:, :], eidx8, 512.0)
            fl_v = cV.n

            @block.tensor
            def _(tn):
                tn.wait_ge(cV.h, fl_v)
                tn.wait_ge(cP.h, sidx_p)
                tn.wait_ge(cS.h, last_exp_pair[0])
                fE = flagsE.rearrange("p (e a) -> p e a", e=E)
                for e in range(E):
                    cT.inc(tn.matmul(PA[:, 16 * e:16 * (e + 1)], stricL,
                                     fE[:, e, :], start=True, stop=True))
                cT.inc(tn.matmul(PB[:, 0:128], ones_128_bf[:, :], flagsE,
                                 start=True, stop=True))
            idx_mm = cT.n

            @block.vector
            def _(v):
                v.wait_ge(cT.h, idx_mm)
                v.tensor_copy(slotsel, PB[:, 0:128])
                cp3 = colpfx_sb.rearrange("p (e a) -> p e a", e=E)
                cs3 = slotsel.rearrange("p (e a) -> p e a", e=E)
                for e in range(E):
                    v.tensor_tensor_scan(cp3[:, e, :], cs3[:, e, :], cs3[:, e, :],
                                         0.0, op0=ALU.add, op1=ALU.bypass)
                v.tensor_sub(colpfx_sb, colpfx_sb, slotsel)
                v.tensor_add(slotsel, PA[:, 0:128], colpfx_sb)
                v.tensor_mul(slotsel, slotsel, flagsF)
                sv = slotsel.rearrange("p (e a) -> p a e", e=E)
                v.reduce_sum(slot_f, sv, axis=AX.X)
                v.tensor_scalar_min(slot_f, slot_f, 511.0)
                e5B = e512[:, :].rearrange("p (e a) -> p e a", a=1).broadcast_to([128, E, 16])
                v.tensor_tensor(slotsel.rearrange("p (e a) -> p e a", e=E),
                                flagsF.rearrange("p (e a) -> p e a", e=E),
                                e5B, ALU.mult)
                v.reduce_sum(e_f, sv, axis=AX.X)
                v.tensor_add(inv_f, slot_f, e_f)
                v.tensor_copy(inv_i32, inv_f)
                v.reduce_max(g_tok, gates[:, :, :], axis=AX.X)
            sp["idx_v"] = cV.n

            for t in range(NT):
                b = t % 2

                @block.sync
                def _(sy, t=t, b=b):
                    if t < 2:
                        for bb in range(2):
                            sy.wait_ge(dXn[bb].h, dXn[bb].n)
                    if t >= 2:
                        sy.wait_ge(dSc[b].h, 16 * (t // 2))
                    dXs[b].inc(sy.dma_start(out=xsc_sb[b],
                                            in_=xn_dram.ap()[128 * t:128 * (t + 1), :]))

                @block.gpsimd
                def _(g, t=t, b=b):
                    g.wait_ge(dXs[b].h, dXs[b].n)
                    if t == 0:
                        g.wait_ge(cV.h, sp["idx_v"])
                        g.wait_ge(dPre.h, dPre.n)
                    dSc[b].inc(g.indirect_dma_start(
                        out=xg_dram.ap(),
                        out_offset=IndirectOffsetOnAxis(inv_i32[:, t:t + 1], 0),
                        in_=xsc_sb[b],
                        in_offset=None))
            sp["sc0"] = dSc[0].n
            sp["sc1"] = dSc[1].n

        def emit_sparse_ffn():
            tr_v = {}
            last_fg = {}
            gelu_s = {}
            w2_v = {}
            st_d = {}
            hps4 = [PA[:, 0:512], PA[:, 512:1024], PB[:, 0:512], PB[:, 512:1024]]
            xgT3 = xgT.rearrange("p (d u) -> p d u", d=ND)
            def emit_xg_load(e):
                @block.sync
                def _(sy, e=e):
                    if e == 0:
                        sy.wait_ge(dSc[0].h, sp["sc0"])
                        sy.wait_ge(dSc[1].h, sp["sc1"])
                    for a in range(4):
                        if e > 0:
                            sy.wait_ge(cT.h, tr_v[("mm", e - 1, a)])
                        dXgA[a].inc(sy.dma_start(
                            out=xg_sb[:, 1024 * a:1024 * (a + 1)],
                            in_=xg_dram.ap()[512 * e + 128 * a:
                                             512 * e + 128 * (a + 1), :]))
                        sp[("xg", e, a)] = dXgA[a].n

            emit_xg_load(0)
            for e in range(E):
                for a in range(4):
                    @block.tensor
                    def _(tn, e=e, a=a):
                        tn.wait_ge(dXgA[a].h, sp[("xg", e, a)])
                        if a == 0 and e == 0:
                            tn.wait_ge(cV.h, sp["h1_znorm_cv"])
                        pcb = ((PC, PD)[a % 2])[:, :].bitcast(BF16)[:, 0:1024]
                        prev = tr_v.get((e, a - 2))
                        if prev is not None:
                            tn.wait_ge(cV.h, prev)
                        if a < 2 and e > 0:
                            prev = w2_v.get((e - 1, 2 + a))
                            if prev is not None:
                                tn.wait_ge(cV.h, prev)
                        for c in range(ND):
                            ins = tn.transpose(pcb[:, 128 * c:128 * (c + 1)],
                                               xg_sb[:, 1024 * a + 128 * c:
                                                     1024 * a + 128 * (c + 1)],
                                               idbf[:, :])
                        cT.inc(ins)
                        tr_v[("mm", e, a)] = cT.n

                    @block.vector
                    def _(v, e=e, a=a):
                        v.wait_ge(cT.h, tr_v[("mm", e, a)])
                        pcb = ((PC, PD)[a % 2])[:, :].bitcast(BF16)[:, 0:1024]
                        v.tensor_copy(xgT3[:, :, 128 * a:128 * (a + 1)],
                                      pcb.rearrange("p (c u) -> p c u", c=ND))
                        tr_v[(e, a)] = cV.n
                sp[("tr", e)] = cT.n
                tr_all = cV.n

                g_base = cS.n
                for f in range(NF):
                    gelu_s[(e, f)] = g_base + f + 1
                for f4 in range(0, NF, 4):
                    @block.tensor
                    def _(tn, e=e, f4=f4, tr_all=tr_all):
                        tn.wait_ge(dW1.h, w1_arr[(0, e)])
                        tn.wait_ge(cV.h, tr_all)
                        for f in range(f4, f4 + 4):
                            if f - 4 >= 0:
                                tn.wait_ge(cS.h, gelu_s[(e, f - 4)])
                            for d in range(ND):
                                ins = tn.matmul(hps4[f % 4],
                                                w1_sb[:, d, 128 * f:128 * (f + 1)],
                                                xgT3[:, d, :],
                                                start=(d == 0), stop=(d == ND - 1))
                            cT.inc(ins)
                            gelu_s[("mm", e, f)] = cT.n
                sp[("w1", e)] = cT.n
                if e + 1 < E:
                    emit_xg_load(e + 1)

                @block.scalar
                def _(s, e=e):
                    for f in range(NF):
                        s.wait_ge(cT.h, gelu_s[("mm", e, f)])
                        s.activation(hG[:, 512 * f:512 * (f + 1)],
                                     hps4[f % 4], AF.Gelu)
                assert cS.n == gelu_s[(e, NF - 1)], (cS.n, e)

                nxt = e + 1
                if nxt < E:
                    @block.gpsimd
                    def _(g, e=e, nxt=nxt):
                        g.wait_ge(cT.h, sp[("w1", e)])
                        for d in range(ND):
                            dW1.inc(g.dma_start(out=w1_sb[:, d, :],
                                                in_=w1_hbm[nxt, 128 * d:128 * (d + 1), :]))
                    w1_arr[(0, nxt)] = dW1.n

                glast = gelu_s[(e, NF - 1)]
                for a in range(4):
                    @block.tensor
                    def _(tn, e=e, a=a, glast=glast):
                        tn.wait_ge(cS.h, glast)
                        tn.wait_ge(dW2.h, w2_arr[(0, e)])
                        if e == 0 and a < 2:
                            tn.wait_ge(cV.h, sp["routing_cv"])
                        wps = PC if a % 2 == 0 else PD
                        prev = w2_v.get((e, a - 2))
                        if prev is not None:
                            tn.wait_ge(cV.h, prev)
                        for f in range(NF):
                            i0 = tn.matmul(wps[:, 0:512],
                                           hG[:, 512 * f + 128 * a:512 * f + 128 * (a + 1)],
                                           w2_sb[:, f, 0:512],
                                           start=(f == 0), stop=(f == NF - 1))
                            i1 = tn.matmul(wps[:, 512:1024],
                                           hG[:, 512 * f + 128 * a:512 * f + 128 * (a + 1)],
                                           w2_sb[:, f, 512:1024],
                                           start=(f == 0), stop=(f == NF - 1))
                        cT.inc(i0)
                        cT.inc(i1)
                        w2_v[("mm", e, a)] = cT.n

                    @block.vector
                    def _(v, e=e, a=a):
                        v.wait_ge(cT.h, w2_v[("mm", e, a)])
                        ls = last_fg.get(a % 2)
                        if ls is not None:
                            v.wait_ge(dFgP[a % 2].h, ls)
                        wps = PC if a % 2 == 0 else PD
                        with nc.allow_low_precision(reason="ffn out staged bf16 like other matmul operands"):
                            v.tensor_copy(ffg2[a % 2], wps[:, :])
                        w2_v[(e, a)] = cV.n

                    @block.sync
                    def _(sy, e=e, a=a):
                        sy.wait_ge(cV.h, w2_v[(e, a)])
                        dFgP[a % 2].inc(sy.dma_start(
                            out=fg_dram.ap()[512 * e + 128 * a:
                                             512 * e + 128 * (a + 1), :],
                            in_=ffg2[a % 2]))
                        last_fg[a % 2] = dFgP[a % 2].n
                sp[("w2", e)] = cT.n

                nxt = e + 1
                if nxt < E:
                    @block.gpsimd
                    def _(g, e=e, nxt=nxt):
                        g.wait_ge(cT.h, sp[("w2", e)])
                        for f in range(NF):
                            dW2.inc(g.dma_start(out=w2_sb[:, f, :],
                                                in_=w2_hbm[nxt, 128 * f:128 * (f + 1), :]))
                    w2_arr[(0, nxt)] = dW2.n
            sp["ffn_ct"] = cT.n
            sp["fg_d0"] = dFgP[0].n
            sp["fg_d1"] = dFgP[1].n

        epi_seq = []
        epi_state = {"pf": 0, "st": 0}
        epi_gath = {}
        add_done = {}

        def epi_push(half):
            for th in range(8):
                epi_seq.append((half, th))

        def epi_prefetch(k=1):
            while k > 0 and epi_state["pf"] < len(epi_seq):
                i = epi_state["pf"]
                if i >= 2 and (i - 2) not in add_done:
                    break
                half, th = epi_seq[i]
                tglob = 8 * half + th
                b = i % 2

                @block.gpsimd
                def _(g, i=i, tglob=tglob, b=b):
                    g.wait_ge(dFgP[0].h, sp["fg_d0"])
                    g.wait_ge(dFgP[1].h, sp["fg_d1"])
                    prev = add_done.get(i - 2)
                    if prev is not None:
                        g.wait_ge(cV.h, prev)
                    dFo[b].inc(g.indirect_dma_start(
                        out=fo_sb[b],
                        out_offset=None,
                        in_=fg_dram.ap(),
                        in_offset=IndirectOffsetOnAxis(inv_i32[:, tglob:tglob + 1], 0)))

                @block.sync
                def _(sy, i=i, tglob=tglob, b=b):
                    sy.wait_ge(cV.h, sp["routing_cv"])
                    sy.wait_ge(dSc[0].h, sp["sc0"])
                    sy.wait_ge(dSc[1].h, sp["sc1"])
                    prev = add_done.get(i - 2)
                    if prev is not None:
                        sy.wait_ge(cV.h, prev)
                    for q in range(4):
                        dXr[b].inc(sy.dma_start(
                            out=xrel[b][:, 256 * q:256 * (q + 1)],
                            in_=x_hbm[128 * tglob:128 * (tglob + 1),
                                      256 * q:256 * (q + 1)]))
                epi_gath[i] = (dFo[b].n, dXr[b].n)
                epi_state["pf"] += 1
                k -= 1

        def epi_step():
            i = epi_state["st"]
            if i >= len(epi_seq) or i >= epi_state["pf"]:
                return False
            half, th = epi_seq[i]
            tglob = 8 * half + th
            if oev_v.get(tglob) is None:
                return False
            b = i % 2
            fod, xrd = epi_gath[i]

            @block.vector
            def _(v, th=th, tglob=tglob, b=b, fod=fod, xrd=xrd):
                v.wait_ge(dFo[b].h, fod)
                v.wait_ge(dXr[b].h, xrd)
                v.wait_ge(cV.h, oev_v[tglob])
                v.scalar_tensor_tensor(acc[:, th, :], fo_sb[b],
                                       g_tok[:, tglob:tglob + 1], acc[:, th, :],
                                       op0=ALU.mult, op1=ALU.add)
                v.tensor_add(acc[:, th, :], acc[:, th, :], xrel[b])
            add_done[i] = cV.n

            @block.sync
            def _(sy, half=half, th=th, tglob=tglob, adv=add_done[i]):
                sy.wait_ge(cV.h, adv)
                dOutT[th].inc(sy.dma_start(
                    out=out_hbm[128 * tglob:128 * (tglob + 1), :],
                    in_=acc[:, th, :]))
                out_store_d[(half, th)] = dOutT[th].n
            epi_state["st"] += 1
            epi_prefetch(1)
            return True

        def epi_hook():
            epi_step()

        emit_qproj(0)
        emit_heads(0)
        emit_oproj(0)
        bis.enabled = True
        emit_qproj(1)
        emit_heads(1)
        sp["h1_znorm_cv"] = cV.n

        while bis.done < BIS_ITERS:
            if bis.stage == "mm":
                @block.vector
                def _(v):
                    bis.cmp2(v)

            @block.gpsimd
            def _(g):
                bis.upd(g)
                bis.pre(g)

            @block.vector
            def _(v):
                bis.red(v)

            @block.tensor
            def _(tn):
                bis.mm(tn)

        @block.gpsimd
        def _(g):
            bis.upd(g)
        bis_done_p = cP.n

        @block.vector
        def _(v):
            v.wait_ge(cP.h, bis_done_p)
            emit_gates(v)
        sp["routing_cv"] = cV.n

        if "tau" in DBG:
            @block.sync
            def _(sy):
                sy.wait_ge(cV.h, gates_done_v[0])
                dDbg.inc(sy.dma_start(out=dbg_hbm["tau"], in_=lo[:, :]))
        if "gates" in DBG:
            @block.sync
            def _(sy):
                sy.wait_ge(cV.h, gates_done_v[0])
                dDbg.inc(sy.dma_start(out=dbg_hbm["gates"], in_=gates[:, :, :]))

        emit_indices_and_scatter()
        emit_oproj(1)

        oproj_h0_ct = cT.n

        @block.gpsimd
        def _(g):
            g.wait_ge(cT.h, oproj_h0_ct)
            g.wait_ge(dStash.h, dStash.n)
            for f in range(NF):
                dW2.inc(g.dma_start(out=w2_sb[:, f, :],
                                    in_=w2_hbm[0, 128 * f:128 * (f + 1), :]))
        w2_arr[(0, 0)] = dW2.n

        emit_sparse_ffn()

        @block.gpsimd
        def _(g):
            g.wait_ge(dStash.h, dStash.n)
            g.wait_ge(cT.h, sp[("w1", E - 1)])
            for k in range(4):
                dWq2.inc(g.dma_start(out=regB[:, 2048 * k:2048 * (k + 1)],
                                     in_=wq_dram.ap()[:, 2048 * k:2048 * (k + 1)]))

        @block.gpsimd
        def _(g):
            g.wait_ge(cT.h, sp["ffn_ct"])
            for k in range(4):
                dWo2.inc(g.dma_start(out=regR[:, 8192 + 2048 * k:
                                              8192 + 2048 * (k + 1)],
                                     in_=wo_dram.ap()[:, 2048 * k:2048 * (k + 1)]))
        w_q_done[0] = (dWq2, dWq2.n)
        w_o_done[0] = (dWo2, dWo2.n)
        wq_view[0] = wq_sb2

        epi_push(0)
        epi_prefetch(2)
        emit_qproj(2)
        emit_heads(2, pair_hook=epi_hook)
        while epi_state["st"] < 8:
            epi_prefetch(1)
            assert epi_step(), "half-0 epilogue pump stalled"
        emit_oproj(2)
        epi_push(1)
        epi_prefetch(1)
        emit_qproj(3)
        emit_heads(3, pair_hook=epi_hook)
        emit_oproj(3)
        guard = 0
        while epi_state["st"] < 16:
            epi_prefetch(1)
            epi_step()
            guard += 1
            assert guard < 200, "epilogue drain stalled"

        @block.sync
        def _(sy):
            for th in range(8):
                sy.wait_ge(dOutT[th].h, dOutT[th].n)
            if dDbg.n:
                sy.wait_ge(dDbg.h, dDbg.n)

    return nc


def kernel(**inputs):
    x = np.ascontiguousarray(np.asarray(inputs["L0_attn_x"], dtype=np.float32))
    wq = np.ascontiguousarray(np.asarray(inputs["L0_attn_Wq"], dtype=np.float32))
    bq = np.ascontiguousarray(np.asarray(inputs["L0_attn_bq"], dtype=np.float32)).reshape(1, DM)
    wk = np.ascontiguousarray(np.asarray(inputs["L0_attn_Wk"], dtype=np.float32))
    bk = np.ascontiguousarray(np.asarray(inputs["L0_attn_bk"], dtype=np.float32)).reshape(1, HD)
    wv = np.ascontiguousarray(np.asarray(inputs["L0_attn_Wv"], dtype=np.float32))
    bv = np.ascontiguousarray(np.asarray(inputs["L0_attn_bv"], dtype=np.float32)).reshape(1, HD)
    wo = np.ascontiguousarray(np.asarray(inputs["L0_attn_Wo"], dtype=np.float32))
    bo = np.ascontiguousarray(np.asarray(inputs["L0_attn_bo"], dtype=np.float32)).reshape(1, DM)
    wg = np.ascontiguousarray(np.asarray(inputs["L0_ffn_Wg"], dtype=np.float32))
    w1 = np.ascontiguousarray(np.asarray(inputs["L0_ffn_W1"], dtype=np.float32))
    w2 = np.ascontiguousarray(np.asarray(inputs["L0_ffn_W2"], dtype=np.float32))

    nc = build_nc()
    in_maps = []
    for c in range(NCORES):
        in_maps.append({
            "x": x[c], "wq": wq, "bq": bq, "wk": wk, "bk": bk, "wv": wv, "bv": bv,
            "wo": wo, "bo": bo, "wg": wg, "w1": w1, "w2": w2,
        })
    res = run_bass_kernel_spmd(nc, in_maps, core_ids=list(range(NCORES)))
    return np.stack([res.results[c]["out"] for c in range(NCORES)], axis=0)


# version 25
# speedup vs baseline: 1.3730x; 1.3730x over previous
import math
from contextlib import ExitStack

import numpy as np

import concourse.bass as bass
import concourse.mybir as mybir
from concourse.bass_core_rust import IndirectOffsetOnAxis
from concourse.bass_utils import run_bass_kernel_spmd

F32 = mybir.dt.float32
BF16 = mybir.dt.bfloat16
U8 = mybir.dt.uint8
I32 = mybir.dt.int32
AF = mybir.ActivationFunctionType
ALU = mybir.AluOpType
AX = mybir.AxisListType

NCORES = 8
T, DM, H, HD, FF, E = 2048, 1024, 16, 64, 2048, 8
NT, ND, NF = T // 128, DM // 128, FF // 128
CAP = float(T * NCORES // E)
CEXP = 16.0
TWO_PI = 2.0 * math.pi
BIS_ITERS = 40
LN_EPS = 1e-5

DBG = []


class Ctr:
    def __init__(self, h, step=1):
        self.h = h
        self.n = 0
        self.step = step

    def inc(self, instr, k=None):
        k = self.step if k is None else k
        instr.then_inc(self.h, k)
        self.n += k
        return instr



class AutoSync:

    _PASS = ("wait_ge", "dma_start", "dma_start_transpose", "collective_compute",
             "indirect_dma_start", "sem_inc", "wait_op")

    def __init__(self, eng, ctr):
        self._eng = eng
        self._ctr = ctr

    def __getattr__(self, name):
        f = getattr(self._eng, name)
        if name.startswith("_") or name in AutoSync._PASS or not callable(f):
            return f
        eng, ctr = self._eng, self._ctr

        def wrapped(*a, **k):
            eng.wait_ge(ctr.h, ctr.n)
            ins = f(*a, **k)
            ctr.inc(ins)
            return ins

        return wrapped


class BlockWrap:
    def __init__(self, block, cV, cS, cP):
        self._b = block
        self._cV, self._cS, self._cP = cV, cS, cP

    def vector(self, f):
        return self._b.vector(lambda v: f(AutoSync(v, self._cV)))

    def scalar(self, f):
        return self._b.scalar(lambda s: f(AutoSync(s, self._cS)))

    def gpsimd(self, f):
        return self._b.gpsimd(lambda g: f(AutoSync(g, self._cP)))

    def tensor(self, f):
        return self._b.tensor(f)

    def sync(self, f):
        return self._b.sync(f)


def build_nc():
    nc = bass.Bass("TRN2", target_bir_lowering=False)

    x_hbm = nc.dram_tensor("x", [T, DM], F32, kind="ExternalInput").ap()
    wq_hbm = nc.dram_tensor("wq", [DM, DM], F32, kind="ExternalInput").ap()
    bq_hbm = nc.dram_tensor("bq", [1, DM], F32, kind="ExternalInput").ap()
    wk_hbm = nc.dram_tensor("wk", [DM, HD], F32, kind="ExternalInput").ap()
    bk_hbm = nc.dram_tensor("bk", [1, HD], F32, kind="ExternalInput").ap()
    wv_hbm = nc.dram_tensor("wv", [DM, HD], F32, kind="ExternalInput").ap()
    bv_hbm = nc.dram_tensor("bv", [1, HD], F32, kind="ExternalInput").ap()
    wo_hbm = nc.dram_tensor("wo", [DM, DM], F32, kind="ExternalInput").ap()
    bo_hbm = nc.dram_tensor("bo", [1, DM], F32, kind="ExternalInput").ap()
    wg_hbm = nc.dram_tensor("wg", [DM, E], F32, kind="ExternalInput").ap()
    w1_hbm = nc.dram_tensor("w1", [E, DM, FF], F32, kind="ExternalInput").ap()
    w2_hbm = nc.dram_tensor("w2", [E, FF, DM], F32, kind="ExternalInput").ap()
    out_hbm = nc.dram_tensor("out", [T, DM], F32, kind="ExternalOutput").ap()

    ag_in = nc.dram_tensor("ag_in", [128, E, NT], F32)
    wq_dram = nc.dram_tensor("wq_dram", [128, 8192], BF16)
    wo_dram = nc.dram_tensor("wo_dram", [128, 8192], BF16)
    xn_dram = nc.dram_tensor("xn_dram", [T, DM], BF16)
    xg_dram = nc.dram_tensor("xg_dram", [E * 512, DM], BF16)
    fg_dram = nc.dram_tensor("fg_dram", [E * 512, DM], BF16)
    ag_out = nc.dram_tensor("ag_out", [NCORES, 128, E, NT], F32, addr_space="Shared")

    dbg_specs = {
        "xnT": ([128, ND, T], BF16), "kT": ([128, T], BF16),
        "vstat": ([128, NT, 65], BF16), "probs": ([128, NT, E], F32),
        "tau": ([128, E], F32), "gates": ([128, NT, E], F32),
        "qT0": ([128, ND * 512], BF16), "attnT0": ([128, ND * 512], BF16),
        "acc0": ([128, NT // 2, DM], F32),
        "probs_all": ([128, E, 128], F32), "kvtm": ([128, NT, 128], F32),
    }
    dbg_hbm = {}
    for name in DBG:
        shp, dt = dbg_specs[name]
        dbg_hbm[name] = nc.dram_tensor("dbg_" + name, shp, dt, kind="ExternalOutput").ap()

    ctx = ExitStack()
    with ctx:
        def sb(name, shape, dt=F32):
            return ctx.enter_context(nc.sbuf_tensor(name, shape, dt))

        def psum_t(name, shape, dt=F32):
            return ctx.enter_context(nc.psum_tensor(name, shape, dt))

        def sem(name, step=1):
            return Ctr(ctx.enter_context(nc.semaphore(name)), step)

        xnT = sb("xnT", [128, ND, T], BF16)
        acc = sb("acc", [128, NT // 2, DM], F32)
        costab = sb("costab", [128, NT, 32], F32)
        sintab = sb("sintab", [128, NT, 32], F32)
        costab_bf = sb("costab_bf", [128, NT, 32], BF16)
        sintab_bf = sb("sintab_bf", [128, NT, 32], BF16)
        trimask = sb("trimask", [128, 128], BF16)
        idf32 = sb("idf32", [128, 128], F32)
        idbf = sb("idbf", [128, 128], BF16)
        ones_1x128_bf = sb("ones_1x128_bf", [1, 128], BF16)
        ones_1x128_f = sb("ones_1x128_f", [1, 128], F32)
        ones_128_bf = sb("ones_128_bf", [128, 128], BF16)
        negc = sb("negc", [128, 1], F32)
        kT = sb("kT", [128, T], BF16)
        vstat = sb("vstat", [128, NT, 65], BF16)
        kvtm = sb("kvtm", [128, NT, 128], F32)
        probs = sb("probs", [128, NT, E], F32)
        probs_all = sb("probs_all", [128, E, 128], F32)
        probs_em = sb("probs_em", [128, E, NT], F32)
        cmpb = sb("cmpb", [128, E, 128], BF16)
        cntpp = sb("cntpp", [128, E], BF16)
        tau = sb("tau", [128, E], F32)
        lo = sb("lo", [128, E], F32)
        hi = sb("hi", [128, E], F32)
        geb = sb("geb", [128, E], U8)
        geb_inv = sb("geb_inv", [128, E], U8)
        gsel = sb("gsel", [128, NT, E], F32)
        gmax = sb("gmax", [128, NT], F32)
        gates = sb("gates", [128, NT, E], F32)
        wkv = sb("wkv", [128, ND, 128], BF16)
        wg_sb = sb("wg_sb", [128, ND, E], BF16)
        bq_row = sb("bq_row", [1, DM], BF16)
        bkv_row = sb("bkv_row", [1, 128], BF16)
        bo_row = sb("bo_row", [1, DM], BF16)
        rz_pair = sb("rz_pair", [1, 1024], BF16)
        zi_sb = sb("zi_sb", [64, 512], BF16)
        lmax = sb("lmax", [128, NT], F32)
        zrow = sb("zrow", [128, NT], F32)
        st_ssq = sb("st_ssq", [128, 4], F32)
        st_m = sb("st_m", [128, 4], F32)
        st_nm = sb("st_nm", [128, 4], F32)
        st_r = sb("st_r", [128, 4], F32)
        st_rs = sb("st_rs", [128, 4], F32)
        st_nmr = sb("st_nmr", [128, 4], F32)
        st_ln = sb("st_ln", [128, 4], F32)
        tgen = sb("tgen", [128, NT, 32], F32)
        tgen3 = sb("tgen3", [128, NT, 32], F32)
        tgen2 = sb("tgen2", [128, NT, 32], F32)

        tt_iota = sb("tt_iota", [128, NT], F32)
        tgen_i = tgen3[:, :, :].bitcast(I32)
        zln_sb = tgen[0:1, :, :].rearrange("p a b -> p (a b)")
        j_iota = sb("j_iota", [1, 32], F32)
        theta = sb("theta", [1, 32], F32)
        thetaR = sb("thetaR", [128, 32], F32)

        kvflat = kvtm[:, :, :].rearrange("p a b -> p (a b)")
        inv_i32 = kvflat[:, 0:16].bitcast(I32)
        g_tok = kvflat[:, 16:32]
        inv_f = kvflat[:, 32:48]
        slot_f = kvflat[:, 48:64]
        e_f = kvflat[:, 64:80]
        e512 = kvflat[:, 80:88]
        flagsE = kvflat[:, 128:192].bitcast(BF16)
        colpfx_sb = kvflat[:, 192:320]
        slotsel = kvflat[:, 320:448]
        flagsF = kvflat[:, 448:576]
        eidx8 = kvflat[:, 576:584]
        stricL = kvflat[:, 584:648].bitcast(BF16)
        xsc_sb = [kvflat[:, 768 + k * 512:1280 + k * 512].bitcast(BF16)
                  for k in range(2)]

        regA = sb("regA", [128, 16384], BF16)

        def carveA(off_bytes, n_elems, dt):
            if dt == BF16:
                return regA[:, off_bytes // 2: off_bytes // 2 + n_elems]
            return regA[:, off_bytes // 2:(off_bytes + n_elems * 4) // 2].bitcast(dt)

        xbuf = [carveA(0, 1024, F32), carveA(4096, 1024, F32),
                carveA(16384, 1024, F32), carveA(20480, 1024, F32)]
        dLx4 = None
        sq_scr = carveA(8192, 1024, F32)
        xnstage = [carveA(12288 + k * 2048, 1024, BF16) for k in range(2)]

        qTq = carveA(0, ND * 512, BF16)
        attnTq = carveA(8192, ND * 512, BF16)
        q_tm = [carveA(16384 + k * 2048, 1024, BF16) for k in range(2)]
        q_tm3 = [a.rearrange("p (h j) -> p h j", h=16) for a in q_tm]
        rope_scr = carveA(20480, 1024, BF16).rearrange("p (h j) -> p h j", h=16)
        krope_scr = carveA(0, 1024, F32).rearrange("p (h j) -> p h j", h=16)
        pbuf = [carveA(22528 + k * 2048, 1024, BF16) for k in range(4)]

        hT = carveA(0, NF * 1024, BF16)
        xg_sb = carveA(0, 4096, BF16)
        xgT = carveA(8192, 4096, BF16)
        hG = carveA(16384, 8192, BF16)

        ffg2 = [tgen2[:, :, :].rearrange("p a b -> p (a b)").bitcast(BF16),
                tgen3[:, :, :].rearrange("p a b -> p (a b)").bitcast(BF16)]
        fo_sb = ffg2
        xrel = [probs_all[:, :, :].rearrange("p e t -> p (e t)"),
                kvflat[:, 1024:2048]]

        regB = sb("regB", [128, 16384], BF16)
        w1_sb = regB.rearrange("p (d f) -> p d f", d=ND)

        regR = sb("regR", [128, 16384], BF16)
        wq_sb = regR[:, 0:8192].rearrange("p (d f) -> p d f", d=ND)
        wq_sb2 = regB[:, 0:8192].rearrange("p (d f) -> p d f", d=ND)
        wo_sb = regR[:, 8192:16384].rearrange("p (d f) -> p d f", d=ND)
        w2_sb = regR.rearrange("p (f d) -> p f d", f=NF)

        PA = psum_t("PA", [128, 1024], F32)
        PB = psum_t("PB", [128, 1024], F32)
        PC = psum_t("PC", [128, 1024], F32)
        PD = psum_t("PD", [128, 1024], F32)

        dLx = [sem("dLx%d" % i, 16) for i in range(4)]
        dWa = sem("dWa", 16)
        dWq = sem("dWq", 16)
        dWo = sem("dWo", 16)
        dW1 = sem("dW1", 16)
        dW2 = sem("dW2", 16)
        dPr = sem("dPr", 16)
        dXr = [sem("dXr0", 16), sem("dXr1", 16)]
        dOutT = [sem("dOutT%d" % i, 16) for i in range(8)]
        dDbg = sem("dDbg", 16)
        dStash = sem("dStash", 16)
        dWq2 = sem("dWq2", 16)
        dWo2 = sem("dWo2", 16)
        dXn = [sem("dXn0", 16), sem("dXn1", 16)]
        dPre = sem("dPre", 16)
        dXs = [sem("dXs0", 16), sem("dXs1", 16)]
        dSc = [sem("dSc0", 16), sem("dSc1", 16)]
        dXgA = [sem("dXg%d" % a, 16) for a in range(4)]
        dFgP = [sem("dFg0", 16), sem("dFg1", 16)]
        dFo = [sem("dFo0", 16), sem("dFo1", 16)]
        cT = sem("cT")
        cS = sem("cS")
        cV = sem("cV")
        cP = sem("cP")
        ccc = sem("ccc")

        block = BlockWrap(ctx.enter_context(nc.Block()), cV, cS, cP)

        @block.gpsimd
        def _(g):
            g.memset(ones_1x128_bf[:, :], 1.0)
            g.memset(ones_1x128_f[:, :], 1.0)
            g.memset(ones_128_bf[:, :], 1.0)
            g.memset(negc[:, :], -CEXP)
            g.memset(vstat[:, :, :], 1.0)
            g.memset(lo[:, :], 0.0)
            g.memset(hi[:, :], 1.0)
            g.memset(tau[:, :], 0.5)
            g.memset(idf32[:, :], 1.0)
            g.memset(idbf[:, :], 1.0)
            g.memset(trimask[:, :], 1.0)
            g.wait_ge(cP.h, cP.n)
            g.affine_select(out=idf32[:, :], in_=idf32[:, :], compare_op=ALU.is_equal,
                            fill=0.0, base=0, pattern=[[-1, 128]], channel_multiplier=1)
            g.affine_select(out=idbf[:, :], in_=idbf[:, :], compare_op=ALU.is_equal,
                            fill=0.0, base=0, pattern=[[-1, 128]], channel_multiplier=1)
            g.affine_select(out=trimask[:, :], in_=trimask[:, :], compare_op=ALU.is_ge,
                            fill=0.0, base=0, pattern=[[1, 128]], channel_multiplier=-1)
            g.iota(tt_iota[:, :], [[128, NT]], channel_multiplier=1,
                   allow_small_or_imprecise_dtypes=True)
            g.iota(j_iota[:, :], [[1, 32]], channel_multiplier=0,
                          allow_small_or_imprecise_dtypes=True)
            dWa.inc(g.dma_start(out=wkv[:, :, 0:64],
                                in_=wk_hbm.rearrange("(d p) f -> p d f", p=128)))
            dWa.inc(g.dma_start(out=wkv[:, :, 64:128],
                                in_=wv_hbm.rearrange("(d p) f -> p d f", p=128)))
            dWa.inc(g.dma_start(out=wg_sb[:, :, :],
                                in_=wg_hbm.rearrange("(d p) e -> p d e", p=128)))
            dWa.inc(g.dma_start(out=bkv_row[:, 0:64], in_=bk_hbm))
            dWa.inc(g.dma_start(out=bkv_row[:, 64:128], in_=bv_hbm))
        w_kv_done = [dWa.n]

        w_q_done = [None]
        w_o_done = [None]
        w1_arr = {}

        @block.scalar
        def _(s):
            s.wait_ge(cP.h, cP.n)
            s.activation(theta[:, :], j_iota[:, :], AF.Exp,
                                scale=-math.log(10000.0) / 32.0)

        @block.tensor
        def _(t):
            t.wait_ge(cS.h, cS.n)
            cT.inc(t.matmul(PA[:, 0:32], ones_1x128_f[:, :], theta[:, :],
                            start=True, stop=True))

        @block.vector
        def _(v):
            v.wait_ge(cT.h, cT.n)
            v.tensor_copy(thetaR[:, :], PA[:, 0:32])
            for a in range(NT):
                v.tensor_scalar_mul(tgen[:, a, :], thetaR[:, :], tt_iota[:, a:a + 1])
            v.tensor_scalar_mul(tgen[:, :, :], tgen[:, :, :], 1.0 / TWO_PI)
            v.tensor_copy(tgen_i, tgen[:, :, :])
            v.tensor_copy(tgen2[:, :, :], tgen_i)
            v.tensor_sub(tgen2[:, :, :], tgen[:, :, :], tgen2[:, :, :])
            v.tensor_single_scalar(tgen3[:, :, :], tgen2[:, :, :], 0.5, ALU.is_gt)
            v.tensor_sub(tgen2[:, :, :], tgen2[:, :, :], tgen3[:, :, :])

        @block.scalar
        def _(s):
            s.wait_ge(cV.h, cV.n)
            s.activation(sintab[:, :, :], tgen2[:, :, :], AF.Sin, scale=TWO_PI)

        @block.vector
        def _(v):
            v.wait_ge(cS.h, cS.n)
            v.tensor_scalar_add(tgen[:, :, :], tgen[:, :, :], 0.25)
            v.tensor_copy(tgen_i, tgen[:, :, :])
            v.tensor_copy(tgen2[:, :, :], tgen_i)
            v.tensor_sub(tgen2[:, :, :], tgen[:, :, :], tgen2[:, :, :])
            v.tensor_single_scalar(tgen3[:, :, :], tgen2[:, :, :], 0.5, ALU.is_gt)
            v.tensor_sub(tgen2[:, :, :], tgen2[:, :, :], tgen3[:, :, :])

        @block.scalar
        def _(s):
            s.wait_ge(cV.h, cV.n)
            s.activation(costab[:, :, :], tgen2[:, :, :], AF.Sin, scale=TWO_PI)
        tab_s = cS.n

        @block.vector
        def _(v):
            v.wait_ge(cS.h, tab_s)
            with nc.allow_low_precision(reason="bf16 rope tables; q is bf16"):
                v.tensor_copy(costab_bf[:, :, :], costab[:, :, :])
                v.tensor_copy(sintab_bf[:, :, :], sintab[:, :, :])

        xn_sca = [0] * NT
        xn_done_v = [0] * NT
        rec_done = [0] * NT
        xn_st_d = [0] * NT
        kv_copy = [0] * NT
        trp_t = [0] * NT
        lg_mm_t = [0] * NT
        ktr_t = [0] * NT
        neg_done_v = [0] * NT
        lg_sca = [0] * NT
        kt_v = [0] * NT

        krs = probs_em[:, :, :].rearrange("p e a -> p (e a)")[:, 0:64]
        kv_slots = [PA[:, 0:128], PA[:, 512:640]]
        lg_slots = [PB[:, 0:8], PB[:, 512:520]]

        for r in range(NT + 4):
            if r < NT:
                b = r % 4

                @block.sync
                def _(sy, r=r, b=b):
                    if r >= 4:
                        sy.wait_ge(cS.h, xn_sca[r - 4])
                    for q in range(4):
                        dLx[b].inc(sy.dma_start(
                            out=xbuf[b][:, 256 * q:256 * (q + 1)],
                            in_=x_hbm[128 * r:128 * (r + 1), 256 * q:256 * (q + 1)]))

            if 4 <= r < 8:
                k = r - 4
                @block.gpsimd
                def _(g, k=k):
                    if k == 0:
                        dWq.inc(g.dma_start(out=bq_row[:, :], in_=bq_hbm))
                    dWq.inc(g.dma_start(
                        out=wq_sb[:, 2 * k:2 * k + 2, :],
                        in_=wq_hbm.rearrange("(d p) f -> p d f",
                                             p=128)[:, 2 * k:2 * k + 2, :]))
                if r == 7:
                    w_q_done[0] = (dWq, dWq.n)
            if 8 <= r < 12:
                k = r - 8
                @block.gpsimd
                def _(g, k=k):
                    if k == 0:
                        dWo.inc(g.dma_start(out=bo_row[:, :], in_=bo_hbm))
                    dWo.inc(g.dma_start(
                        out=wo_sb[:, 2 * k:2 * k + 2, :],
                        in_=wo_hbm.rearrange("(d p) f -> p d f",
                                             p=128)[:, 2 * k:2 * k + 2, :]))
                if r == 11:
                    w_o_done[0] = (dWo, dWo.n)
            if 12 <= r < 16:
                k = r - 12
                @block.gpsimd
                def _(g, k=k):
                    for d in (2 * k, 2 * k + 1):
                        dW1.inc(g.dma_start(out=w1_sb[:, d, :],
                                            in_=w1_hbm[0, 128 * d:128 * (d + 1), :]))
                if r == 15:
                    w1_arr[(0, 0)] = dW1.n

            sq_plan = cS.n + 1 if r < NT else cS.n

            @block.scalar
            def _(s, r=r):
                if r < NT:
                    b = r % 4
                    bs = r % 4
                    s.wait_ge(dLx[b].h, 64 * (r // 4 + 1))
                    if r >= 4:
                        s.wait_ge(cV.h, rec_done[r - 4])
                    s.activation(sq_scr, xbuf[b], AF.Square,
                                 accum_out=st_ssq[:, bs:bs + 1])
                if 1 <= r <= NT:
                    t = r - 1
                    bb = t % 2
                    tb = t % 4
                    if t >= 2:
                        s.wait_ge(dXn[bb].h, xn_st_d[t - 2])
                        s.wait_ge(cT.h, trp_t[t - 2])
                    s.wait_ge(cV.h, rec_done[t])
                    s.activation(st_ln[:, tb:tb + 1], st_r[:, tb:tb + 1], AF.Ln)
                    s.activation(st_rs[:, tb:tb + 1], st_ln[:, tb:tb + 1], AF.Exp,
                                 scale=-0.5)
                    s.activation(st_nmr[:, tb:tb + 1], st_nm[:, tb:tb + 1], AF.Identity,
                                 scale=st_rs[:, tb:tb + 1])
                    s.activation(xnstage[bb], xbuf[t % 4], AF.Identity,
                                 bias=st_nmr[:, tb:tb + 1], scale=st_rs[:, tb:tb + 1])
                    xn_sca[t] = cS.n
                if 4 <= r < NT + 4:
                    te = r - 4
                    lgps_e = lg_slots[te % 2]
                    s.wait_ge(cV.h, neg_done_v[te])
                    s.activation(probs[:, te, :], lgps_e, AF.Exp,
                                 bias=lmax[:, te:te + 1], accum_out=zrow[:, te:te + 1])
                    lg_sca[te] = cS.n

            if r < NT:
                b = r % 4
                bs = r % 4

                @block.vector
                def _(v, r=r, b=b, bs=bs, sq_done=sq_plan):
                    v.wait_ge(dLx[b].h, 64 * (r // 4 + 1))
                    if r >= 4:
                        v.wait_ge(cS.h, xn_sca[r - 4])
                    v.reduce_sum(st_m[:, bs:bs + 1], xbuf[b], axis=AX.X)
                    v.tensor_scalar_mul(st_m[:, bs:bs + 1], st_m[:, bs:bs + 1], 1.0 / DM)
                    v.tensor_scalar_mul(st_nm[:, bs:bs + 1], st_m[:, bs:bs + 1], -1.0)
                    v.wait_ge(cS.h, sq_done)
                    v.tensor_mul(st_r[:, bs:bs + 1], st_m[:, bs:bs + 1], st_m[:, bs:bs + 1])
                    v.scalar_tensor_tensor(st_r[:, bs:bs + 1], st_ssq[:, bs:bs + 1], 1.0 / DM,
                                           st_r[:, bs:bs + 1], op0=ALU.mult, op1=ALU.subtract)
                    v.tensor_scalar_add(st_r[:, bs:bs + 1], st_r[:, bs:bs + 1], LN_EPS)
                    rec_done[r] = cV.n

            if 1 <= r <= NT:
                t = r - 1
                bb = t % 2

                @block.gpsimd
                def _(g, t=t, bb=bb):
                    g.wait_ge(cS.h, xn_sca[t])
                    dXn[bb].inc(g.dma_start(out=xn_dram.ap()[128 * t:128 * (t + 1), :],
                                            in_=xnstage[bb]))
                xn_st_d[t] = dXn[bb].n

                @block.tensor
                def _(tn, t=t, bb=bb):
                    tn.wait_ge(cS.h, xn_sca[t])
                    if t >= 2:
                        tn.wait_ge(cV.h, xn_done_v[t - 2])
                    pc = PC.bitcast(BF16)[:, 1024 * bb:1024 * (bb + 1)]
                    for c in range(ND):
                        ins = tn.transpose(pc[:, 128 * c:128 * (c + 1)],
                                           xnstage[bb][:, 128 * c:128 * (c + 1)],
                                           idbf[:, :])
                    cT.inc(ins)
                trp_t[t] = cT.n

            if 2 <= r <= NT + 1:
                t = r - 2
                bb = t % 2

                @block.vector
                def _(v, t=t, bb=bb):
                    v.wait_ge(cT.h, trp_t[t])
                    pc = PC.bitcast(BF16)[:, 1024 * bb:1024 * (bb + 1)]
                    v.tensor_copy(
                        xnT[:, :, 128 * t:128 * (t + 1)],
                        pc.rearrange("p (c t) -> p c t", c=ND))
                    xn_done_v[t] = cV.n

                kvps = kv_slots[t % 2]
                lgps = lg_slots[t % 2]

                @block.tensor
                def _(tn, t=t, kvps=kvps, lgps=lgps):
                    tn.wait_ge(dWa.h, w_kv_done[0])
                    tn.wait_ge(cV.h, xn_done_v[t])
                    if t >= 2:
                        tn.wait_ge(cV.h, kv_copy[t - 2])
                        tn.wait_ge(cS.h, lg_sca[t - 2])
                    ts = slice(128 * t, 128 * (t + 1))
                    tn.matmul(kvps, ones_1x128_bf[:, :], bkv_row[:, :],
                              start=True, stop=False)
                    for d in range(ND):
                        ins = tn.matmul(kvps, xnT[:, d, ts], wkv[:, d, :],
                                        start=False, stop=(d == ND - 1))
                    cT.inc(ins)
                    for d in range(ND):
                        ins = tn.matmul(lgps, xnT[:, d, ts], wg_sb[:, d, :],
                                        start=(d == 0), stop=(d == ND - 1))
                    cT.inc(ins)
                lg_mm_t[t] = cT.n

            if 3 <= r <= NT + 2:
                t = r - 3
                kvps = kv_slots[t % 2]
                lgps = lg_slots[t % 2]

                @block.vector
                def _(v, t=t, kvps=kvps, lgps=lgps):
                    v.wait_ge(cT.h, lg_mm_t[t])
                    v.tensor_copy(kvtm[:, t, :], kvps)
                    v.reduce_max(lmax[:, t:t + 1], lgps, axis=AX.X, negate=True)
                neg_done_v[t] = cV.n
                kv_copy[t] = cV.n

                @block.gpsimd
                def _(g, t=t):
                    g.wait_ge(cV.h, kv_copy[t])
                    k2 = kvtm[:, t, 0:64]
                    cA, cB = costab[:, t, 0:16], costab[:, t, 16:32]
                    sA, sB = sintab[:, t, 0:16], sintab[:, t, 16:32]
                    g.tensor_tensor(krs[:, 32:48], k2[:, 16:32], sA, ALU.mult)
                    g.tensor_tensor(krs[:, 0:16], k2[:, 0:16], cA, ALU.mult)
                    g.tensor_sub(krs[:, 0:16], krs[:, 0:16], krs[:, 32:48])
                    g.tensor_tensor(krs[:, 48:64], k2[:, 0:16], sB, ALU.mult)
                    g.tensor_tensor(krs[:, 16:32], k2[:, 16:32], cB, ALU.mult)
                    g.tensor_add(krs[:, 16:32], krs[:, 16:32], krs[:, 48:64])
                    g.tensor_copy(k2[:, 0:32], krs[:, 0:32])
                krope_p = cP.n

                @block.tensor
                def _(tn, t=t, krope_p=krope_p):
                    tn.wait_ge(cP.h, krope_p)
                    if t >= 2:
                        tn.wait_ge(cV.h, kt_v[t - 2])
                    ktp = PD[0:64, 0:128] if t % 2 == 0 else PD[0:64, 512:640]
                    cT.inc(tn.transpose(ktp, kvtm[:, t, 0:64], idf32[:, :]))
                ktr_t[t] = cT.n

            if 4 <= r <= NT + 3:
                t = r - 4

                @block.vector
                def _(v, t=t):
                    v.wait_ge(cT.h, ktr_t[t])
                    ktp = PD[0:64, 0:128] if t % 2 == 0 else PD[0:64, 512:640]
                    v.tensor_copy(kT[0:64, 128 * t:128 * (t + 1)], ktp)
                    kt_v[t] = cV.n

        ln_last_v = cV.n

        @block.sync
        def _(sy):
            for bq4 in range(4):
                sy.wait_ge(dLx[bq4].h, dLx[bq4].n)
            sy.wait_ge(dWq.h, dWq.n)
            dStash.inc(sy.dma_start(out=wq_dram.ap(), in_=regR[:, 0:8192]))
            sy.wait_ge(dWo.h, dWo.n)
            dStash.inc(sy.dma_start(out=wo_dram.ap(), in_=regR[:, 8192:16384]))
            for k in range(E * 512 // 128):
                dPre.inc(sy.dma_start(
                    out=xg_dram.ap()[128 * k:128 * (k + 1), :],
                    in_=regR[:, 0:1024]))

        if "xnT" in DBG:
            @block.sync
            def _(sy):
                sy.wait_ge(cV.h, ln_last_v)
                dDbg.inc(sy.dma_start(out=dbg_hbm["xnT"], in_=xnT[:, :, :]))

        def emit_rope(v, view3, nh, t):
            cosA = costab_bf[:, t:t + 1, 0:16].broadcast_to([128, nh, 16])
            cosB = costab_bf[:, t:t + 1, 16:32].broadcast_to([128, nh, 16])
            sinA = sintab_bf[:, t:t + 1, 0:16].broadcast_to([128, nh, 16])
            sinB = sintab_bf[:, t:t + 1, 16:32].broadcast_to([128, nh, 16])
            scr = rope_scr[:, 0:nh, :]
            v.tensor_tensor(scr[:, :, 32:48], view3[:, :, 16:32], sinA, ALU.mult)
            v.tensor_tensor(scr[:, :, 0:16], view3[:, :, 0:16], cosA, ALU.mult)
            v.tensor_sub(scr[:, :, 0:16], scr[:, :, 0:16], scr[:, :, 32:48])
            v.tensor_tensor(scr[:, :, 48:64], view3[:, :, 0:16], sinB, ALU.mult)
            v.tensor_tensor(scr[:, :, 16:32], view3[:, :, 16:32], cosB, ALU.mult)
            v.tensor_add(scr[:, :, 16:32], scr[:, :, 16:32], scr[:, :, 48:64])
            v.tensor_copy(view3[:, :, 0:32], scr[:, :, 0:32])

        @block.vector
        def _(v):
            v.wait_ge(cS.h, lg_sca[NT - 1])
            v.reciprocal(zrow[:, :], zrow[:, :])
            zB = zrow[:, :].rearrange("p (a e) -> p a e", e=1).broadcast_to([128, NT, E])
            v.tensor_tensor(probs[:, :, :], probs[:, :, :], zB, ALU.mult)
            v.tensor_copy(vstat[:, :, 0:64], kvtm[:, :, 64:128])

        @block.vector
        def _(v):
            v.wait_ge(cV.h, kt_v[NT - 1])
            v.tensor_copy(kT[64:128, :], kT[0:64, :])
            v.tensor_copy(probs_em[:, :, :].rearrange("p e a -> p a e"),
                          probs[:, :, :])

        kv_last_v = cV.n
        probs_last_s = cS.n

        @block.gpsimd
        def _(g):
            g.wait_ge(cV.h, kv_last_v)
            g.memset(stricL, 1.0)
            g.wait_ge(cP.h, cP.n)
            g.affine_select(out=stricL, in_=stricL, compare_op=ALU.is_gt,
                            fill=0.0, base=0, pattern=[[1, 128]], channel_multiplier=-1)
            g.iota(eidx8, [[1, E]], channel_multiplier=0,
                   allow_small_or_imprecise_dtypes=True)
        sidx_p = cP.n

        @block.sync
        def _(sy):
            sy.wait_ge(cV.h, kv_last_v)
            dPr.inc(sy.dma_start(out=ag_in.ap(), in_=probs_em[:, :, :]))

        @block.gpsimd
        def _(g):
            g.wait_ge(dPr.h, 16)
            ccc.inc(g.collective_compute(
                "AllGather", ALU.bypass,
                replica_groups=[list(range(NCORES))],
                ins=[ag_in.ap()], outs=[ag_out.ap()]))

        @block.sync
        def _(sy):
            sy.wait_ge(ccc.h, 1)
            for e in range(E):
                dPr.inc(sy.dma_start(
                    out=probs_all[:, e, :].rearrange("p (r a) -> p r a", r=NCORES),
                    in_=ag_out.ap()[:, :, e:e + 1, :]
                    .rearrange("r p e a -> p r (e a)")))
        pall_done = dPr.n

        for name, gate in (("kT", kv_last_v), ("vstat", kv_last_v),
                           ("kvtm", kv_last_v), ("probs", kv_last_v)):
            if name in DBG:
                src = {"kT": kT[:, :], "vstat": vstat[:, :, :],
                       "kvtm": kvtm[:, :, :], "probs": probs[:, :, :]}[name]

                @block.sync
                def _(sy, name=name, src=src, gate=gate):
                    sy.wait_ge(cV.h, gate)
                    dDbg.inc(sy.dma_start(out=dbg_hbm[name], in_=src))
        if "probs_all" in DBG:
            @block.sync
            def _(sy):
                sy.wait_ge(dPr.h, pall_done)
                dDbg.inc(sy.dma_start(out=dbg_hbm["probs_all"], in_=probs_all[:, :, :]))

        cnt_ps = PD[:, 512:520]

        class Bisect:
            def __init__(self):
                self.started = 0
                self.done = 0
                self.stage = "idle"
                self.red_cv = 0
                self.mm_ct = 0
                self.first = True
                self.enabled = False

            def pre(self, v):
                if not self.enabled:
                    return
                if self.stage != "idle" or self.started >= BIS_ITERS:
                    return
                if self.first:
                    v.wait_ge(dPr.h, pall_done)
                    self.first = False
                tauB = tau[:, :].rearrange("p (e a) -> p e a", a=1) \
                    .broadcast_to([128, E, 128])
                with nc.allow_low_precision(reason="counts <=128 exact in bf16"):
                    v.tensor_tensor(cmpb[:, :, :], probs_all[:, :, :], tauB,
                                    ALU.is_ge)
                    v.reduce_sum(cntpp[:, :], cmpb[:, :, :], axis=AX.X)
                self.red_cv = cV.n
                self.started += 1
                self.stage = "cmp"

            def mm(self, tn):
                if not self.enabled or self.stage != "cmp":
                    return
                tn.wait_ge(cV.h, self.red_cv)
                cT.inc(tn.matmul(cnt_ps, ones_128_bf[:, :], cntpp[:, :],
                                 start=True, stop=True))
                self.mm_ct = cT.n
                self.stage = "mm"

            def upd(self, v):
                if not self.enabled or self.stage != "mm":
                    return
                v.wait_ge(cT.h, self.mm_ct)
                v.tensor_single_scalar(geb[:, :], cnt_ps, CAP, ALU.is_ge)
                v.tensor_single_scalar(geb_inv[:, :], cnt_ps, CAP, ALU.is_lt)
                v.copy_predicated(lo[:, :], geb[:, :], tau[:, :])
                v.copy_predicated(hi[:, :], geb_inv[:, :], tau[:, :])
                v.tensor_add(tau[:, :], lo[:, :], hi[:, :])
                v.tensor_scalar_mul(tau[:, :], tau[:, :], 0.5)
                self.done += 1
                self.stage = "idle"

        bis = Bisect()
        gates_done_v = [0]

        def emit_gates(v):
            loB = lo[:, :].rearrange("p (a e) -> p a e", a=1).broadcast_to([128, NT, E])
            v.tensor_tensor(gsel[:, :, :], probs[:, :, :], loB, ALU.is_ge)
            v.scalar_tensor_tensor(gsel[:, :, :], gsel[:, :, :], 2.0, probs[:, :, :],
                                   op0=ALU.mult, op1=ALU.add)
            v.reduce_max(gmax[:, :], gsel[:, :, :], axis=AX.X)
            gmB = gmax[:, :].rearrange("p (a e) -> p a e", e=1).broadcast_to([128, NT, E])
            v.tensor_tensor(gsel[:, :, :], gsel[:, :, :], gmB, ALU.is_equal)
            v.tensor_mul(gates[:, :, :], gsel[:, :, :], probs[:, :, :])
            gates_done_v[0] = cV.n

        qev_s = {}
        qtr_t = {}
        last_exp_pair = [0]
        qevq_v = {}
        oev_v = {}
        last_S_mm = [0]
        attn_tt_v = {}
        zi_evac_v = {}
        zi_mm_last = [0]
        out_store_d = {}
        w2_arr = {}
        w1_last_ct = {}
        w2_last_ct = {}
        stt_v = {}

        wq_view = [None]

        def emit_qproj(w):
            wqv = wq_view[0] if wq_view[0] is not None else wq_sb
            for t in range(4 * w, 4 * w + 4):
                b = t % 2
                qps = PA if b == 0 else PB
                tps = PC if b == 0 else PD

                @block.tensor
                def _(tn, t=t, qps=qps, w=w):
                    tn.wait_ge(w_q_done[0][0].h, w_q_done[0][1])
                    if w == 0 and t < 2:
                        tn.wait_ge(cS.h, probs_last_s)
                    prev = qev_s.get(t - 2)
                    if prev is not None:
                        tn.wait_ge(cS.h, prev)
                    prevo = oev_v.get(t - 2)
                    if prevo is not None:
                        tn.wait_ge(cV.h, prevo)
                    ts = slice(128 * t, 128 * (t + 1))
                    tn.matmul(qps[:, 0:512], ones_1x128_bf[:, :], bq_row[:, 0:512],
                              start=True, stop=False)
                    tn.matmul(qps[:, 512:1024], ones_1x128_bf[:, :], bq_row[:, 512:1024],
                              start=True, stop=False)
                    for d in range(ND):
                        i0 = tn.matmul(qps[:, 0:512], xnT[:, d, ts], wqv[:, d, 0:512],
                                       start=False, stop=(d == ND - 1))
                        i1 = tn.matmul(qps[:, 512:1024], xnT[:, d, ts],
                                       wqv[:, d, 512:1024],
                                       start=False, stop=(d == ND - 1))
                    cT.inc(i0)
                    cT.inc(i1)
                qmm = cT.n

                @block.scalar
                def _(s, t=t, b=b, qps=qps, qmm=qmm, w=w):
                    s.wait_ge(cT.h, qmm)
                    prev = qtr_t.get(t - 2)
                    if prev is not None:
                        s.wait_ge(cT.h, prev)
                    if w == 0 and t < 2:
                        s.wait_ge(cV.h, ln_last_v)
                    s.activation(q_tm[b], qps[:, :], AF.Identity)
                    qev_s[t] = cS.n

                @block.vector
                def _(v, t=t, b=b, w=w):
                    v.wait_ge(cS.h, qev_s[t])
                    emit_rope(v, q_tm3[b], 16, t)
                    v.engine_nop()
                qrope = cV.n

                @block.tensor
                def _(tn, t=t, b=b, tps=tps, qrope=qrope, w=w):
                    tn.wait_ge(cV.h, qrope)
                    prev = qevq_v.get(t - 2)
                    if prev is not None:
                        tn.wait_ge(cV.h, prev)
                    if w == 0 and t < 2:
                        tn.wait_ge(cV.h, kt_v[NT - 1])
                    tpsb = tps[:, :].bitcast(BF16)[:, 0:1024]
                    for c in range(ND):
                        ins = tn.transpose(tpsb[:, 128 * c:128 * (c + 1)],
                                           q_tm[b][:, 128 * c:128 * (c + 1)],
                                           idbf[:, :])
                    cT.inc(ins)
                    qtr_t[t] = cT.n

                @block.vector
                def _(v, t=t, b=b, tps=tps, w=w):
                    v.wait_ge(cT.h, qtr_t[t])
                    if last_S_mm[0]:
                        v.wait_ge(cT.h, last_S_mm[0])
                    toff = 128 * (t - 4 * w)
                    tpsb = tps[:, :].bitcast(BF16)[:, 0:1024]
                    v.tensor_copy(
                        qTq[:, :].rearrange("p (c u) -> p c u", c=ND)[:, :, toff:toff + 128],
                        tpsb.rearrange("p (c u) -> p c u", c=ND))
                    qevq_v[t] = cV.n

        def emit_heads(w, pair_hook=None):
            nj = 4 * w + 4
            pend_mm = []
            pend_fin = []
            rz_free_ct = [0]

            def c0_of(j):
                k = j - 4 * w
                return 128 * k if k > 0 else 0

            def inject_znorm(tn):
                if pend_mm:
                    pw, ph, hh, rzc = pend_mm.pop(0)
                    tn.wait_ge(cS.h, rzc)
                    prev_zi = zi_evac_v.get(("z", pw, ph - 1)) or zi_evac_v.get("last")
                    if prev_zi:
                        tn.wait_ge(cV.h, prev_zi)
                    cT.inc(tn.matmul(PD[0:64, 0:512], ones_1x128_bf[:, 0:64],
                                     rz_pair[:, 512 * hh:512 * (hh + 1)],
                                     start=True, stop=True))
                    pend_fin.append((pw, ph, hh, cT.n))
                    rz_free_ct[0] = cT.n

            def finish_znorm(v):
                if pend_fin:
                    pw, ph, hh, zic = pend_fin.pop(0)
                    v.wait_ge(cT.h, zic)
                    v.tensor_copy(zi_sb[:, :], PD[0:64, 0:512])
                    po = PC[0:65, 512 * hh:512 * (hh + 1)]
                    v.tensor_tensor(
                        attnTq[(ph % 2) * 64:(ph % 2) * 64 + 64,
                               512 * (ph // 2):512 * (ph // 2) + 512],
                        po[0:64, :], zi_sb[:, :], ALU.mult)
                    attn_tt_v[(pw, ph)] = cV.n
                    zi_evac_v["last"] = cV.n

            jlist = list(range(nj - 1, -1, -1))

            prev_pvl = [0]
            for hp in range(H // 2):
                h0, h1 = 2 * hp, 2 * hp + 1
                ops = [PC[0:65, 0:512], PC[0:65, 512:1024]]
                spair = [(PA[:, 0:512], PA[:, 512:1024]),
                         (PB[:, 0:512], PB[:, 512:1024])]
                exp_cs = {}
                mask_cv = {}
                pv_ct = {}

                @block.vector
                def _(v, hp=hp, w=w):
                    if w == 0 and hp == 0:
                        for bb in range(2):
                            v.wait_ge(dXn[bb].h, dXn[bb].n)
                    if prev_pvl[0]:
                        v.wait_ge(cT.h, prev_pvl[0])
                    for si, cz in ((0, 384), (1, 256), (2, 128)):
                        v.memset(pbuf[si][:, :].rearrange(
                            "p (h c) -> p h c", h=2)[:, :, 0:cz], 0.0)
                    finish_znorm(v)
                    bis.upd(v)
                    bis.pre(v)
                pb_zero = cV.n

                planned = {}
                pn = cS.n
                for i in range(nj):
                    j = jlist[i]
                    pn += 1 if c0_of(j) == 0 else 2
                    planned[j] = pn
                rzc_planned = pn + 4

                for i0 in range(0, nj, 2):
                    @block.tensor
                    def _(tn, i0=i0, hp=hp, h0=h0, h1=h1, w=w):
                        tn.wait_ge(cV.h, qevq_v[4 * w + 3])
                        if hp == 0 and i0 == 0:
                            tn.wait_ge(cS.h, qev_s[4 * w + 3])
                        if i0 >= 4:
                            for i in (i0 - 4, i0 - 3):
                                jp = jlist[i]
                                wg_ = mask_cv.get(jp)
                                if wg_ is not None:
                                    tn.wait_ge(cV.h, wg_)
                                else:
                                    tn.wait_ge(cS.h, planned[jp])
                                if i == 0 and hp > 0:
                                    pa = attn_tt_v.get((w, 2 * hp - 1))
                                    if pa is not None:
                                        tn.wait_ge(cV.h, pa)
                                for hh in (0, 1):
                                    cT.inc(tn.matmul(
                                        ops[hh],
                                        vstat[:, jp, :],
                                        pbuf[i % 4][:, 512 * hh:512 * (hh + 1)],
                                        start=(jp == jlist[0]), stop=(jp == 0)))
                                pv_ct[jp] = cT.n
                        for i in (i0, i0 + 1):
                            j = jlist[i]
                            if i >= 2:
                                tn.wait_ge(cS.h, planned[jlist[i - 2]])
                            elif hp > 0:
                                tn.wait_ge(cS.h, last_exp_pair[0])
                            c0 = c0_of(j)
                            for hh, h in ((0, h0), (1, h1)):
                                kbase = (h % 2) * 64
                                mv = qTq[(h % 2) * 64:(h % 2) * 64 + 64,
                                         512 * (h // 2):512 * (h // 2) + 512]
                                cT.inc(tn.matmul(
                                    spair[j % 2][hh][:, c0:512] if c0 else spair[j % 2][hh],
                                    kT[kbase:kbase + 64, 128 * j:128 * (j + 1)],
                                    mv[:, c0:512] if c0 else mv,
                                    start=True, stop=True))
                            exp_cs[("smm", j)] = cT.n
                        inject_znorm(tn)
                        bis.mm(tn)

                    @block.vector
                    def _(v, i0=i0, hp=hp, w=w):
                        finish_znorm(v)
                        bis.upd(v)
                        bis.pre(v)
                        for i in (i0, i0 + 1):
                            j = jlist[i]
                            if j - 4 * w >= 0:
                                c0 = c0_of(j)
                                v.wait_ge(cS.h, planned[j])
                                pv2 = pbuf[i % 4][:, :].rearrange(
                                    "p (h c) -> p h c", h=2)[:, :, c0:c0 + 128]
                                tm = trimask[:, :].rearrange(
                                    "p (a c) -> p a c", a=1).broadcast_to([128, 2, 128])
                                v.tensor_tensor(pv2, pv2, tm, ALU.mult)
                                mask_cv[j] = cV.n

                for t0 in (nj, nj + 2):
                    @block.tensor
                    def _(tn, t0=t0, hp=hp, w=w):
                        for i in (t0 - 4, t0 - 3):
                            jp = jlist[i]
                            wg_ = mask_cv.get(jp)
                            if wg_ is not None:
                                tn.wait_ge(cV.h, wg_)
                            else:
                                tn.wait_ge(cS.h, planned[jp])
                            if i == 0 and hp > 0:
                                pa = attn_tt_v.get((w, 2 * hp - 1))
                                if pa is not None:
                                    tn.wait_ge(cV.h, pa)
                            for hh in (0, 1):
                                cT.inc(tn.matmul(
                                    ops[hh],
                                    vstat[:, jp, :],
                                    pbuf[i % 4][:, 512 * hh:512 * (hh + 1)],
                                    start=(jp == jlist[0]), stop=(jp == 0)))
                            pv_ct[jp] = cT.n
                pvl = cT.n
                prev_pvl[0] = cT.n
                last_exp_pair[0] = planned[0]

                @block.scalar
                def _(s, hp=hp, w=w, pb_zero=pb_zero, pvl=pvl, nj=nj):
                    for i in range(nj):
                        j = jlist[i]
                        c0 = c0_of(j)
                        if i < 2:
                            s.wait_ge(cV.h, pb_zero)
                        s.wait_ge(cT.h, exp_cs[("smm", j)])
                        if i >= 4:
                            pc = pv_ct.get(jlist[i - 4])
                            if pc is not None:
                                s.wait_ge(cT.h, pc)
                        if c0 == 0:
                            s.activation(pbuf[i % 4][:, :],
                                         PA[:, :] if j % 2 == 0 else PB[:, :],
                                         AF.Exp, scale=0.125, bias=negc[:, :])
                        else:
                            for hh in (0, 1):
                                s.activation(pbuf[i % 4][:, 512 * hh + c0:512 * (hh + 1)],
                                             spair[j % 2][hh][:, c0:512],
                                             AF.Exp, scale=0.125, bias=negc[:, :])
                    s.wait_ge(cT.h, pvl)
                    if rz_free_ct[0]:
                        s.wait_ge(cT.h, rz_free_ct[0])
                    for hh in (0, 1):
                        s.activation(zln_sb[:, :], PC[64:65, 512 * hh:512 * (hh + 1)],
                                     AF.Ln)
                        with nc.allow_low_precision(reason="1/Z bf16; attnT is bf16"):
                            s.activation(rz_pair[:, 512 * hh:512 * (hh + 1)],
                                         zln_sb[:, :], AF.Exp, scale=-1.0)
                assert cS.n == rzc_planned, (cS.n, rzc_planned)
                rzc = cS.n
                pend_mm.append((w, h0, 0, rzc))
                pend_mm.append((w, h1, 1, rzc))
                if pair_hook:
                    pair_hook()

            for _ in range(2):
                @block.tensor
                def _(tn):
                    inject_znorm(tn)

                @block.vector
                def _(v):
                    finish_znorm(v)
            last_S_mm[0] = cT.n

        def emit_oproj(w):
            half = w // 2
            att_done = cV.n
            for t in range(4 * w, 4 * w + 4):
                b = t % 2
                ops = PA if b == 0 else PB
                th = t - 8 * half

                @block.tensor
                def _(tn, t=t, ops=ops, w=w, att_done=att_done):
                    tn.wait_ge(w_o_done[0][0].h, w_o_done[0][1])
                    tn.wait_ge(cV.h, att_done)
                    prev = oev_v.get(t - 2)
                    if prev is not None:
                        tn.wait_ge(cV.h, prev)
                    toff = 128 * (t - 4 * w)
                    tn.matmul(ops[:, 0:512], ones_1x128_bf[:, :], bo_row[:, 0:512],
                              start=True, stop=False)
                    tn.matmul(ops[:, 512:1024], ones_1x128_bf[:, :], bo_row[:, 512:1024],
                              start=True, stop=False)
                    for c in range(ND):
                        stat = attnTq[:, 512 * c + toff:512 * c + toff + 128]
                        i0 = tn.matmul(ops[:, 0:512], stat, wo_sb[:, c, 0:512],
                                       start=False, stop=(c == ND - 1))
                        i1 = tn.matmul(ops[:, 512:1024], stat, wo_sb[:, c, 512:1024],
                                       start=False, stop=(c == ND - 1))
                    cT.inc(i0)
                    cT.inc(i1)
                omm = cT.n

                @block.vector
                def _(v, t=t, ops=ops, omm=omm, th=th, half=half):
                    v.wait_ge(cT.h, omm)
                    if half == 1:
                        prev_store = out_store_d.get((0, th))
                        if prev_store is not None:
                            v.wait_ge(dOutT[th].h, prev_store)
                    v.tensor_copy(acc[:, th, :], ops[:, :])
                    oev_v[t] = cV.n

        sp = {}

        def emit_indices_and_scatter():
            @block.vector
            def _(v):
                v.tensor_copy(flagsF.rearrange("p (e a) -> p e a", e=E),
                              gsel[:, :, :].rearrange("p a e -> p e a"))
                with nc.allow_low_precision(reason="0/1 flags exact in bf16"):
                    v.tensor_copy(flagsE, flagsF)
                v.tensor_scalar_mul(e512[:, :], eidx8, 512.0)
            fl_v = cV.n

            @block.tensor
            def _(tn):
                tn.wait_ge(cV.h, fl_v)
                tn.wait_ge(cP.h, sidx_p)
                tn.wait_ge(cS.h, last_exp_pair[0])
                fE = flagsE.rearrange("p (e a) -> p e a", e=E)
                for e in range(E):
                    cT.inc(tn.matmul(PA[:, 16 * e:16 * (e + 1)], stricL,
                                     fE[:, e, :], start=True, stop=True))
                cT.inc(tn.matmul(PB[:, 0:128], ones_128_bf[:, :], flagsE,
                                 start=True, stop=True))
            idx_mm = cT.n

            @block.vector
            def _(v):
                v.wait_ge(cT.h, idx_mm)
                v.tensor_copy(slotsel, PB[:, 0:128])
                cp3 = colpfx_sb.rearrange("p (e a) -> p e a", e=E)
                cs3 = slotsel.rearrange("p (e a) -> p e a", e=E)
                for e in range(E):
                    v.tensor_tensor_scan(cp3[:, e, :], cs3[:, e, :], cs3[:, e, :],
                                         0.0, op0=ALU.add, op1=ALU.bypass)
                v.tensor_sub(colpfx_sb, colpfx_sb, slotsel)
                v.tensor_add(slotsel, PA[:, 0:128], colpfx_sb)
                v.tensor_mul(slotsel, slotsel, flagsF)
                sv = slotsel.rearrange("p (e a) -> p a e", e=E)
                v.reduce_sum(slot_f, sv, axis=AX.X)
                v.tensor_scalar_min(slot_f, slot_f, 511.0)
                e5B = e512[:, :].rearrange("p (e a) -> p e a", a=1).broadcast_to([128, E, 16])
                v.tensor_tensor(slotsel.rearrange("p (e a) -> p e a", e=E),
                                flagsF.rearrange("p (e a) -> p e a", e=E),
                                e5B, ALU.mult)
                v.reduce_sum(e_f, sv, axis=AX.X)
                v.tensor_add(inv_f, slot_f, e_f)
                v.tensor_copy(inv_i32, inv_f)
                v.reduce_max(g_tok, gates[:, :, :], axis=AX.X)
            sp["idx_v"] = cV.n

            for t in range(NT):
                b = t % 2

                @block.sync
                def _(sy, t=t, b=b):
                    if t < 2:
                        for bb in range(2):
                            sy.wait_ge(dXn[bb].h, dXn[bb].n)
                    if t >= 2:
                        sy.wait_ge(dSc[b].h, 16 * (t // 2))
                    dXs[b].inc(sy.dma_start(out=xsc_sb[b],
                                            in_=xn_dram.ap()[128 * t:128 * (t + 1), :]))

                @block.gpsimd
                def _(g, t=t, b=b):
                    g.wait_ge(dXs[b].h, dXs[b].n)
                    if t == 0:
                        g.wait_ge(cV.h, sp["idx_v"])
                        g.wait_ge(dPre.h, dPre.n)
                    dSc[b].inc(g.indirect_dma_start(
                        out=xg_dram.ap(),
                        out_offset=IndirectOffsetOnAxis(inv_i32[:, t:t + 1], 0),
                        in_=xsc_sb[b],
                        in_offset=None))
            sp["sc0"] = dSc[0].n
            sp["sc1"] = dSc[1].n

        def emit_sparse_ffn():
            tr_v = {}
            last_fg = {}
            gelu_s = {}
            w2_v = {}
            st_d = {}
            hps4 = [PA[:, 0:512], PA[:, 512:1024], PB[:, 0:512], PB[:, 512:1024]]
            xgT3 = xgT.rearrange("p (d u) -> p d u", d=ND)
            def emit_xg_load(e):
                @block.sync
                def _(sy, e=e):
                    if e == 0:
                        sy.wait_ge(dSc[0].h, sp["sc0"])
                        sy.wait_ge(dSc[1].h, sp["sc1"])
                    for a in range(4):
                        if e > 0:
                            sy.wait_ge(cT.h, tr_v[("mm", e - 1, a)])
                        dXgA[a].inc(sy.dma_start(
                            out=xg_sb[:, 1024 * a:1024 * (a + 1)],
                            in_=xg_dram.ap()[512 * e + 128 * a:
                                             512 * e + 128 * (a + 1), :]))
                        sp[("xg", e, a)] = dXgA[a].n

            emit_xg_load(0)
            for e in range(E):
                for a in range(4):
                    @block.tensor
                    def _(tn, e=e, a=a):
                        tn.wait_ge(dXgA[a].h, sp[("xg", e, a)])
                        if a == 0 and e == 0:
                            tn.wait_ge(cV.h, sp["h1_znorm_cv"])
                        pcb = ((PC, PD)[a % 2])[:, :].bitcast(BF16)[:, 0:1024]
                        prev = tr_v.get((e, a - 2))
                        if prev is not None:
                            tn.wait_ge(cV.h, prev)
                        if a < 2 and e > 0:
                            prev = w2_v.get((e - 1, 2 + a))
                            if prev is not None:
                                tn.wait_ge(cV.h, prev)
                        for c in range(ND):
                            ins = tn.transpose(pcb[:, 128 * c:128 * (c + 1)],
                                               xg_sb[:, 1024 * a + 128 * c:
                                                     1024 * a + 128 * (c + 1)],
                                               idbf[:, :])
                        cT.inc(ins)
                        tr_v[("mm", e, a)] = cT.n

                    @block.vector
                    def _(v, e=e, a=a):
                        v.wait_ge(cT.h, tr_v[("mm", e, a)])
                        pcb = ((PC, PD)[a % 2])[:, :].bitcast(BF16)[:, 0:1024]
                        v.tensor_copy(xgT3[:, :, 128 * a:128 * (a + 1)],
                                      pcb.rearrange("p (c u) -> p c u", c=ND))
                        tr_v[(e, a)] = cV.n
                sp[("tr", e)] = cT.n
                tr_all = cV.n

                g_base = cS.n
                for f in range(NF):
                    gelu_s[(e, f)] = g_base + f + 1
                for f4 in range(0, NF, 4):
                    @block.tensor
                    def _(tn, e=e, f4=f4, tr_all=tr_all):
                        tn.wait_ge(dW1.h, w1_arr[(0, e)])
                        tn.wait_ge(cV.h, tr_all)
                        for f in range(f4, f4 + 4):
                            if f - 4 >= 0:
                                tn.wait_ge(cS.h, gelu_s[(e, f - 4)])
                            for d in range(ND):
                                ins = tn.matmul(hps4[f % 4],
                                                w1_sb[:, d, 128 * f:128 * (f + 1)],
                                                xgT3[:, d, :],
                                                start=(d == 0), stop=(d == ND - 1))
                            cT.inc(ins)
                            gelu_s[("mm", e, f)] = cT.n
                sp[("w1", e)] = cT.n
                if e + 1 < E:
                    emit_xg_load(e + 1)

                @block.scalar
                def _(s, e=e):
                    for f in range(NF):
                        s.wait_ge(cT.h, gelu_s[("mm", e, f)])
                        s.activation(hG[:, 512 * f:512 * (f + 1)],
                                     hps4[f % 4], AF.Gelu)
                assert cS.n == gelu_s[(e, NF - 1)], (cS.n, e)

                nxt = e + 1
                if nxt < E:
                    @block.gpsimd
                    def _(g, e=e, nxt=nxt):
                        g.wait_ge(cT.h, sp[("w1", e)])
                        for d in range(ND):
                            dW1.inc(g.dma_start(out=w1_sb[:, d, :],
                                                in_=w1_hbm[nxt, 128 * d:128 * (d + 1), :]))
                    w1_arr[(0, nxt)] = dW1.n

                glast = gelu_s[(e, NF - 1)]
                for a in range(4):
                    @block.tensor
                    def _(tn, e=e, a=a, glast=glast):
                        tn.wait_ge(cS.h, glast)
                        tn.wait_ge(dW2.h, w2_arr[(0, e)])
                        if e == 0 and a < 2:
                            tn.wait_ge(cV.h, sp["routing_cv"])
                        wps = PC if a % 2 == 0 else PD
                        prev = w2_v.get((e, a - 2))
                        if prev is not None:
                            tn.wait_ge(cV.h, prev)
                        for f in range(NF):
                            i0 = tn.matmul(wps[:, 0:512],
                                           hG[:, 512 * f + 128 * a:512 * f + 128 * (a + 1)],
                                           w2_sb[:, f, 0:512],
                                           start=(f == 0), stop=(f == NF - 1))
                            i1 = tn.matmul(wps[:, 512:1024],
                                           hG[:, 512 * f + 128 * a:512 * f + 128 * (a + 1)],
                                           w2_sb[:, f, 512:1024],
                                           start=(f == 0), stop=(f == NF - 1))
                        cT.inc(i0)
                        cT.inc(i1)
                        w2_v[("mm", e, a)] = cT.n

                    @block.vector
                    def _(v, e=e, a=a):
                        v.wait_ge(cT.h, w2_v[("mm", e, a)])
                        ls = last_fg.get(a % 2)
                        if ls is not None:
                            v.wait_ge(dFgP[a % 2].h, ls)
                        wps = PC if a % 2 == 0 else PD
                        with nc.allow_low_precision(reason="ffn out staged bf16 like other matmul operands"):
                            v.tensor_copy(ffg2[a % 2], wps[:, :])
                        w2_v[(e, a)] = cV.n

                    @block.sync
                    def _(sy, e=e, a=a):
                        sy.wait_ge(cV.h, w2_v[(e, a)])
                        dFgP[a % 2].inc(sy.dma_start(
                            out=fg_dram.ap()[512 * e + 128 * a:
                                             512 * e + 128 * (a + 1), :],
                            in_=ffg2[a % 2]))
                        last_fg[a % 2] = dFgP[a % 2].n
                sp[("w2", e)] = cT.n

                nxt = e + 1
                if nxt < E:
                    @block.gpsimd
                    def _(g, e=e, nxt=nxt):
                        g.wait_ge(cT.h, sp[("w2", e)])
                        for f in range(NF):
                            dW2.inc(g.dma_start(out=w2_sb[:, f, :],
                                                in_=w2_hbm[nxt, 128 * f:128 * (f + 1), :]))
                    w2_arr[(0, nxt)] = dW2.n
            sp["ffn_ct"] = cT.n
            sp["fg_d0"] = dFgP[0].n
            sp["fg_d1"] = dFgP[1].n

        epi_seq = []
        epi_state = {"pf": 0, "st": 0}
        epi_gath = {}
        add_done = {}

        def epi_push(half):
            for th in range(8):
                epi_seq.append((half, th))

        def epi_prefetch(k=1):
            while k > 0 and epi_state["pf"] < len(epi_seq):
                i = epi_state["pf"]
                if i >= 2 and (i - 2) not in add_done:
                    break
                half, th = epi_seq[i]
                tglob = 8 * half + th
                b = i % 2

                @block.gpsimd
                def _(g, i=i, tglob=tglob, b=b):
                    g.wait_ge(dFgP[0].h, sp["fg_d0"])
                    g.wait_ge(dFgP[1].h, sp["fg_d1"])
                    prev = add_done.get(i - 2)
                    if prev is not None:
                        g.wait_ge(cV.h, prev)
                    dFo[b].inc(g.indirect_dma_start(
                        out=fo_sb[b],
                        out_offset=None,
                        in_=fg_dram.ap(),
                        in_offset=IndirectOffsetOnAxis(inv_i32[:, tglob:tglob + 1], 0)))

                @block.sync
                def _(sy, i=i, tglob=tglob, b=b):
                    sy.wait_ge(cV.h, sp["routing_cv"])
                    sy.wait_ge(dSc[0].h, sp["sc0"])
                    sy.wait_ge(dSc[1].h, sp["sc1"])
                    prev = add_done.get(i - 2)
                    if prev is not None:
                        sy.wait_ge(cV.h, prev)
                    for q in range(4):
                        dXr[b].inc(sy.dma_start(
                            out=xrel[b][:, 256 * q:256 * (q + 1)],
                            in_=x_hbm[128 * tglob:128 * (tglob + 1),
                                      256 * q:256 * (q + 1)]))
                epi_gath[i] = (dFo[b].n, dXr[b].n)
                epi_state["pf"] += 1
                k -= 1

        def epi_step():
            i = epi_state["st"]
            if i >= len(epi_seq) or i >= epi_state["pf"]:
                return False
            half, th = epi_seq[i]
            tglob = 8 * half + th
            if oev_v.get(tglob) is None:
                return False
            b = i % 2
            fod, xrd = epi_gath[i]

            @block.vector
            def _(v, th=th, tglob=tglob, b=b, fod=fod, xrd=xrd):
                v.wait_ge(dFo[b].h, fod)
                v.wait_ge(dXr[b].h, xrd)
                v.wait_ge(cV.h, oev_v[tglob])
                v.scalar_tensor_tensor(acc[:, th, :], fo_sb[b],
                                       g_tok[:, tglob:tglob + 1], acc[:, th, :],
                                       op0=ALU.mult, op1=ALU.add)
                v.tensor_add(acc[:, th, :], acc[:, th, :], xrel[b])
            add_done[i] = cV.n

            @block.sync
            def _(sy, half=half, th=th, tglob=tglob, adv=add_done[i]):
                sy.wait_ge(cV.h, adv)
                dOutT[th].inc(sy.dma_start(
                    out=out_hbm[128 * tglob:128 * (tglob + 1), :],
                    in_=acc[:, th, :]))
                out_store_d[(half, th)] = dOutT[th].n
            epi_state["st"] += 1
            epi_prefetch(1)
            return True

        def epi_hook():
            epi_step()

        emit_qproj(0)
        emit_heads(0)
        emit_oproj(0)
        bis.enabled = True
        emit_qproj(1)
        emit_heads(1)
        sp["h1_znorm_cv"] = cV.n

        while bis.done < BIS_ITERS:
            @block.vector
            def _(v):
                bis.upd(v)
                bis.pre(v)

            @block.tensor
            def _(tn):
                bis.mm(tn)

        @block.vector
        def _(v):
            bis.upd(v)
            emit_gates(v)
        sp["routing_cv"] = cV.n

        if "tau" in DBG:
            @block.sync
            def _(sy):
                sy.wait_ge(cV.h, gates_done_v[0])
                dDbg.inc(sy.dma_start(out=dbg_hbm["tau"], in_=lo[:, :]))
        if "gates" in DBG:
            @block.sync
            def _(sy):
                sy.wait_ge(cV.h, gates_done_v[0])
                dDbg.inc(sy.dma_start(out=dbg_hbm["gates"], in_=gates[:, :, :]))

        emit_indices_and_scatter()
        emit_oproj(1)

        oproj_h0_ct = cT.n

        @block.gpsimd
        def _(g):
            g.wait_ge(cT.h, oproj_h0_ct)
            g.wait_ge(dStash.h, dStash.n)
            for f in range(NF):
                dW2.inc(g.dma_start(out=w2_sb[:, f, :],
                                    in_=w2_hbm[0, 128 * f:128 * (f + 1), :]))
        w2_arr[(0, 0)] = dW2.n

        emit_sparse_ffn()

        @block.gpsimd
        def _(g):
            g.wait_ge(dStash.h, dStash.n)
            g.wait_ge(cT.h, sp[("w1", E - 1)])
            for k in range(4):
                dWq2.inc(g.dma_start(out=regB[:, 2048 * k:2048 * (k + 1)],
                                     in_=wq_dram.ap()[:, 2048 * k:2048 * (k + 1)]))

        @block.gpsimd
        def _(g):
            g.wait_ge(cT.h, sp["ffn_ct"])
            for k in range(4):
                dWo2.inc(g.dma_start(out=regR[:, 8192 + 2048 * k:
                                              8192 + 2048 * (k + 1)],
                                     in_=wo_dram.ap()[:, 2048 * k:2048 * (k + 1)]))
        w_q_done[0] = (dWq2, dWq2.n)
        w_o_done[0] = (dWo2, dWo2.n)
        wq_view[0] = wq_sb2

        epi_push(0)
        epi_prefetch(2)
        emit_qproj(2)
        emit_heads(2, pair_hook=epi_hook)
        while epi_state["st"] < 8:
            epi_prefetch(1)
            assert epi_step(), "half-0 epilogue pump stalled"
        emit_oproj(2)
        epi_push(1)
        epi_prefetch(1)
        emit_qproj(3)
        emit_heads(3, pair_hook=epi_hook)
        emit_oproj(3)
        guard = 0
        while epi_state["st"] < 16:
            epi_prefetch(1)
            epi_step()
            guard += 1
            assert guard < 200, "epilogue drain stalled"

        @block.sync
        def _(sy):
            for th in range(8):
                sy.wait_ge(dOutT[th].h, dOutT[th].n)
            if dDbg.n:
                sy.wait_ge(dDbg.h, dDbg.n)

    return nc


def kernel(**inputs):
    x = np.ascontiguousarray(np.asarray(inputs["L0_attn_x"], dtype=np.float32))
    wq = np.ascontiguousarray(np.asarray(inputs["L0_attn_Wq"], dtype=np.float32))
    bq = np.ascontiguousarray(np.asarray(inputs["L0_attn_bq"], dtype=np.float32)).reshape(1, DM)
    wk = np.ascontiguousarray(np.asarray(inputs["L0_attn_Wk"], dtype=np.float32))
    bk = np.ascontiguousarray(np.asarray(inputs["L0_attn_bk"], dtype=np.float32)).reshape(1, HD)
    wv = np.ascontiguousarray(np.asarray(inputs["L0_attn_Wv"], dtype=np.float32))
    bv = np.ascontiguousarray(np.asarray(inputs["L0_attn_bv"], dtype=np.float32)).reshape(1, HD)
    wo = np.ascontiguousarray(np.asarray(inputs["L0_attn_Wo"], dtype=np.float32))
    bo = np.ascontiguousarray(np.asarray(inputs["L0_attn_bo"], dtype=np.float32)).reshape(1, DM)
    wg = np.ascontiguousarray(np.asarray(inputs["L0_ffn_Wg"], dtype=np.float32))
    w1 = np.ascontiguousarray(np.asarray(inputs["L0_ffn_W1"], dtype=np.float32))
    w2 = np.ascontiguousarray(np.asarray(inputs["L0_ffn_W2"], dtype=np.float32))

    nc = build_nc()
    in_maps = []
    for c in range(NCORES):
        in_maps.append({
            "x": x[c], "wq": wq, "bq": bq, "wk": wk, "bk": bk, "wv": wv, "bv": bv,
            "wo": wo, "bo": bo, "wg": wg, "w1": w1, "w2": w2,
        })
    res = run_bass_kernel_spmd(nc, in_maps, core_ids=list(range(NCORES)))
    return np.stack([res.results[c]["out"] for c in range(NCORES)], axis=0)


# version 26 (final)
# speedup vs baseline: 1.3810x; 1.0058x over previous
import math
from contextlib import ExitStack

import numpy as np

import concourse.bass as bass
import concourse.mybir as mybir
from concourse.bass_core_rust import IndirectOffsetOnAxis
from concourse.bass_utils import run_bass_kernel_spmd

F32 = mybir.dt.float32
BF16 = mybir.dt.bfloat16
U8 = mybir.dt.uint8
I32 = mybir.dt.int32
AF = mybir.ActivationFunctionType
ALU = mybir.AluOpType
AX = mybir.AxisListType

NCORES = 8
T, DM, H, HD, FF, E = 2048, 1024, 16, 64, 2048, 8
NT, ND, NF = T // 128, DM // 128, FF // 128
CAP = float(T * NCORES // E)
CEXP = 16.0
TWO_PI = 2.0 * math.pi
BIS_ITERS = 40
LN_EPS = 1e-5

DBG = []


class Ctr:
    def __init__(self, h, step=1):
        self.h = h
        self.n = 0
        self.step = step

    def inc(self, instr, k=None):
        k = self.step if k is None else k
        instr.then_inc(self.h, k)
        self.n += k
        return instr



class AutoSync:

    _PASS = ("wait_ge", "dma_start", "dma_start_transpose", "collective_compute",
             "indirect_dma_start", "sem_inc", "wait_op")

    def __init__(self, eng, ctr):
        self._eng = eng
        self._ctr = ctr

    def __getattr__(self, name):
        f = getattr(self._eng, name)
        if name.startswith("_") or name in AutoSync._PASS or not callable(f):
            return f
        eng, ctr = self._eng, self._ctr

        def wrapped(*a, **k):
            eng.wait_ge(ctr.h, ctr.n)
            ins = f(*a, **k)
            ctr.inc(ins)
            return ins

        return wrapped


class BlockWrap:
    def __init__(self, block, cV, cS, cP):
        self._b = block
        self._cV, self._cS, self._cP = cV, cS, cP

    def vector(self, f):
        return self._b.vector(lambda v: f(AutoSync(v, self._cV)))

    def scalar(self, f):
        return self._b.scalar(lambda s: f(AutoSync(s, self._cS)))

    def gpsimd(self, f):
        return self._b.gpsimd(lambda g: f(AutoSync(g, self._cP)))

    def tensor(self, f):
        return self._b.tensor(f)

    def sync(self, f):
        return self._b.sync(f)


def build_nc():
    nc = bass.Bass("TRN2", target_bir_lowering=False)

    x_hbm = nc.dram_tensor("x", [T, DM], F32, kind="ExternalInput").ap()
    wq_hbm = nc.dram_tensor("wq", [DM, DM], F32, kind="ExternalInput").ap()
    bq_hbm = nc.dram_tensor("bq", [1, DM], F32, kind="ExternalInput").ap()
    wk_hbm = nc.dram_tensor("wk", [DM, HD], F32, kind="ExternalInput").ap()
    bk_hbm = nc.dram_tensor("bk", [1, HD], F32, kind="ExternalInput").ap()
    wv_hbm = nc.dram_tensor("wv", [DM, HD], F32, kind="ExternalInput").ap()
    bv_hbm = nc.dram_tensor("bv", [1, HD], F32, kind="ExternalInput").ap()
    wo_hbm = nc.dram_tensor("wo", [DM, DM], F32, kind="ExternalInput").ap()
    bo_hbm = nc.dram_tensor("bo", [1, DM], F32, kind="ExternalInput").ap()
    wg_hbm = nc.dram_tensor("wg", [DM, E], F32, kind="ExternalInput").ap()
    w1_hbm = nc.dram_tensor("w1", [E, DM, FF], F32, kind="ExternalInput").ap()
    w2_hbm = nc.dram_tensor("w2", [E, FF, DM], F32, kind="ExternalInput").ap()
    out_hbm = nc.dram_tensor("out", [T, DM], F32, kind="ExternalOutput").ap()

    ag_in = nc.dram_tensor("ag_in", [128, E, NT], F32)
    wq_dram = nc.dram_tensor("wq_dram", [128, 8192], BF16)
    wo_dram = nc.dram_tensor("wo_dram", [128, 8192], BF16)
    xn_dram = nc.dram_tensor("xn_dram", [T, DM], BF16)
    xg_dram = nc.dram_tensor("xg_dram", [E * 512, DM], BF16)
    fg_dram = nc.dram_tensor("fg_dram", [E * 512, DM], BF16)
    ag_out = nc.dram_tensor("ag_out", [NCORES, 128, E, NT], F32, addr_space="Shared")

    dbg_specs = {
        "xnT": ([128, ND, T], BF16), "kT": ([128, T], BF16),
        "vstat": ([128, NT, 65], BF16), "probs": ([128, NT, E], F32),
        "tau": ([128, E], F32), "gates": ([128, NT, E], F32),
        "qT0": ([128, ND * 512], BF16), "attnT0": ([128, ND * 512], BF16),
        "acc0": ([128, NT // 2, DM], F32),
        "probs_all": ([128, E, 128], F32), "kvtm": ([128, NT, 128], F32),
    }
    dbg_hbm = {}
    for name in DBG:
        shp, dt = dbg_specs[name]
        dbg_hbm[name] = nc.dram_tensor("dbg_" + name, shp, dt, kind="ExternalOutput").ap()

    ctx = ExitStack()
    with ctx:
        def sb(name, shape, dt=F32):
            return ctx.enter_context(nc.sbuf_tensor(name, shape, dt))

        def psum_t(name, shape, dt=F32):
            return ctx.enter_context(nc.psum_tensor(name, shape, dt))

        def sem(name, step=1):
            return Ctr(ctx.enter_context(nc.semaphore(name)), step)

        xnT = sb("xnT", [128, ND, T], BF16)
        acc = sb("acc", [128, NT // 2, DM], F32)
        costab = sb("costab", [128, NT, 32], F32)
        sintab = sb("sintab", [128, NT, 32], F32)
        costab_bf = sb("costab_bf", [128, NT, 32], BF16)
        sintab_bf = sb("sintab_bf", [128, NT, 32], BF16)
        trimask = sb("trimask", [128, 128], BF16)
        idf32 = sb("idf32", [128, 128], F32)
        idbf = sb("idbf", [128, 128], BF16)
        ones_1x128_bf = sb("ones_1x128_bf", [1, 128], BF16)
        ones_1x128_f = sb("ones_1x128_f", [1, 128], F32)
        ones_128_bf = sb("ones_128_bf", [128, 128], BF16)
        negc = sb("negc", [128, 1], F32)
        kT = sb("kT", [128, T], BF16)
        vstat = sb("vstat", [128, NT, 65], BF16)
        kvtm = sb("kvtm", [128, NT, 128], F32)
        probs = sb("probs", [128, NT, E], F32)
        probs_all = sb("probs_all", [128, E, 128], F32)
        probs_em = sb("probs_em", [128, E, NT], F32)
        cmpb = sb("cmpb", [128, E, 128], BF16)
        cntpp = sb("cntpp", [128, E], BF16)
        tau = sb("tau", [128, E], F32)
        lo = sb("lo", [128, E], F32)
        hi = sb("hi", [128, E], F32)
        geb = sb("geb", [128, E], U8)
        geb_inv = sb("geb_inv", [128, E], U8)
        gsel = sb("gsel", [128, NT, E], F32)
        gmax = sb("gmax", [128, NT], F32)
        gates = sb("gates", [128, NT, E], F32)
        wkv = sb("wkv", [128, ND, 128], BF16)
        wg_sb = sb("wg_sb", [128, ND, E], BF16)
        bq_row = sb("bq_row", [1, DM], BF16)
        bkv_row = sb("bkv_row", [1, 128], BF16)
        bo_row = sb("bo_row", [1, DM], BF16)
        rz_pair = sb("rz_pair", [1, 1024], BF16)
        zi_sb = sb("zi_sb", [64, 512], BF16)
        lmax = sb("lmax", [128, NT], F32)
        zrow = sb("zrow", [128, NT], F32)
        st_ssq = sb("st_ssq", [128, 4], F32)
        st_m = sb("st_m", [128, 4], F32)
        st_nm = sb("st_nm", [128, 4], F32)
        st_r = sb("st_r", [128, 4], F32)
        st_rs = sb("st_rs", [128, 4], F32)
        st_nmr = sb("st_nmr", [128, 4], F32)
        st_ln = sb("st_ln", [128, 4], F32)
        tgen = sb("tgen", [128, NT, 32], F32)
        tgen3 = sb("tgen3", [128, NT, 32], F32)
        tgen2 = sb("tgen2", [128, NT, 32], F32)

        tt_iota = sb("tt_iota", [128, NT], F32)
        tgen_i = tgen3[:, :, :].bitcast(I32)
        zln_sb = tgen[0:1, :, :].rearrange("p a b -> p (a b)")
        j_iota = sb("j_iota", [1, 32], F32)
        theta = sb("theta", [1, 32], F32)
        thetaR = sb("thetaR", [128, 32], F32)

        kvflat = kvtm[:, :, :].rearrange("p a b -> p (a b)")
        inv_i32 = kvflat[:, 0:16].bitcast(I32)
        g_tok = kvflat[:, 16:32]
        inv_f = kvflat[:, 32:48]
        slot_f = kvflat[:, 48:64]
        e_f = kvflat[:, 64:80]
        e512 = kvflat[:, 80:88]
        flagsE = kvflat[:, 128:192].bitcast(BF16)
        colpfx_sb = kvflat[:, 192:320]
        slotsel = kvflat[:, 320:448]
        flagsF = kvflat[:, 448:576]
        eidx8 = kvflat[:, 576:584]
        stricL = kvflat[:, 584:648].bitcast(BF16)
        xsc_sb = [kvflat[:, 768 + k * 512:1280 + k * 512].bitcast(BF16)
                  for k in range(2)]

        regA = sb("regA", [128, 16384], BF16)

        def carveA(off_bytes, n_elems, dt):
            if dt == BF16:
                return regA[:, off_bytes // 2: off_bytes // 2 + n_elems]
            return regA[:, off_bytes // 2:(off_bytes + n_elems * 4) // 2].bitcast(dt)

        xbuf = [carveA(0, 1024, F32), carveA(4096, 1024, F32),
                carveA(16384, 1024, F32), carveA(20480, 1024, F32)]
        dLx4 = None
        sq_scr = carveA(8192, 1024, F32)
        xnstage = [carveA(12288 + k * 2048, 1024, BF16) for k in range(2)]

        qTq = carveA(0, ND * 512, BF16)
        attnTq = carveA(8192, ND * 512, BF16)
        q_tm = [carveA(16384 + k * 2048, 1024, BF16) for k in range(2)]
        q_tm3 = [a.rearrange("p (h j) -> p h j", h=16) for a in q_tm]
        rope_scr = carveA(20480, 1024, BF16).rearrange("p (h j) -> p h j", h=16)
        krope_scr = carveA(0, 1024, F32).rearrange("p (h j) -> p h j", h=16)
        pbuf = [carveA(22528 + k * 2048, 1024, BF16) for k in range(4)]

        hT = carveA(0, NF * 1024, BF16)
        xg_sb = carveA(0, 4096, BF16)
        xgT = carveA(8192, 4096, BF16)
        hG = carveA(16384, 8192, BF16)

        ffg2 = [tgen2[:, :, :].rearrange("p a b -> p (a b)").bitcast(BF16),
                tgen3[:, :, :].rearrange("p a b -> p (a b)").bitcast(BF16)]
        fo_sb = ffg2
        xrel = [probs_all[:, :, :].rearrange("p e t -> p (e t)"),
                kvflat[:, 1024:2048]]

        regB = sb("regB", [128, 16384], BF16)
        w1_sb = regB.rearrange("p (d f) -> p d f", d=ND)

        regR = sb("regR", [128, 16384], BF16)
        wq_sb = regR[:, 0:8192].rearrange("p (d f) -> p d f", d=ND)
        wq_sb2 = regB[:, 0:8192].rearrange("p (d f) -> p d f", d=ND)
        wo_sb = regR[:, 8192:16384].rearrange("p (d f) -> p d f", d=ND)
        w2_sb = regR.rearrange("p (f d) -> p f d", f=NF)

        PA = psum_t("PA", [128, 1024], F32)
        PB = psum_t("PB", [128, 1024], F32)
        PC = psum_t("PC", [128, 1024], F32)
        PD = psum_t("PD", [128, 1024], F32)

        dLx = [sem("dLx%d" % i, 16) for i in range(4)]
        dWa = sem("dWa", 16)
        dWq = sem("dWq", 16)
        dWo = sem("dWo", 16)
        dW1 = sem("dW1", 16)
        dW2 = sem("dW2", 16)
        dPr = sem("dPr", 16)
        dXr = [sem("dXr0", 16), sem("dXr1", 16)]
        dOutT = [sem("dOutT%d" % i, 16) for i in range(8)]
        dDbg = sem("dDbg", 16)
        dStash = sem("dStash", 16)
        dWq2 = sem("dWq2", 16)
        dWo2 = sem("dWo2", 16)
        dXn = [sem("dXn0", 16), sem("dXn1", 16)]
        dPre = sem("dPre", 16)
        dXs = [sem("dXs0", 16), sem("dXs1", 16)]
        dSc = [sem("dSc0", 16), sem("dSc1", 16)]
        dXgA = [sem("dXg%d" % a, 16) for a in range(4)]
        dFgP = [sem("dFg0", 16), sem("dFg1", 16)]
        dFo = [sem("dFo0", 16), sem("dFo1", 16)]
        cT = sem("cT")
        cS = sem("cS")
        cV = sem("cV")
        cP = sem("cP")
        ccc = sem("ccc")

        block = BlockWrap(ctx.enter_context(nc.Block()), cV, cS, cP)

        @block.gpsimd
        def _(g):
            g.memset(ones_1x128_bf[:, :], 1.0)
            g.memset(ones_1x128_f[:, :], 1.0)
            g.memset(ones_128_bf[:, :], 1.0)
            g.memset(negc[:, :], -CEXP)
            g.memset(vstat[:, :, :], 1.0)
            g.memset(lo[:, :], 0.0)
            g.memset(hi[:, :], 1.0)
            g.memset(tau[:, :], 0.5)
            g.memset(idf32[:, :], 1.0)
            g.memset(idbf[:, :], 1.0)
            g.memset(trimask[:, :], 1.0)
            g.wait_ge(cP.h, cP.n)
            g.affine_select(out=idf32[:, :], in_=idf32[:, :], compare_op=ALU.is_equal,
                            fill=0.0, base=0, pattern=[[-1, 128]], channel_multiplier=1)
            g.affine_select(out=idbf[:, :], in_=idbf[:, :], compare_op=ALU.is_equal,
                            fill=0.0, base=0, pattern=[[-1, 128]], channel_multiplier=1)
            g.affine_select(out=trimask[:, :], in_=trimask[:, :], compare_op=ALU.is_ge,
                            fill=0.0, base=0, pattern=[[1, 128]], channel_multiplier=-1)
            g.iota(tt_iota[:, :], [[128, NT]], channel_multiplier=1,
                   allow_small_or_imprecise_dtypes=True)
            g.iota(j_iota[:, :], [[1, 32]], channel_multiplier=0,
                          allow_small_or_imprecise_dtypes=True)
            dWa.inc(g.dma_start(out=wkv[:, :, 0:64],
                                in_=wk_hbm.rearrange("(d p) f -> p d f", p=128)))
            dWa.inc(g.dma_start(out=wkv[:, :, 64:128],
                                in_=wv_hbm.rearrange("(d p) f -> p d f", p=128)))
            dWa.inc(g.dma_start(out=wg_sb[:, :, :],
                                in_=wg_hbm.rearrange("(d p) e -> p d e", p=128)))
            dWa.inc(g.dma_start(out=bkv_row[:, 0:64], in_=bk_hbm))
            dWa.inc(g.dma_start(out=bkv_row[:, 64:128], in_=bv_hbm))
        w_kv_done = [dWa.n]

        w_q_done = [None]
        w_o_done = [None]
        w1_arr = {}

        @block.scalar
        def _(s):
            s.wait_ge(cP.h, cP.n)
            s.activation(theta[:, :], j_iota[:, :], AF.Exp,
                                scale=-math.log(10000.0) / 32.0)

        @block.tensor
        def _(t):
            t.wait_ge(cS.h, cS.n)
            cT.inc(t.matmul(PA[:, 0:32], ones_1x128_f[:, :], theta[:, :],
                            start=True, stop=True))

        @block.vector
        def _(v):
            v.wait_ge(cT.h, cT.n)
            v.tensor_copy(thetaR[:, :], PA[:, 0:32])
            for a in range(NT):
                v.tensor_scalar_mul(tgen[:, a, :], thetaR[:, :], tt_iota[:, a:a + 1])
            v.tensor_scalar_mul(tgen[:, :, :], tgen[:, :, :], 1.0 / TWO_PI)
            v.tensor_copy(tgen_i, tgen[:, :, :])
            v.tensor_copy(tgen2[:, :, :], tgen_i)
            v.tensor_sub(tgen2[:, :, :], tgen[:, :, :], tgen2[:, :, :])
            v.tensor_single_scalar(tgen3[:, :, :], tgen2[:, :, :], 0.5, ALU.is_gt)
            v.tensor_sub(tgen2[:, :, :], tgen2[:, :, :], tgen3[:, :, :])

        @block.scalar
        def _(s):
            s.wait_ge(cV.h, cV.n)
            s.activation(sintab[:, :, :], tgen2[:, :, :], AF.Sin, scale=TWO_PI)

        @block.vector
        def _(v):
            v.wait_ge(cS.h, cS.n)
            v.tensor_scalar_add(tgen[:, :, :], tgen[:, :, :], 0.25)
            v.tensor_copy(tgen_i, tgen[:, :, :])
            v.tensor_copy(tgen2[:, :, :], tgen_i)
            v.tensor_sub(tgen2[:, :, :], tgen[:, :, :], tgen2[:, :, :])
            v.tensor_single_scalar(tgen3[:, :, :], tgen2[:, :, :], 0.5, ALU.is_gt)
            v.tensor_sub(tgen2[:, :, :], tgen2[:, :, :], tgen3[:, :, :])

        @block.scalar
        def _(s):
            s.wait_ge(cV.h, cV.n)
            s.activation(costab[:, :, :], tgen2[:, :, :], AF.Sin, scale=TWO_PI)
        tab_s = cS.n

        @block.vector
        def _(v):
            v.wait_ge(cS.h, tab_s)
            with nc.allow_low_precision(reason="bf16 rope tables; q is bf16"):
                v.tensor_copy(costab_bf[:, :, :], costab[:, :, :])
                v.tensor_copy(sintab_bf[:, :, :], sintab[:, :, :])

        xn_sca = [0] * NT
        xn_done_v = [0] * NT
        rec_done = [0] * NT
        xn_st_d = [0] * NT
        kv_copy = [0] * NT
        trp_t = [0] * NT
        lg_mm_t = [0] * NT
        ktr_t = [0] * NT
        neg_done_v = [0] * NT
        lg_sca = [0] * NT
        kt_v = [0] * NT

        krs = probs_em[:, :, :].rearrange("p e a -> p (e a)")[:, 0:64]
        kv_slots = [PA[:, 0:128], PA[:, 512:640]]
        lg_slots = [PB[:, 0:8], PB[:, 512:520]]

        for r in range(NT + 4):
            if r < NT:
                b = r % 4

                @block.sync
                def _(sy, r=r, b=b):
                    if r >= 4:
                        sy.wait_ge(cS.h, xn_sca[r - 4])
                    for q in range(4):
                        dLx[b].inc(sy.dma_start(
                            out=xbuf[b][:, 256 * q:256 * (q + 1)],
                            in_=x_hbm[128 * r:128 * (r + 1), 256 * q:256 * (q + 1)]))

            if 4 <= r < 8:
                k = r - 4
                @block.gpsimd
                def _(g, k=k):
                    if k == 0:
                        dWq.inc(g.dma_start(out=bq_row[:, :], in_=bq_hbm))
                    dWq.inc(g.dma_start(
                        out=wq_sb[:, 2 * k:2 * k + 2, :],
                        in_=wq_hbm.rearrange("(d p) f -> p d f",
                                             p=128)[:, 2 * k:2 * k + 2, :]))
                if r == 7:
                    w_q_done[0] = (dWq, dWq.n)
            if 8 <= r < 12:
                k = r - 8
                @block.gpsimd
                def _(g, k=k):
                    if k == 0:
                        dWo.inc(g.dma_start(out=bo_row[:, :], in_=bo_hbm))
                    dWo.inc(g.dma_start(
                        out=wo_sb[:, 2 * k:2 * k + 2, :],
                        in_=wo_hbm.rearrange("(d p) f -> p d f",
                                             p=128)[:, 2 * k:2 * k + 2, :]))
                if r == 11:
                    w_o_done[0] = (dWo, dWo.n)
            if 12 <= r < 16:
                k = r - 12
                @block.gpsimd
                def _(g, k=k):
                    for d in (2 * k, 2 * k + 1):
                        dW1.inc(g.dma_start(out=w1_sb[:, d, :],
                                            in_=w1_hbm[0, 128 * d:128 * (d + 1), :]))
                if r == 15:
                    w1_arr[(0, 0)] = dW1.n

            sq_plan = cS.n + 1 if r < NT else cS.n

            @block.scalar
            def _(s, r=r):
                if r < NT:
                    b = r % 4
                    bs = r % 4
                    s.wait_ge(dLx[b].h, 64 * (r // 4 + 1))
                    if r >= 4:
                        s.wait_ge(cV.h, rec_done[r - 4])
                    s.activation(sq_scr, xbuf[b], AF.Square,
                                 accum_out=st_ssq[:, bs:bs + 1])
                if 1 <= r <= NT:
                    t = r - 1
                    bb = t % 2
                    tb = t % 4
                    if t >= 2:
                        s.wait_ge(dXn[bb].h, xn_st_d[t - 2])
                        s.wait_ge(cT.h, trp_t[t - 2])
                    s.wait_ge(cV.h, rec_done[t])
                    s.activation(st_ln[:, tb:tb + 1], st_r[:, tb:tb + 1], AF.Ln)
                    s.activation(st_rs[:, tb:tb + 1], st_ln[:, tb:tb + 1], AF.Exp,
                                 scale=-0.5)
                    s.activation(st_nmr[:, tb:tb + 1], st_nm[:, tb:tb + 1], AF.Identity,
                                 scale=st_rs[:, tb:tb + 1])
                    s.activation(xnstage[bb], xbuf[t % 4], AF.Identity,
                                 bias=st_nmr[:, tb:tb + 1], scale=st_rs[:, tb:tb + 1])
                    xn_sca[t] = cS.n
                if 4 <= r < NT + 4:
                    te = r - 4
                    lgps_e = lg_slots[te % 2]
                    s.wait_ge(cV.h, neg_done_v[te])
                    s.activation(probs[:, te, :], lgps_e, AF.Exp,
                                 bias=lmax[:, te:te + 1], accum_out=zrow[:, te:te + 1])
                    lg_sca[te] = cS.n

            if r < NT:
                b = r % 4
                bs = r % 4

                @block.vector
                def _(v, r=r, b=b, bs=bs, sq_done=sq_plan):
                    v.wait_ge(dLx[b].h, 64 * (r // 4 + 1))
                    if r >= 4:
                        v.wait_ge(cS.h, xn_sca[r - 4])
                    v.reduce_sum(st_m[:, bs:bs + 1], xbuf[b], axis=AX.X)
                    v.tensor_scalar_mul(st_m[:, bs:bs + 1], st_m[:, bs:bs + 1], 1.0 / DM)
                    v.tensor_scalar_mul(st_nm[:, bs:bs + 1], st_m[:, bs:bs + 1], -1.0)
                    v.wait_ge(cS.h, sq_done)
                    v.tensor_mul(st_r[:, bs:bs + 1], st_m[:, bs:bs + 1], st_m[:, bs:bs + 1])
                    v.scalar_tensor_tensor(st_r[:, bs:bs + 1], st_ssq[:, bs:bs + 1], 1.0 / DM,
                                           st_r[:, bs:bs + 1], op0=ALU.mult, op1=ALU.subtract)
                    v.tensor_scalar_add(st_r[:, bs:bs + 1], st_r[:, bs:bs + 1], LN_EPS)
                    rec_done[r] = cV.n

            if 1 <= r <= NT:
                t = r - 1
                bb = t % 2

                @block.gpsimd
                def _(g, t=t, bb=bb):
                    g.wait_ge(cS.h, xn_sca[t])
                    dXn[bb].inc(g.dma_start(out=xn_dram.ap()[128 * t:128 * (t + 1), :],
                                            in_=xnstage[bb]))
                xn_st_d[t] = dXn[bb].n

                @block.tensor
                def _(tn, t=t, bb=bb):
                    tn.wait_ge(cS.h, xn_sca[t])
                    if t >= 2:
                        tn.wait_ge(cV.h, xn_done_v[t - 2])
                    pc = PC.bitcast(BF16)[:, 1024 * bb:1024 * (bb + 1)]
                    for c in range(ND):
                        ins = tn.transpose(pc[:, 128 * c:128 * (c + 1)],
                                           xnstage[bb][:, 128 * c:128 * (c + 1)],
                                           idbf[:, :])
                    cT.inc(ins)
                trp_t[t] = cT.n

            if 2 <= r <= NT + 1:
                t = r - 2
                bb = t % 2

                @block.vector
                def _(v, t=t, bb=bb):
                    v.wait_ge(cT.h, trp_t[t])
                    pc = PC.bitcast(BF16)[:, 1024 * bb:1024 * (bb + 1)]
                    v.tensor_copy(
                        xnT[:, :, 128 * t:128 * (t + 1)],
                        pc.rearrange("p (c t) -> p c t", c=ND))
                    xn_done_v[t] = cV.n

                kvps = kv_slots[t % 2]
                lgps = lg_slots[t % 2]

                @block.tensor
                def _(tn, t=t, kvps=kvps, lgps=lgps):
                    tn.wait_ge(dWa.h, w_kv_done[0])
                    tn.wait_ge(cV.h, xn_done_v[t])
                    if t >= 2:
                        tn.wait_ge(cV.h, kv_copy[t - 2])
                        tn.wait_ge(cS.h, lg_sca[t - 2])
                    ts = slice(128 * t, 128 * (t + 1))
                    tn.matmul(kvps, ones_1x128_bf[:, :], bkv_row[:, :],
                              start=True, stop=False)
                    for d in range(ND):
                        ins = tn.matmul(kvps, xnT[:, d, ts], wkv[:, d, :],
                                        start=False, stop=(d == ND - 1))
                    cT.inc(ins)
                    for d in range(ND):
                        ins = tn.matmul(lgps, xnT[:, d, ts], wg_sb[:, d, :],
                                        start=(d == 0), stop=(d == ND - 1))
                    cT.inc(ins)
                lg_mm_t[t] = cT.n

            if 3 <= r <= NT + 2:
                t = r - 3
                kvps = kv_slots[t % 2]
                lgps = lg_slots[t % 2]

                @block.vector
                def _(v, t=t, kvps=kvps, lgps=lgps):
                    v.wait_ge(cT.h, lg_mm_t[t])
                    v.tensor_copy(kvtm[:, t, :], kvps)
                    v.reduce_max(lmax[:, t:t + 1], lgps, axis=AX.X, negate=True)
                neg_done_v[t] = cV.n
                kv_copy[t] = cV.n

                @block.gpsimd
                def _(g, t=t):
                    g.wait_ge(cV.h, kv_copy[t])
                    k2 = kvtm[:, t, 0:64]
                    cA, cB = costab[:, t, 0:16], costab[:, t, 16:32]
                    sA, sB = sintab[:, t, 0:16], sintab[:, t, 16:32]
                    g.tensor_tensor(krs[:, 32:48], k2[:, 16:32], sA, ALU.mult)
                    g.tensor_tensor(krs[:, 0:16], k2[:, 0:16], cA, ALU.mult)
                    g.tensor_sub(krs[:, 0:16], krs[:, 0:16], krs[:, 32:48])
                    g.tensor_tensor(krs[:, 48:64], k2[:, 0:16], sB, ALU.mult)
                    g.tensor_tensor(krs[:, 16:32], k2[:, 16:32], cB, ALU.mult)
                    g.tensor_add(krs[:, 16:32], krs[:, 16:32], krs[:, 48:64])
                    g.tensor_copy(k2[:, 0:32], krs[:, 0:32])
                krope_p = cP.n

                @block.tensor
                def _(tn, t=t, krope_p=krope_p):
                    tn.wait_ge(cP.h, krope_p)
                    if t >= 2:
                        tn.wait_ge(cV.h, kt_v[t - 2])
                    ktp = PD[0:64, 0:128] if t % 2 == 0 else PD[0:64, 512:640]
                    cT.inc(tn.transpose(ktp, kvtm[:, t, 0:64], idf32[:, :]))
                ktr_t[t] = cT.n

            if 4 <= r <= NT + 3:
                t = r - 4

                @block.vector
                def _(v, t=t):
                    v.wait_ge(cT.h, ktr_t[t])
                    ktp = PD[0:64, 0:128] if t % 2 == 0 else PD[0:64, 512:640]
                    v.tensor_copy(kT[0:64, 128 * t:128 * (t + 1)], ktp)
                    kt_v[t] = cV.n

        ln_last_v = cV.n

        @block.sync
        def _(sy):
            for bq4 in range(4):
                sy.wait_ge(dLx[bq4].h, dLx[bq4].n)
            sy.wait_ge(dWq.h, dWq.n)
            dStash.inc(sy.dma_start(out=wq_dram.ap(), in_=regR[:, 0:8192]))
            sy.wait_ge(dWo.h, dWo.n)
            dStash.inc(sy.dma_start(out=wo_dram.ap(), in_=regR[:, 8192:16384]))
            for k in range(E * 512 // 128):
                dPre.inc(sy.dma_start(
                    out=xg_dram.ap()[128 * k:128 * (k + 1), :],
                    in_=regR[:, 0:1024]))

        if "xnT" in DBG:
            @block.sync
            def _(sy):
                sy.wait_ge(cV.h, ln_last_v)
                dDbg.inc(sy.dma_start(out=dbg_hbm["xnT"], in_=xnT[:, :, :]))

        def emit_rope(v, view3, nh, t):
            cosA = costab_bf[:, t:t + 1, 0:16].broadcast_to([128, nh, 16])
            cosB = costab_bf[:, t:t + 1, 16:32].broadcast_to([128, nh, 16])
            sinA = sintab_bf[:, t:t + 1, 0:16].broadcast_to([128, nh, 16])
            sinB = sintab_bf[:, t:t + 1, 16:32].broadcast_to([128, nh, 16])
            scr = rope_scr[:, 0:nh, :]
            v.tensor_tensor(scr[:, :, 32:48], view3[:, :, 16:32], sinA, ALU.mult)
            v.tensor_tensor(scr[:, :, 0:16], view3[:, :, 0:16], cosA, ALU.mult)
            v.tensor_sub(scr[:, :, 0:16], scr[:, :, 0:16], scr[:, :, 32:48])
            v.tensor_tensor(scr[:, :, 48:64], view3[:, :, 0:16], sinB, ALU.mult)
            v.tensor_tensor(scr[:, :, 16:32], view3[:, :, 16:32], cosB, ALU.mult)
            v.tensor_add(scr[:, :, 16:32], scr[:, :, 16:32], scr[:, :, 48:64])
            v.tensor_copy(view3[:, :, 0:32], scr[:, :, 0:32])

        @block.vector
        def _(v):
            v.wait_ge(cS.h, lg_sca[NT - 1])
            v.reciprocal(zrow[:, :], zrow[:, :])
            zB = zrow[:, :].rearrange("p (a e) -> p a e", e=1).broadcast_to([128, NT, E])
            v.tensor_tensor(probs[:, :, :], probs[:, :, :], zB, ALU.mult)
            v.tensor_copy(vstat[:, :, 0:64], kvtm[:, :, 64:128])

        @block.vector
        def _(v):
            v.wait_ge(cV.h, kt_v[NT - 1])
            v.tensor_copy(kT[64:128, :], kT[0:64, :])
            v.tensor_copy(probs_em[:, :, :].rearrange("p e a -> p a e"),
                          probs[:, :, :])

        kv_last_v = cV.n
        probs_last_s = cS.n

        @block.gpsimd
        def _(g):
            g.wait_ge(cV.h, kv_last_v)
            g.memset(stricL, 1.0)
            g.wait_ge(cP.h, cP.n)
            g.affine_select(out=stricL, in_=stricL, compare_op=ALU.is_gt,
                            fill=0.0, base=0, pattern=[[1, 128]], channel_multiplier=-1)
            g.iota(eidx8, [[1, E]], channel_multiplier=0,
                   allow_small_or_imprecise_dtypes=True)
        sidx_p = cP.n

        @block.sync
        def _(sy):
            sy.wait_ge(cV.h, kv_last_v)
            dPr.inc(sy.dma_start(out=ag_in.ap(), in_=probs_em[:, :, :]))

        @block.gpsimd
        def _(g):
            g.wait_ge(dPr.h, 16)
            ccc.inc(g.collective_compute(
                "AllGather", ALU.bypass,
                replica_groups=[list(range(NCORES))],
                ins=[ag_in.ap()], outs=[ag_out.ap()]))

        @block.sync
        def _(sy):
            sy.wait_ge(ccc.h, 1)
            for e in range(E):
                dPr.inc(sy.dma_start(
                    out=probs_all[:, e, :].rearrange("p (r a) -> p r a", r=NCORES),
                    in_=ag_out.ap()[:, :, e:e + 1, :]
                    .rearrange("r p e a -> p r (e a)")))
        pall_done = dPr.n

        for name, gate in (("kT", kv_last_v), ("vstat", kv_last_v),
                           ("kvtm", kv_last_v), ("probs", kv_last_v)):
            if name in DBG:
                src = {"kT": kT[:, :], "vstat": vstat[:, :, :],
                       "kvtm": kvtm[:, :, :], "probs": probs[:, :, :]}[name]

                @block.sync
                def _(sy, name=name, src=src, gate=gate):
                    sy.wait_ge(cV.h, gate)
                    dDbg.inc(sy.dma_start(out=dbg_hbm[name], in_=src))
        if "probs_all" in DBG:
            @block.sync
            def _(sy):
                sy.wait_ge(dPr.h, pall_done)
                dDbg.inc(sy.dma_start(out=dbg_hbm["probs_all"], in_=probs_all[:, :, :]))

        cnt_ps = PD[:, 512:520]

        class Bisect:
            def __init__(self):
                self.started = 0
                self.done = 0
                self.stage = "idle"
                self.red_cv = 0
                self.mm_ct = 0
                self.first = True
                self.enabled = False

            def pre(self, v):
                if not self.enabled:
                    return
                if self.stage != "idle" or self.started >= BIS_ITERS:
                    return
                if self.first:
                    v.wait_ge(dPr.h, pall_done)
                    self.first = False
                tauB = tau[:, :].rearrange("p (e a) -> p e a", a=1) \
                    .broadcast_to([128, E, 128])
                with nc.allow_low_precision(reason="counts <=128 exact in bf16"):
                    v.tensor_tensor(cmpb[:, :, :], probs_all[:, :, :], tauB,
                                    ALU.is_ge)
                    v.reduce_sum(cntpp[:, :], cmpb[:, :, :], axis=AX.X)
                self.red_cv = cV.n
                self.started += 1
                self.stage = "cmp"

            def mm(self, tn):
                if not self.enabled or self.stage != "cmp":
                    return
                tn.wait_ge(cV.h, self.red_cv)
                cT.inc(tn.matmul(cnt_ps, ones_128_bf[:, :], cntpp[:, :],
                                 start=True, stop=True))
                self.mm_ct = cT.n
                self.stage = "mm"

            def upd(self, v):
                if not self.enabled or self.stage != "mm":
                    return
                v.wait_ge(cT.h, self.mm_ct)
                v.tensor_single_scalar(geb[:, :], cnt_ps, CAP, ALU.is_ge)
                v.tensor_single_scalar(geb_inv[:, :], cnt_ps, CAP, ALU.is_lt)
                v.copy_predicated(lo[:, :], geb[:, :], tau[:, :])
                v.copy_predicated(hi[:, :], geb_inv[:, :], tau[:, :])
                v.tensor_add(tau[:, :], lo[:, :], hi[:, :])
                v.tensor_scalar_mul(tau[:, :], tau[:, :], 0.5)
                self.done += 1
                self.stage = "idle"

        bis = Bisect()
        gates_done_v = [0]

        def emit_gates(v):
            loB = lo[:, :].rearrange("p (a e) -> p a e", a=1).broadcast_to([128, NT, E])
            v.tensor_tensor(gsel[:, :, :], probs[:, :, :], loB, ALU.is_ge)
            v.scalar_tensor_tensor(gsel[:, :, :], gsel[:, :, :], 2.0, probs[:, :, :],
                                   op0=ALU.mult, op1=ALU.add)
            v.reduce_max(gmax[:, :], gsel[:, :, :], axis=AX.X)
            gmB = gmax[:, :].rearrange("p (a e) -> p a e", e=1).broadcast_to([128, NT, E])
            v.tensor_tensor(gsel[:, :, :], gsel[:, :, :], gmB, ALU.is_equal)
            v.tensor_mul(gates[:, :, :], gsel[:, :, :], probs[:, :, :])
            gates_done_v[0] = cV.n

        qev_s = {}
        qtr_t = {}
        last_exp_pair = [0]
        qevq_v = {}
        oev_v = {}
        last_S_mm = [0]
        attn_tt_v = {}
        zi_evac_v = {}
        zi_mm_last = [0]
        out_store_d = {}
        w2_arr = {}
        w1_last_ct = {}
        w2_last_ct = {}
        stt_v = {}

        wq_view = [None]

        def emit_qproj(w):
            wqv = wq_view[0] if wq_view[0] is not None else wq_sb
            for t in range(4 * w, 4 * w + 4):
                b = t % 2
                qps = PA if b == 0 else PB
                tps = PC if b == 0 else PD

                @block.tensor
                def _(tn, t=t, qps=qps, w=w):
                    tn.wait_ge(w_q_done[0][0].h, w_q_done[0][1])
                    if w == 0 and t < 2:
                        tn.wait_ge(cS.h, probs_last_s)
                    prev = qev_s.get(t - 2)
                    if prev is not None:
                        tn.wait_ge(cS.h, prev)
                    prevo = oev_v.get(t - 2)
                    if prevo is not None:
                        tn.wait_ge(cV.h, prevo)
                    ts = slice(128 * t, 128 * (t + 1))
                    tn.matmul(qps[:, 0:512], ones_1x128_bf[:, :], bq_row[:, 0:512],
                              start=True, stop=False)
                    tn.matmul(qps[:, 512:1024], ones_1x128_bf[:, :], bq_row[:, 512:1024],
                              start=True, stop=False)
                    for d in range(ND):
                        i0 = tn.matmul(qps[:, 0:512], xnT[:, d, ts], wqv[:, d, 0:512],
                                       start=False, stop=(d == ND - 1))
                        i1 = tn.matmul(qps[:, 512:1024], xnT[:, d, ts],
                                       wqv[:, d, 512:1024],
                                       start=False, stop=(d == ND - 1))
                    cT.inc(i0)
                    cT.inc(i1)
                qmm = cT.n

                @block.scalar
                def _(s, t=t, b=b, qps=qps, qmm=qmm, w=w):
                    s.wait_ge(cT.h, qmm)
                    prev = qtr_t.get(t - 2)
                    if prev is not None:
                        s.wait_ge(cT.h, prev)
                    if w == 0 and t < 2:
                        s.wait_ge(cV.h, ln_last_v)
                    s.activation(q_tm[b], qps[:, :], AF.Identity)
                    qev_s[t] = cS.n

                @block.vector
                def _(v, t=t, b=b, w=w):
                    v.wait_ge(cS.h, qev_s[t])
                    emit_rope(v, q_tm3[b], 16, t)
                    v.engine_nop()
                qrope = cV.n

                @block.tensor
                def _(tn, t=t, b=b, tps=tps, qrope=qrope, w=w):
                    tn.wait_ge(cV.h, qrope)
                    prev = qevq_v.get(t - 2)
                    if prev is not None:
                        tn.wait_ge(cV.h, prev)
                    if w == 0 and t < 2:
                        tn.wait_ge(cV.h, kt_v[NT - 1])
                    tpsb = tps[:, :].bitcast(BF16)[:, 0:1024]
                    for c in range(ND):
                        ins = tn.transpose(tpsb[:, 128 * c:128 * (c + 1)],
                                           q_tm[b][:, 128 * c:128 * (c + 1)],
                                           idbf[:, :])
                    cT.inc(ins)
                    qtr_t[t] = cT.n

                @block.vector
                def _(v, t=t, b=b, tps=tps, w=w):
                    v.wait_ge(cT.h, qtr_t[t])
                    if last_S_mm[0]:
                        v.wait_ge(cT.h, last_S_mm[0])
                    toff = 128 * (t - 4 * w)
                    tpsb = tps[:, :].bitcast(BF16)[:, 0:1024]
                    v.tensor_copy(
                        qTq[:, :].rearrange("p (c u) -> p c u", c=ND)[:, :, toff:toff + 128],
                        tpsb.rearrange("p (c u) -> p c u", c=ND))
                    qevq_v[t] = cV.n

        def emit_heads(w, pair_hook=None):
            nj = 4 * w + 4
            pend_mm = []
            pend_fin = []
            rz_free_ct = [0]

            def c0_of(j):
                k = j - 4 * w
                return 128 * k if k > 0 else 0

            def inject_znorm(tn):
                if pend_mm:
                    pw, ph, hh, rzc = pend_mm.pop(0)
                    tn.wait_ge(cS.h, rzc)
                    prev_zi = zi_evac_v.get(("z", pw, ph - 1)) or zi_evac_v.get("last")
                    if prev_zi:
                        tn.wait_ge(cV.h, prev_zi)
                    cT.inc(tn.matmul(PD[0:64, 0:512], ones_1x128_bf[:, 0:64],
                                     rz_pair[:, 512 * hh:512 * (hh + 1)],
                                     start=True, stop=True))
                    pend_fin.append((pw, ph, hh, cT.n))
                    rz_free_ct[0] = cT.n

            def finish_znorm(v):
                if pend_fin:
                    pw, ph, hh, zic = pend_fin.pop(0)
                    v.wait_ge(cT.h, zic)
                    v.tensor_copy(zi_sb[:, :], PD[0:64, 0:512])
                    po = PC[0:65, 512 * hh:512 * (hh + 1)]
                    v.tensor_tensor(
                        attnTq[(ph % 2) * 64:(ph % 2) * 64 + 64,
                               512 * (ph // 2):512 * (ph // 2) + 512],
                        po[0:64, :], zi_sb[:, :], ALU.mult)
                    attn_tt_v[(pw, ph)] = cV.n
                    zi_evac_v["last"] = cV.n

            jlist = list(range(nj - 1, -1, -1))

            prev_pvl = [0]
            for hp in range(H // 2):
                h0, h1 = 2 * hp, 2 * hp + 1
                ops = [PC[0:65, 0:512], PC[0:65, 512:1024]]
                spair = [(PA[:, 0:512], PA[:, 512:1024]),
                         (PB[:, 0:512], PB[:, 512:1024])]
                exp_cs = {}
                mask_cv = {}
                pv_ct = {}

                @block.vector
                def _(v, hp=hp, w=w):
                    if w == 0 and hp == 0:
                        for bb in range(2):
                            v.wait_ge(dXn[bb].h, dXn[bb].n)
                    if prev_pvl[0]:
                        v.wait_ge(cT.h, prev_pvl[0])
                    for si, cz in ((0, 384), (1, 256), (2, 128)):
                        v.memset(pbuf[si][:, :].rearrange(
                            "p (h c) -> p h c", h=2)[:, :, 0:cz], 0.0)
                    finish_znorm(v)
                    bis.upd(v)
                    bis.pre(v)
                pb_zero = cV.n

                planned = {}
                pn = cS.n
                for i in range(nj):
                    j = jlist[i]
                    pn += 1 if c0_of(j) == 0 else 2
                    planned[j] = pn
                rzc_planned = pn + 4

                for i0 in range(0, nj, 2):
                    @block.tensor
                    def _(tn, i0=i0, hp=hp, h0=h0, h1=h1, w=w):
                        tn.wait_ge(cV.h, qevq_v[4 * w + 3])
                        if hp == 0 and i0 == 0:
                            tn.wait_ge(cS.h, qev_s[4 * w + 3])
                        if i0 >= 4:
                            for i in (i0 - 4, i0 - 3):
                                jp = jlist[i]
                                wg_ = mask_cv.get(jp)
                                if wg_ is not None:
                                    tn.wait_ge(cV.h, wg_)
                                else:
                                    tn.wait_ge(cS.h, planned[jp])
                                if i == 0 and hp > 0:
                                    pa = attn_tt_v.get((w, 2 * hp - 1))
                                    if pa is not None:
                                        tn.wait_ge(cV.h, pa)
                                for hh in (0, 1):
                                    cT.inc(tn.matmul(
                                        ops[hh],
                                        vstat[:, jp, :],
                                        pbuf[i % 4][:, 512 * hh:512 * (hh + 1)],
                                        start=(jp == jlist[0]), stop=(jp == 0)))
                                pv_ct[jp] = cT.n
                        for i in (i0, i0 + 1):
                            j = jlist[i]
                            if i >= 2:
                                tn.wait_ge(cS.h, planned[jlist[i - 2]])
                            elif hp > 0:
                                tn.wait_ge(cS.h, last_exp_pair[0])
                            c0 = c0_of(j)
                            for hh, h in ((0, h0), (1, h1)):
                                kbase = (h % 2) * 64
                                mv = qTq[(h % 2) * 64:(h % 2) * 64 + 64,
                                         512 * (h // 2):512 * (h // 2) + 512]
                                cT.inc(tn.matmul(
                                    spair[j % 2][hh][:, c0:512] if c0 else spair[j % 2][hh],
                                    kT[kbase:kbase + 64, 128 * j:128 * (j + 1)],
                                    mv[:, c0:512] if c0 else mv,
                                    start=True, stop=True))
                            exp_cs[("smm", j)] = cT.n
                        inject_znorm(tn)
                        bis.mm(tn)

                    @block.vector
                    def _(v, i0=i0, hp=hp, w=w):
                        finish_znorm(v)
                        bis.upd(v)
                        bis.pre(v)
                        for i in (i0, i0 + 1):
                            j = jlist[i]
                            if j - 4 * w >= 0:
                                c0 = c0_of(j)
                                v.wait_ge(cS.h, planned[j])
                                pv2 = pbuf[i % 4][:, :].rearrange(
                                    "p (h c) -> p h c", h=2)[:, :, c0:c0 + 128]
                                tm = trimask[:, :].rearrange(
                                    "p (a c) -> p a c", a=1).broadcast_to([128, 2, 128])
                                v.tensor_tensor(pv2, pv2, tm, ALU.mult)
                                mask_cv[j] = cV.n

                for t0 in (nj, nj + 2):
                    @block.tensor
                    def _(tn, t0=t0, hp=hp, w=w):
                        for i in (t0 - 4, t0 - 3):
                            jp = jlist[i]
                            wg_ = mask_cv.get(jp)
                            if wg_ is not None:
                                tn.wait_ge(cV.h, wg_)
                            else:
                                tn.wait_ge(cS.h, planned[jp])
                            if i == 0 and hp > 0:
                                pa = attn_tt_v.get((w, 2 * hp - 1))
                                if pa is not None:
                                    tn.wait_ge(cV.h, pa)
                            for hh in (0, 1):
                                cT.inc(tn.matmul(
                                    ops[hh],
                                    vstat[:, jp, :],
                                    pbuf[i % 4][:, 512 * hh:512 * (hh + 1)],
                                    start=(jp == jlist[0]), stop=(jp == 0)))
                            pv_ct[jp] = cT.n
                pvl = cT.n
                prev_pvl[0] = cT.n
                last_exp_pair[0] = planned[0]

                @block.scalar
                def _(s, hp=hp, w=w, pb_zero=pb_zero, pvl=pvl, nj=nj):
                    for i in range(nj):
                        j = jlist[i]
                        c0 = c0_of(j)
                        if i < 2:
                            s.wait_ge(cV.h, pb_zero)
                        s.wait_ge(cT.h, exp_cs[("smm", j)])
                        if i >= 4:
                            pc = pv_ct.get(jlist[i - 4])
                            if pc is not None:
                                s.wait_ge(cT.h, pc)
                        if c0 == 0:
                            s.activation(pbuf[i % 4][:, :],
                                         PA[:, :] if j % 2 == 0 else PB[:, :],
                                         AF.Exp, scale=0.125, bias=negc[:, :])
                        else:
                            for hh in (0, 1):
                                s.activation(pbuf[i % 4][:, 512 * hh + c0:512 * (hh + 1)],
                                             spair[j % 2][hh][:, c0:512],
                                             AF.Exp, scale=0.125, bias=negc[:, :])
                    s.wait_ge(cT.h, pvl)
                    if rz_free_ct[0]:
                        s.wait_ge(cT.h, rz_free_ct[0])
                    for hh in (0, 1):
                        s.activation(zln_sb[:, :], PC[64:65, 512 * hh:512 * (hh + 1)],
                                     AF.Ln)
                        with nc.allow_low_precision(reason="1/Z bf16; attnT is bf16"):
                            s.activation(rz_pair[:, 512 * hh:512 * (hh + 1)],
                                         zln_sb[:, :], AF.Exp, scale=-1.0)
                assert cS.n == rzc_planned, (cS.n, rzc_planned)
                rzc = cS.n
                pend_mm.append((w, h0, 0, rzc))
                pend_mm.append((w, h1, 1, rzc))
                if pair_hook:
                    pair_hook()

            for _ in range(2):
                @block.tensor
                def _(tn):
                    inject_znorm(tn)

                @block.vector
                def _(v):
                    finish_znorm(v)
            last_S_mm[0] = cT.n

        def emit_oproj(w):
            half = w // 2
            att_done = cV.n
            for t in range(4 * w, 4 * w + 4):
                b = t % 2
                ops = PA if b == 0 else PB
                th = t - 8 * half

                @block.tensor
                def _(tn, t=t, ops=ops, w=w, att_done=att_done):
                    tn.wait_ge(w_o_done[0][0].h, w_o_done[0][1])
                    tn.wait_ge(cV.h, att_done)
                    prev = oev_v.get(t - 2)
                    if prev is not None:
                        tn.wait_ge(cV.h, prev)
                    toff = 128 * (t - 4 * w)
                    tn.matmul(ops[:, 0:512], ones_1x128_bf[:, :], bo_row[:, 0:512],
                              start=True, stop=False)
                    tn.matmul(ops[:, 512:1024], ones_1x128_bf[:, :], bo_row[:, 512:1024],
                              start=True, stop=False)
                    for c in range(ND):
                        stat = attnTq[:, 512 * c + toff:512 * c + toff + 128]
                        i0 = tn.matmul(ops[:, 0:512], stat, wo_sb[:, c, 0:512],
                                       start=False, stop=(c == ND - 1))
                        i1 = tn.matmul(ops[:, 512:1024], stat, wo_sb[:, c, 512:1024],
                                       start=False, stop=(c == ND - 1))
                    cT.inc(i0)
                    cT.inc(i1)
                omm = cT.n

                @block.vector
                def _(v, t=t, ops=ops, omm=omm, th=th, half=half):
                    v.wait_ge(cT.h, omm)
                    if half == 1:
                        prev_store = out_store_d.get((0, th))
                        if prev_store is not None:
                            v.wait_ge(dOutT[th].h, prev_store)
                    v.tensor_copy(acc[:, th, :], ops[:, :])
                    oev_v[t] = cV.n

        sp = {}

        def emit_indices_and_scatter():
            @block.vector
            def _(v):
                v.tensor_copy(flagsF.rearrange("p (e a) -> p e a", e=E),
                              gsel[:, :, :].rearrange("p a e -> p e a"))
                with nc.allow_low_precision(reason="0/1 flags exact in bf16"):
                    v.tensor_copy(flagsE, flagsF)
                v.tensor_scalar_mul(e512[:, :], eidx8, 512.0)
            fl_v = cV.n

            @block.tensor
            def _(tn):
                tn.wait_ge(cV.h, fl_v)
                tn.wait_ge(cP.h, sidx_p)
                tn.wait_ge(cS.h, last_exp_pair[0])
                fE = flagsE.rearrange("p (e a) -> p e a", e=E)
                for e in range(E):
                    cT.inc(tn.matmul(PA[:, 16 * e:16 * (e + 1)], stricL,
                                     fE[:, e, :], start=True, stop=True))
                cT.inc(tn.matmul(PB[:, 0:128], ones_128_bf[:, :], flagsE,
                                 start=True, stop=True))
            idx_mm = cT.n

            @block.vector
            def _(v):
                v.wait_ge(cT.h, idx_mm)
                v.tensor_copy(slotsel, PB[:, 0:128])
                cp3 = colpfx_sb.rearrange("p (e a) -> p e a", e=E)
                cs3 = slotsel.rearrange("p (e a) -> p e a", e=E)
                for e in range(E):
                    v.tensor_tensor_scan(cp3[:, e, :], cs3[:, e, :], cs3[:, e, :],
                                         0.0, op0=ALU.add, op1=ALU.bypass)
                v.tensor_sub(colpfx_sb, colpfx_sb, slotsel)
                v.tensor_add(slotsel, PA[:, 0:128], colpfx_sb)
                v.tensor_mul(slotsel, slotsel, flagsF)
                sv = slotsel.rearrange("p (e a) -> p a e", e=E)
                v.reduce_sum(slot_f, sv, axis=AX.X)
                v.tensor_scalar_min(slot_f, slot_f, 511.0)
                e5B = e512[:, :].rearrange("p (e a) -> p e a", a=1).broadcast_to([128, E, 16])
                v.tensor_tensor(slotsel.rearrange("p (e a) -> p e a", e=E),
                                flagsF.rearrange("p (e a) -> p e a", e=E),
                                e5B, ALU.mult)
                v.reduce_sum(e_f, sv, axis=AX.X)
                v.tensor_add(inv_f, slot_f, e_f)
                v.tensor_copy(inv_i32, inv_f)
                v.reduce_max(g_tok, gates[:, :, :], axis=AX.X)
            sp["idx_v"] = cV.n

            for t in range(NT):
                b = t % 2

                @block.sync
                def _(sy, t=t, b=b):
                    if t < 2:
                        for bb in range(2):
                            sy.wait_ge(dXn[bb].h, dXn[bb].n)
                    if t >= 2:
                        sy.wait_ge(dSc[b].h, 16 * (t // 2))
                    dXs[b].inc(sy.dma_start(out=xsc_sb[b],
                                            in_=xn_dram.ap()[128 * t:128 * (t + 1), :]))

                @block.gpsimd
                def _(g, t=t, b=b):
                    g.wait_ge(dXs[b].h, dXs[b].n)
                    if t == 0:
                        g.wait_ge(cV.h, sp["idx_v"])
                        g.wait_ge(dPre.h, dPre.n)
                    dSc[b].inc(g.indirect_dma_start(
                        out=xg_dram.ap(),
                        out_offset=IndirectOffsetOnAxis(inv_i32[:, t:t + 1], 0),
                        in_=xsc_sb[b],
                        in_offset=None))
            sp["sc0"] = dSc[0].n
            sp["sc1"] = dSc[1].n

        def emit_sparse_ffn():
            tr_v = {}
            last_fg = {}
            gelu_s = {}
            w2_v = {}
            st_d = {}
            hps4 = [PA[:, 0:512], PA[:, 512:1024], PB[:, 0:512], PB[:, 512:1024]]
            xgT3 = xgT.rearrange("p (d u) -> p d u", d=ND)
            def emit_xg_load(e):
                @block.sync
                def _(sy, e=e):
                    if e == 0:
                        sy.wait_ge(dSc[0].h, sp["sc0"])
                        sy.wait_ge(dSc[1].h, sp["sc1"])
                    for a in range(4):
                        if e > 0:
                            sy.wait_ge(cT.h, tr_v[("mm", e - 1, a)])
                        dXgA[a].inc(sy.dma_start(
                            out=xg_sb[:, 1024 * a:1024 * (a + 1)],
                            in_=xg_dram.ap()[512 * e + 128 * a:
                                             512 * e + 128 * (a + 1), :]))
                        sp[("xg", e, a)] = dXgA[a].n

            emit_xg_load(0)
            for e in range(E):
                for a in range(4):
                    @block.tensor
                    def _(tn, e=e, a=a):
                        tn.wait_ge(dXgA[a].h, sp[("xg", e, a)])
                        if a == 0 and e == 0:
                            tn.wait_ge(cV.h, sp["h1_znorm_cv"])
                        pcb = ((PC, PD)[a % 2])[:, :].bitcast(BF16)[:, 0:1024]
                        prev = tr_v.get((e, a - 2))
                        if prev is not None:
                            tn.wait_ge(cV.h, prev)
                        if a < 2 and e > 0:
                            prev = w2_v.get((e - 1, 2 + a))
                            if prev is not None:
                                tn.wait_ge(cV.h, prev)
                        for c in range(ND):
                            ins = tn.transpose(pcb[:, 128 * c:128 * (c + 1)],
                                               xg_sb[:, 1024 * a + 128 * c:
                                                     1024 * a + 128 * (c + 1)],
                                               idbf[:, :])
                        cT.inc(ins)
                        tr_v[("mm", e, a)] = cT.n

                    @block.vector
                    def _(v, e=e, a=a):
                        v.wait_ge(cT.h, tr_v[("mm", e, a)])
                        pcb = ((PC, PD)[a % 2])[:, :].bitcast(BF16)[:, 0:1024]
                        v.tensor_copy(xgT3[:, :, 128 * a:128 * (a + 1)],
                                      pcb.rearrange("p (c u) -> p c u", c=ND))
                        tr_v[(e, a)] = cV.n
                sp[("tr", e)] = cT.n
                tr_all = cV.n

                g_base = cS.n
                for f in range(NF):
                    gelu_s[(e, f)] = g_base + f + 1
                for f4 in range(0, NF, 4):
                    @block.tensor
                    def _(tn, e=e, f4=f4, tr_all=tr_all):
                        tn.wait_ge(dW1.h, w1_arr[(0, e)])
                        tn.wait_ge(cV.h, tr_all)
                        for f in range(f4, f4 + 4):
                            if f - 4 >= 0:
                                tn.wait_ge(cS.h, gelu_s[(e, f - 4)])
                            for d in range(ND):
                                ins = tn.matmul(hps4[f % 4],
                                                w1_sb[:, d, 128 * f:128 * (f + 1)],
                                                xgT3[:, d, :],
                                                start=(d == 0), stop=(d == ND - 1))
                            cT.inc(ins)
                            gelu_s[("mm", e, f)] = cT.n
                sp[("w1", e)] = cT.n
                if e + 1 < E:
                    emit_xg_load(e + 1)

                @block.scalar
                def _(s, e=e):
                    for f in range(NF):
                        s.wait_ge(cT.h, gelu_s[("mm", e, f)])
                        s.activation(hG[:, 512 * f:512 * (f + 1)],
                                     hps4[f % 4], AF.Gelu)
                assert cS.n == gelu_s[(e, NF - 1)], (cS.n, e)

                nxt = e + 1
                if nxt < E:
                    @block.gpsimd
                    def _(g, e=e, nxt=nxt):
                        g.wait_ge(cT.h, sp[("w1", e)])
                        for d in range(ND):
                            dW1.inc(g.dma_start(out=w1_sb[:, d, :],
                                                in_=w1_hbm[nxt, 128 * d:128 * (d + 1), :]))
                    w1_arr[(0, nxt)] = dW1.n

                glast = gelu_s[(e, NF - 1)]
                for a in range(4):
                    @block.tensor
                    def _(tn, e=e, a=a, glast=glast):
                        tn.wait_ge(cS.h, glast)
                        tn.wait_ge(dW2.h, w2_arr[(0, e)])
                        if e == 0 and a < 2:
                            tn.wait_ge(cV.h, sp["routing_cv"])
                        wps = PC if a % 2 == 0 else PD
                        prev = w2_v.get((e, a - 2))
                        if prev is not None:
                            tn.wait_ge(cV.h, prev)
                        for f in range(NF):
                            i0 = tn.matmul(wps[:, 0:512],
                                           hG[:, 512 * f + 128 * a:512 * f + 128 * (a + 1)],
                                           w2_sb[:, f, 0:512],
                                           start=(f == 0), stop=(f == NF - 1))
                            i1 = tn.matmul(wps[:, 512:1024],
                                           hG[:, 512 * f + 128 * a:512 * f + 128 * (a + 1)],
                                           w2_sb[:, f, 512:1024],
                                           start=(f == 0), stop=(f == NF - 1))
                        cT.inc(i0)
                        cT.inc(i1)
                        w2_v[("mm", e, a)] = cT.n

                    @block.vector
                    def _(v, e=e, a=a):
                        v.wait_ge(cT.h, w2_v[("mm", e, a)])
                        ls = last_fg.get(a % 2)
                        if ls is not None:
                            v.wait_ge(dFgP[a % 2].h, ls)
                        wps = PC if a % 2 == 0 else PD
                        with nc.allow_low_precision(reason="ffn out staged bf16 like other matmul operands"):
                            v.tensor_copy(ffg2[a % 2], wps[:, :])
                        w2_v[(e, a)] = cV.n

                    @block.sync
                    def _(sy, e=e, a=a):
                        sy.wait_ge(cV.h, w2_v[(e, a)])
                        dFgP[a % 2].inc(sy.dma_start(
                            out=fg_dram.ap()[512 * e + 128 * a:
                                             512 * e + 128 * (a + 1), :],
                            in_=ffg2[a % 2]))
                        last_fg[a % 2] = dFgP[a % 2].n
                sp[("w2", e)] = cT.n

                nxt = e + 1
                if nxt < E:
                    @block.gpsimd
                    def _(g, e=e, nxt=nxt):
                        g.wait_ge(cT.h, sp[("w2", e)])
                        for f in range(NF):
                            dW2.inc(g.dma_start(out=w2_sb[:, f, :],
                                                in_=w2_hbm[nxt, 128 * f:128 * (f + 1), :]))
                    w2_arr[(0, nxt)] = dW2.n
            sp["ffn_ct"] = cT.n
            sp["fg_d0"] = dFgP[0].n
            sp["fg_d1"] = dFgP[1].n

        epi_seq = []
        epi_state = {"pf": 0, "st": 0}
        epi_gath = {}
        add_done = {}

        def epi_push(half):
            for th in range(8):
                epi_seq.append((half, th))

        def epi_prefetch(k=1):
            while k > 0 and epi_state["pf"] < len(epi_seq):
                i = epi_state["pf"]
                if i >= 2 and (i - 2) not in add_done:
                    break
                half, th = epi_seq[i]
                tglob = 8 * half + th
                b = i % 2

                @block.gpsimd
                def _(g, i=i, tglob=tglob, b=b):
                    g.wait_ge(dFgP[0].h, sp["fg_d0"])
                    g.wait_ge(dFgP[1].h, sp["fg_d1"])
                    prev = add_done.get(i - 2)
                    if prev is not None:
                        g.wait_ge(cV.h, prev)
                    dFo[b].inc(g.indirect_dma_start(
                        out=fo_sb[b],
                        out_offset=None,
                        in_=fg_dram.ap(),
                        in_offset=IndirectOffsetOnAxis(inv_i32[:, tglob:tglob + 1], 0)))

                @block.sync
                def _(sy, i=i, tglob=tglob, b=b):
                    sy.wait_ge(cV.h, sp["routing_cv"])
                    sy.wait_ge(dSc[0].h, sp["sc0"])
                    sy.wait_ge(dSc[1].h, sp["sc1"])
                    prev = add_done.get(i - 2)
                    if prev is not None:
                        sy.wait_ge(cV.h, prev)
                    for q in range(4):
                        dXr[b].inc(sy.dma_start(
                            out=xrel[b][:, 256 * q:256 * (q + 1)],
                            in_=x_hbm[128 * tglob:128 * (tglob + 1),
                                      256 * q:256 * (q + 1)]))
                epi_gath[i] = (dFo[b].n, dXr[b].n)
                epi_state["pf"] += 1
                k -= 1

        def epi_step():
            i = epi_state["st"]
            if i >= len(epi_seq) or i >= epi_state["pf"]:
                return False
            half, th = epi_seq[i]
            tglob = 8 * half + th
            if oev_v.get(tglob) is None:
                return False
            b = i % 2
            fod, xrd = epi_gath[i]

            @block.vector
            def _(v, th=th, tglob=tglob, b=b, fod=fod, xrd=xrd):
                v.wait_ge(dFo[b].h, fod)
                v.wait_ge(dXr[b].h, xrd)
                v.wait_ge(cV.h, oev_v[tglob])
                v.scalar_tensor_tensor(acc[:, th, :], fo_sb[b],
                                       g_tok[:, tglob:tglob + 1], acc[:, th, :],
                                       op0=ALU.mult, op1=ALU.add)
                v.tensor_add(acc[:, th, :], acc[:, th, :], xrel[b])
            add_done[i] = cV.n

            @block.sync
            def _(sy, half=half, th=th, tglob=tglob, adv=add_done[i]):
                sy.wait_ge(cV.h, adv)
                dOutT[th].inc(sy.dma_start(
                    out=out_hbm[128 * tglob:128 * (tglob + 1), :],
                    in_=acc[:, th, :]))
                out_store_d[(half, th)] = dOutT[th].n
            epi_state["st"] += 1
            epi_prefetch(1)
            return True

        def epi_hook():
            epi_step()

        emit_qproj(0)
        bis_pair = [0]

        def bis_enable_hook():
            bis_pair[0] += 1
            if bis_pair[0] >= 3:
                bis.enabled = True

        emit_heads(0, pair_hook=bis_enable_hook)
        emit_oproj(0)
        bis.enabled = True
        emit_qproj(1)
        emit_heads(1)
        sp["h1_znorm_cv"] = cV.n

        while bis.done < BIS_ITERS:
            @block.vector
            def _(v):
                bis.upd(v)
                bis.pre(v)

            @block.tensor
            def _(tn):
                bis.mm(tn)

        @block.vector
        def _(v):
            bis.upd(v)
            emit_gates(v)
        sp["routing_cv"] = cV.n

        if "tau" in DBG:
            @block.sync
            def _(sy):
                sy.wait_ge(cV.h, gates_done_v[0])
                dDbg.inc(sy.dma_start(out=dbg_hbm["tau"], in_=lo[:, :]))
        if "gates" in DBG:
            @block.sync
            def _(sy):
                sy.wait_ge(cV.h, gates_done_v[0])
                dDbg.inc(sy.dma_start(out=dbg_hbm["gates"], in_=gates[:, :, :]))

        emit_indices_and_scatter()
        emit_oproj(1)

        oproj_h0_ct = cT.n

        @block.gpsimd
        def _(g):
            g.wait_ge(cT.h, oproj_h0_ct)
            g.wait_ge(dStash.h, dStash.n)
            for f in range(NF):
                dW2.inc(g.dma_start(out=w2_sb[:, f, :],
                                    in_=w2_hbm[0, 128 * f:128 * (f + 1), :]))
        w2_arr[(0, 0)] = dW2.n

        emit_sparse_ffn()

        @block.gpsimd
        def _(g):
            g.wait_ge(dStash.h, dStash.n)
            g.wait_ge(cT.h, sp[("w1", E - 1)])
            for k in range(4):
                dWq2.inc(g.dma_start(out=regB[:, 2048 * k:2048 * (k + 1)],
                                     in_=wq_dram.ap()[:, 2048 * k:2048 * (k + 1)]))

        @block.gpsimd
        def _(g):
            g.wait_ge(cT.h, sp["ffn_ct"])
            for k in range(4):
                dWo2.inc(g.dma_start(out=regR[:, 8192 + 2048 * k:
                                              8192 + 2048 * (k + 1)],
                                     in_=wo_dram.ap()[:, 2048 * k:2048 * (k + 1)]))
        w_q_done[0] = (dWq2, dWq2.n)
        w_o_done[0] = (dWo2, dWo2.n)
        wq_view[0] = wq_sb2

        epi_push(0)
        epi_prefetch(2)
        emit_qproj(2)
        emit_heads(2, pair_hook=epi_hook)
        while epi_state["st"] < 8:
            epi_prefetch(1)
            assert epi_step(), "half-0 epilogue pump stalled"
        emit_oproj(2)
        epi_push(1)
        epi_prefetch(1)
        emit_qproj(3)
        emit_heads(3, pair_hook=epi_hook)
        emit_oproj(3)
        guard = 0
        while epi_state["st"] < 16:
            epi_prefetch(1)
            epi_step()
            guard += 1
            assert guard < 200, "epilogue drain stalled"

        @block.sync
        def _(sy):
            for th in range(8):
                sy.wait_ge(dOutT[th].h, dOutT[th].n)
            if dDbg.n:
                sy.wait_ge(dDbg.h, dDbg.n)

    return nc


def kernel(**inputs):
    x = np.ascontiguousarray(np.asarray(inputs["L0_attn_x"], dtype=np.float32))
    wq = np.ascontiguousarray(np.asarray(inputs["L0_attn_Wq"], dtype=np.float32))
    bq = np.ascontiguousarray(np.asarray(inputs["L0_attn_bq"], dtype=np.float32)).reshape(1, DM)
    wk = np.ascontiguousarray(np.asarray(inputs["L0_attn_Wk"], dtype=np.float32))
    bk = np.ascontiguousarray(np.asarray(inputs["L0_attn_bk"], dtype=np.float32)).reshape(1, HD)
    wv = np.ascontiguousarray(np.asarray(inputs["L0_attn_Wv"], dtype=np.float32))
    bv = np.ascontiguousarray(np.asarray(inputs["L0_attn_bv"], dtype=np.float32)).reshape(1, HD)
    wo = np.ascontiguousarray(np.asarray(inputs["L0_attn_Wo"], dtype=np.float32))
    bo = np.ascontiguousarray(np.asarray(inputs["L0_attn_bo"], dtype=np.float32)).reshape(1, DM)
    wg = np.ascontiguousarray(np.asarray(inputs["L0_ffn_Wg"], dtype=np.float32))
    w1 = np.ascontiguousarray(np.asarray(inputs["L0_ffn_W1"], dtype=np.float32))
    w2 = np.ascontiguousarray(np.asarray(inputs["L0_ffn_W2"], dtype=np.float32))

    nc = build_nc()
    in_maps = []
    for c in range(NCORES):
        in_maps.append({
            "x": x[c], "wq": wq, "bq": bq, "wk": wk, "bk": bk, "wv": wv, "bv": bv,
            "wo": wo, "bo": bo, "wg": wg, "w1": w1, "w2": w2,
        })
    res = run_bass_kernel_spmd(nc, in_maps, core_ids=list(range(NCORES)))
    return np.stack([res.results[c]["out"] for c in range(NCORES)], axis=0)


# version 27
# speedup vs baseline: 1.3850x; 1.0029x over previous
import math
from contextlib import ExitStack

import numpy as np

import concourse.bass as bass
import concourse.mybir as mybir
from concourse.bass_core_rust import IndirectOffsetOnAxis
from concourse.bass_utils import run_bass_kernel_spmd

F32 = mybir.dt.float32
BF16 = mybir.dt.bfloat16
U8 = mybir.dt.uint8
I32 = mybir.dt.int32
AF = mybir.ActivationFunctionType
ALU = mybir.AluOpType
AX = mybir.AxisListType

NCORES = 8
T, DM, H, HD, FF, E = 2048, 1024, 16, 64, 2048, 8
NT, ND, NF = T // 128, DM // 128, FF // 128
CAP = float(T * NCORES // E)
CEXP = 16.0
TWO_PI = 2.0 * math.pi
BIS_ITERS = 40
LN_EPS = 1e-5

DBG = []


class Ctr:
    def __init__(self, h, step=1):
        self.h = h
        self.n = 0
        self.step = step

    def inc(self, instr, k=None):
        k = self.step if k is None else k
        instr.then_inc(self.h, k)
        self.n += k
        return instr



class AutoSync:

    _PASS = ("wait_ge", "dma_start", "dma_start_transpose", "collective_compute",
             "indirect_dma_start", "sem_inc", "wait_op")

    def __init__(self, eng, ctr):
        self._eng = eng
        self._ctr = ctr

    def __getattr__(self, name):
        f = getattr(self._eng, name)
        if name.startswith("_") or name in AutoSync._PASS or not callable(f):
            return f
        eng, ctr = self._eng, self._ctr

        def wrapped(*a, **k):
            eng.wait_ge(ctr.h, ctr.n)
            ins = f(*a, **k)
            ctr.inc(ins)
            return ins

        return wrapped


class BlockWrap:
    def __init__(self, block, cV, cS, cP):
        self._b = block
        self._cV, self._cS, self._cP = cV, cS, cP

    def vector(self, f):
        return self._b.vector(lambda v: f(AutoSync(v, self._cV)))

    def scalar(self, f):
        return self._b.scalar(lambda s: f(AutoSync(s, self._cS)))

    def gpsimd(self, f):
        return self._b.gpsimd(lambda g: f(AutoSync(g, self._cP)))

    def tensor(self, f):
        return self._b.tensor(f)

    def sync(self, f):
        return self._b.sync(f)


def build_nc():
    nc = bass.Bass("TRN2", target_bir_lowering=False)

    x_hbm = nc.dram_tensor("x", [T, DM], F32, kind="ExternalInput").ap()
    wq_hbm = nc.dram_tensor("wq", [DM, DM], F32, kind="ExternalInput").ap()
    bq_hbm = nc.dram_tensor("bq", [1, DM], F32, kind="ExternalInput").ap()
    wk_hbm = nc.dram_tensor("wk", [DM, HD], F32, kind="ExternalInput").ap()
    bk_hbm = nc.dram_tensor("bk", [1, HD], F32, kind="ExternalInput").ap()
    wv_hbm = nc.dram_tensor("wv", [DM, HD], F32, kind="ExternalInput").ap()
    bv_hbm = nc.dram_tensor("bv", [1, HD], F32, kind="ExternalInput").ap()
    wo_hbm = nc.dram_tensor("wo", [DM, DM], F32, kind="ExternalInput").ap()
    bo_hbm = nc.dram_tensor("bo", [1, DM], F32, kind="ExternalInput").ap()
    wg_hbm = nc.dram_tensor("wg", [DM, E], F32, kind="ExternalInput").ap()
    w1_hbm = nc.dram_tensor("w1", [E, DM, FF], F32, kind="ExternalInput").ap()
    w2_hbm = nc.dram_tensor("w2", [E, FF, DM], F32, kind="ExternalInput").ap()
    out_hbm = nc.dram_tensor("out", [T, DM], F32, kind="ExternalOutput").ap()

    ag_in = nc.dram_tensor("ag_in", [128, E, NT], F32)
    wq_dram = nc.dram_tensor("wq_dram", [128, 8192], BF16)
    wo_dram = nc.dram_tensor("wo_dram", [128, 8192], BF16)
    xn_dram = nc.dram_tensor("xn_dram", [T, DM], BF16)
    xg_dram = nc.dram_tensor("xg_dram", [E * 512, DM], BF16)
    fg_dram = nc.dram_tensor("fg_dram", [E * 512, DM], BF16)
    ag_out = nc.dram_tensor("ag_out", [NCORES, 128, E, NT], F32, addr_space="Shared")

    dbg_specs = {
        "xnT": ([128, ND, T], BF16), "kT": ([128, T], BF16),
        "vstat": ([128, NT, 65], BF16), "probs": ([128, NT, E], F32),
        "tau": ([128, E], F32), "gates": ([128, NT, E], F32),
        "qT0": ([128, ND * 512], BF16), "attnT0": ([128, ND * 512], BF16),
        "acc0": ([128, NT // 2, DM], F32),
        "probs_all": ([128, E, 128], F32), "kvtm": ([128, NT, 128], F32),
    }
    dbg_hbm = {}
    for name in DBG:
        shp, dt = dbg_specs[name]
        dbg_hbm[name] = nc.dram_tensor("dbg_" + name, shp, dt, kind="ExternalOutput").ap()

    ctx = ExitStack()
    with ctx:
        def sb(name, shape, dt=F32):
            return ctx.enter_context(nc.sbuf_tensor(name, shape, dt))

        def psum_t(name, shape, dt=F32):
            return ctx.enter_context(nc.psum_tensor(name, shape, dt))

        def sem(name, step=1):
            return Ctr(ctx.enter_context(nc.semaphore(name)), step)

        xnT = sb("xnT", [128, ND, T], BF16)
        acc = sb("acc", [128, NT // 2, DM], F32)
        costab = sb("costab", [128, NT, 32], F32)
        sintab = sb("sintab", [128, NT, 32], F32)
        costab_bf = sb("costab_bf", [128, NT, 32], BF16)
        sintab_bf = sb("sintab_bf", [128, NT, 32], BF16)
        trimask = sb("trimask", [128, 128], BF16)
        idf32 = sb("idf32", [128, 128], F32)
        idbf = sb("idbf", [128, 128], BF16)
        ones_1x128_bf = sb("ones_1x128_bf", [1, 128], BF16)
        ones_1x128_f = sb("ones_1x128_f", [1, 128], F32)
        ones_128_bf = sb("ones_128_bf", [128, 128], BF16)
        negc = sb("negc", [128, 1], F32)
        kT = sb("kT", [128, T], BF16)
        vstat = sb("vstat", [128, NT, 65], BF16)
        kvtm = sb("kvtm", [128, NT, 128], F32)
        probs = sb("probs", [128, NT, E], F32)
        probs_all = sb("probs_all", [128, E, 128], F32)
        probs_em = sb("probs_em", [128, E, NT], F32)
        cmpb = sb("cmpb", [128, E, 128], BF16)
        cntpp = sb("cntpp", [128, E], BF16)
        tau = sb("tau", [128, E], F32)
        lo = sb("lo", [128, E], F32)
        hi = sb("hi", [128, E], F32)
        geb = sb("geb", [128, E], U8)
        geb_inv = sb("geb_inv", [128, E], U8)
        gsel = sb("gsel", [128, NT, E], F32)
        gmax = sb("gmax", [128, NT], F32)
        gates = sb("gates", [128, NT, E], F32)
        wkv = sb("wkv", [128, ND, 128], BF16)
        wg_sb = sb("wg_sb", [128, ND, E], BF16)
        bq_row = sb("bq_row", [1, DM], BF16)
        bkv_row = sb("bkv_row", [1, 128], BF16)
        bo_row = sb("bo_row", [1, DM], BF16)
        rz_pair = sb("rz_pair", [1, 1024], BF16)
        zi_sb = sb("zi_sb", [64, 512], BF16)
        lmax = sb("lmax", [128, NT], F32)
        zrow = sb("zrow", [128, NT], F32)
        st_ssq = sb("st_ssq", [128, 4], F32)
        st_m = sb("st_m", [128, 4], F32)
        st_nm = sb("st_nm", [128, 4], F32)
        st_r = sb("st_r", [128, 4], F32)
        st_rs = sb("st_rs", [128, 4], F32)
        st_nmr = sb("st_nmr", [128, 4], F32)
        st_ln = sb("st_ln", [128, 4], F32)
        tgen = sb("tgen", [128, NT, 32], F32)
        tgen3 = sb("tgen3", [128, NT, 32], F32)
        tgen2 = sb("tgen2", [128, NT, 32], F32)

        tt_iota = sb("tt_iota", [128, NT], F32)
        tgen_i = tgen3[:, :, :].bitcast(I32)
        zln_sb = tgen[0:1, :, :].rearrange("p a b -> p (a b)")
        j_iota = sb("j_iota", [1, 32], F32)
        theta = sb("theta", [1, 32], F32)
        thetaR = sb("thetaR", [128, 32], F32)

        kvflat = kvtm[:, :, :].rearrange("p a b -> p (a b)")
        inv_i32 = kvflat[:, 0:16].bitcast(I32)
        g_tok = kvflat[:, 16:32]
        inv_f = kvflat[:, 32:48]
        slot_f = kvflat[:, 48:64]
        e_f = kvflat[:, 64:80]
        e512 = kvflat[:, 80:88]
        flagsE = kvflat[:, 128:192].bitcast(BF16)
        colpfx_sb = kvflat[:, 192:320]
        slotsel = kvflat[:, 320:448]
        flagsF = kvflat[:, 448:576]
        eidx8 = kvflat[:, 576:584]
        stricL = kvflat[:, 584:648].bitcast(BF16)
        xsc_sb = [kvflat[:, 768 + k * 512:1280 + k * 512].bitcast(BF16)
                  for k in range(2)]

        regA = sb("regA", [128, 16384], BF16)

        def carveA(off_bytes, n_elems, dt):
            if dt == BF16:
                return regA[:, off_bytes // 2: off_bytes // 2 + n_elems]
            return regA[:, off_bytes // 2:(off_bytes + n_elems * 4) // 2].bitcast(dt)

        xbuf = [carveA(0, 1024, F32), carveA(4096, 1024, F32),
                carveA(16384, 1024, F32), carveA(20480, 1024, F32)]
        dLx4 = None
        sq_scr = carveA(8192, 1024, F32)
        xnstage = [carveA(12288 + k * 2048, 1024, BF16) for k in range(2)]

        qTq = carveA(0, ND * 512, BF16)
        attnTq = carveA(8192, ND * 512, BF16)
        q_tm = [carveA(16384 + k * 2048, 1024, BF16) for k in range(2)]
        q_tm3 = [a.rearrange("p (h j) -> p h j", h=16) for a in q_tm]
        rope_scr = carveA(20480, 1024, BF16).rearrange("p (h j) -> p h j", h=16)
        krope_scr = carveA(0, 1024, F32).rearrange("p (h j) -> p h j", h=16)
        pbuf = [carveA(22528 + k * 2048, 1024, BF16) for k in range(4)]

        hT = carveA(0, NF * 1024, BF16)
        xg_sb = carveA(0, 4096, BF16)
        xgT = carveA(8192, 4096, BF16)
        hG = carveA(16384, 8192, BF16)

        ffg2 = [tgen2[:, :, :].rearrange("p a b -> p (a b)").bitcast(BF16),
                tgen3[:, :, :].rearrange("p a b -> p (a b)").bitcast(BF16)]
        fo_sb = ffg2
        xrel = [probs_all[:, :, :].rearrange("p e t -> p (e t)"),
                kvflat[:, 1024:2048]]

        regB = sb("regB", [128, 16384], BF16)
        w1_sb = regB.rearrange("p (d f) -> p d f", d=ND)

        regR = sb("regR", [128, 16384], BF16)
        wq_sb = regR[:, 0:8192].rearrange("p (d f) -> p d f", d=ND)
        wq_sb2 = regB[:, 0:8192].rearrange("p (d f) -> p d f", d=ND)
        wo_sb = regR[:, 8192:16384].rearrange("p (d f) -> p d f", d=ND)
        w2_sb = regR.rearrange("p (f d) -> p f d", f=NF)

        PA = psum_t("PA", [128, 1024], F32)
        PB = psum_t("PB", [128, 1024], F32)
        PC = psum_t("PC", [128, 1024], F32)
        PD = psum_t("PD", [128, 1024], F32)

        dLx = [sem("dLx%d" % i, 16) for i in range(4)]
        dWa = sem("dWa", 16)
        dWq = sem("dWq", 16)
        dWo = sem("dWo", 16)
        dW1 = sem("dW1", 16)
        dW2 = sem("dW2", 16)
        dPr = sem("dPr", 16)
        dXr = [sem("dXr0", 16), sem("dXr1", 16)]
        dOutT = [sem("dOutT%d" % i, 16) for i in range(8)]
        dDbg = sem("dDbg", 16)
        dStash = sem("dStash", 16)
        dWq2 = sem("dWq2", 16)
        dWo2 = sem("dWo2", 16)
        dXn = [sem("dXn0", 16), sem("dXn1", 16)]
        dPre = sem("dPre", 16)
        dXs = [sem("dXs0", 16), sem("dXs1", 16)]
        dSc = [sem("dSc0", 16), sem("dSc1", 16)]
        dXgA = [sem("dXg%d" % a, 16) for a in range(4)]
        dFgP = [sem("dFg0", 16), sem("dFg1", 16)]
        dFo = [sem("dFo0", 16), sem("dFo1", 16)]
        cT = sem("cT")
        cS = sem("cS")
        cV = sem("cV")
        cP = sem("cP")
        ccc = sem("ccc")

        block = BlockWrap(ctx.enter_context(nc.Block()), cV, cS, cP)

        @block.gpsimd
        def _(g):
            g.memset(ones_1x128_bf[:, :], 1.0)
            g.memset(ones_1x128_f[:, :], 1.0)
            g.memset(ones_128_bf[:, :], 1.0)
            g.memset(negc[:, :], -CEXP)
            g.memset(vstat[:, :, :], 1.0)
            g.memset(lo[:, :], 0.0)
            g.memset(hi[:, :], 1.0)
            g.memset(tau[:, :], 0.5)
            g.memset(idf32[:, :], 1.0)
            g.memset(idbf[:, :], 1.0)
            g.memset(trimask[:, :], 1.0)
            g.wait_ge(cP.h, cP.n)
            g.affine_select(out=idf32[:, :], in_=idf32[:, :], compare_op=ALU.is_equal,
                            fill=0.0, base=0, pattern=[[-1, 128]], channel_multiplier=1)
            g.affine_select(out=idbf[:, :], in_=idbf[:, :], compare_op=ALU.is_equal,
                            fill=0.0, base=0, pattern=[[-1, 128]], channel_multiplier=1)
            g.affine_select(out=trimask[:, :], in_=trimask[:, :], compare_op=ALU.is_ge,
                            fill=0.0, base=0, pattern=[[1, 128]], channel_multiplier=-1)
            g.iota(tt_iota[:, :], [[128, NT]], channel_multiplier=1,
                   allow_small_or_imprecise_dtypes=True)
            g.iota(j_iota[:, :], [[1, 32]], channel_multiplier=0,
                          allow_small_or_imprecise_dtypes=True)
            dWa.inc(g.dma_start(out=wkv[:, :, 0:64],
                                in_=wk_hbm.rearrange("(d p) f -> p d f", p=128)))
            dWa.inc(g.dma_start(out=wkv[:, :, 64:128],
                                in_=wv_hbm.rearrange("(d p) f -> p d f", p=128)))
            dWa.inc(g.dma_start(out=wg_sb[:, :, :],
                                in_=wg_hbm.rearrange("(d p) e -> p d e", p=128)))
            dWa.inc(g.dma_start(out=bkv_row[:, 0:64], in_=bk_hbm))
            dWa.inc(g.dma_start(out=bkv_row[:, 64:128], in_=bv_hbm))
        w_kv_done = [dWa.n]
        p0_done_p = cP.n

        w_q_done = [None]
        w_o_done = [None]
        w1_arr = {}

        def emit_rope_tables():
            @block.scalar
            def _(s):
                s.wait_ge(cP.h, cP.n)
                s.activation(theta[:, :], j_iota[:, :], AF.Exp,
                                    scale=-math.log(10000.0) / 32.0)

            @block.tensor
            def _(t):
                t.wait_ge(cS.h, cS.n)
                cT.inc(t.matmul(PA[:, 0:32], ones_1x128_f[:, :], theta[:, :],
                                start=True, stop=True))

            @block.vector
            def _(v):
                v.wait_ge(cT.h, cT.n)
                v.tensor_copy(thetaR[:, :], PA[:, 0:32])
                for a in range(NT):
                    v.tensor_scalar_mul(tgen[:, a, :], thetaR[:, :], tt_iota[:, a:a + 1])
                v.tensor_scalar_mul(tgen[:, :, :], tgen[:, :, :], 1.0 / TWO_PI)
                v.tensor_copy(tgen_i, tgen[:, :, :])
                v.tensor_copy(tgen2[:, :, :], tgen_i)
                v.tensor_sub(tgen2[:, :, :], tgen[:, :, :], tgen2[:, :, :])
                v.tensor_single_scalar(tgen3[:, :, :], tgen2[:, :, :], 0.5, ALU.is_gt)
                v.tensor_sub(tgen2[:, :, :], tgen2[:, :, :], tgen3[:, :, :])

            @block.scalar
            def _(s):
                s.wait_ge(cV.h, cV.n)
                s.activation(sintab[:, :, :], tgen2[:, :, :], AF.Sin, scale=TWO_PI)

            @block.vector
            def _(v):
                v.wait_ge(cS.h, cS.n)
                v.tensor_scalar_add(tgen[:, :, :], tgen[:, :, :], 0.25)
                v.tensor_copy(tgen_i, tgen[:, :, :])
                v.tensor_copy(tgen2[:, :, :], tgen_i)
                v.tensor_sub(tgen2[:, :, :], tgen[:, :, :], tgen2[:, :, :])
                v.tensor_single_scalar(tgen3[:, :, :], tgen2[:, :, :], 0.5, ALU.is_gt)
                v.tensor_sub(tgen2[:, :, :], tgen2[:, :, :], tgen3[:, :, :])

            @block.scalar
            def _(s):
                s.wait_ge(cV.h, cV.n)
                s.activation(costab[:, :, :], tgen2[:, :, :], AF.Sin, scale=TWO_PI)
            tab_s = cS.n

            @block.vector
            def _(v):
                v.wait_ge(cS.h, tab_s)
                with nc.allow_low_precision(reason="bf16 rope tables; q is bf16"):
                    v.tensor_copy(costab_bf[:, :, :], costab[:, :, :])
                    v.tensor_copy(sintab_bf[:, :, :], sintab[:, :, :])

        xn_sca = [0] * NT
        xn_done_v = [0] * NT
        rec_done = [0] * NT
        xn_st_d = [0] * NT
        kv_copy = [0] * NT
        trp_t = [0] * NT
        lg_mm_t = [0] * NT
        ktr_t = [0] * NT
        neg_done_v = [0] * NT
        lg_sca = [0] * NT
        kt_v = [0] * NT

        krs = probs_em[:, :, :].rearrange("p e a -> p (e a)")[:, 0:64]
        kv_slots = [PA[:, 0:128], PA[:, 512:640]]
        lg_slots = [PB[:, 0:8], PB[:, 512:520]]

        for r in range(NT + 4):
            if r < NT:
                b = r % 4

                @block.sync
                def _(sy, r=r, b=b):
                    if r >= 4:
                        sy.wait_ge(cS.h, xn_sca[r - 4])
                    for q in range(4):
                        dLx[b].inc(sy.dma_start(
                            out=xbuf[b][:, 256 * q:256 * (q + 1)],
                            in_=x_hbm[128 * r:128 * (r + 1), 256 * q:256 * (q + 1)]))

            if 4 <= r < 8:
                k = r - 4
                @block.gpsimd
                def _(g, k=k):
                    if k == 0:
                        dWq.inc(g.dma_start(out=bq_row[:, :], in_=bq_hbm))
                    dWq.inc(g.dma_start(
                        out=wq_sb[:, 2 * k:2 * k + 2, :],
                        in_=wq_hbm.rearrange("(d p) f -> p d f",
                                             p=128)[:, 2 * k:2 * k + 2, :]))
                if r == 7:
                    w_q_done[0] = (dWq, dWq.n)
            if 8 <= r < 12:
                k = r - 8
                @block.gpsimd
                def _(g, k=k):
                    if k == 0:
                        dWo.inc(g.dma_start(out=bo_row[:, :], in_=bo_hbm))
                    dWo.inc(g.dma_start(
                        out=wo_sb[:, 2 * k:2 * k + 2, :],
                        in_=wo_hbm.rearrange("(d p) f -> p d f",
                                             p=128)[:, 2 * k:2 * k + 2, :]))
                if r == 11:
                    w_o_done[0] = (dWo, dWo.n)
            if 12 <= r < 16:
                k = r - 12
                @block.gpsimd
                def _(g, k=k):
                    for d in (2 * k, 2 * k + 1):
                        dW1.inc(g.dma_start(out=w1_sb[:, d, :],
                                            in_=w1_hbm[0, 128 * d:128 * (d + 1), :]))
                if r == 15:
                    w1_arr[(0, 0)] = dW1.n

            sq_plan = cS.n + 1 if r < NT else cS.n

            @block.scalar
            def _(s, r=r):
                if r < NT:
                    b = r % 4
                    bs = r % 4
                    s.wait_ge(dLx[b].h, 64 * (r // 4 + 1))
                    if r >= 4:
                        s.wait_ge(cV.h, rec_done[r - 4])
                    s.activation(sq_scr, xbuf[b], AF.Square,
                                 accum_out=st_ssq[:, bs:bs + 1])
                if 1 <= r <= NT:
                    t = r - 1
                    bb = t % 2
                    tb = t % 4
                    if t >= 2:
                        s.wait_ge(dXn[bb].h, xn_st_d[t - 2])
                        s.wait_ge(cT.h, trp_t[t - 2])
                    s.wait_ge(cV.h, rec_done[t])
                    s.activation(st_ln[:, tb:tb + 1], st_r[:, tb:tb + 1], AF.Ln)
                    s.activation(st_rs[:, tb:tb + 1], st_ln[:, tb:tb + 1], AF.Exp,
                                 scale=-0.5)
                    s.activation(st_nmr[:, tb:tb + 1], st_nm[:, tb:tb + 1], AF.Identity,
                                 scale=st_rs[:, tb:tb + 1])
                    s.activation(xnstage[bb], xbuf[t % 4], AF.Identity,
                                 bias=st_nmr[:, tb:tb + 1], scale=st_rs[:, tb:tb + 1])
                    xn_sca[t] = cS.n
                if 4 <= r < NT + 4:
                    te = r - 4
                    lgps_e = lg_slots[te % 2]
                    s.wait_ge(cV.h, neg_done_v[te])
                    s.activation(probs[:, te, :], lgps_e, AF.Exp,
                                 bias=lmax[:, te:te + 1], accum_out=zrow[:, te:te + 1])
                    lg_sca[te] = cS.n

            if r < NT:
                b = r % 4
                bs = r % 4

                @block.vector
                def _(v, r=r, b=b, bs=bs, sq_done=sq_plan):
                    v.wait_ge(dLx[b].h, 64 * (r // 4 + 1))
                    if r >= 4:
                        v.wait_ge(cS.h, xn_sca[r - 4])
                    v.reduce_sum(st_m[:, bs:bs + 1], xbuf[b], axis=AX.X)
                    v.tensor_scalar_mul(st_m[:, bs:bs + 1], st_m[:, bs:bs + 1], 1.0 / DM)
                    v.tensor_scalar_mul(st_nm[:, bs:bs + 1], st_m[:, bs:bs + 1], -1.0)
                    v.wait_ge(cS.h, sq_done)
                    v.tensor_mul(st_r[:, bs:bs + 1], st_m[:, bs:bs + 1], st_m[:, bs:bs + 1])
                    v.scalar_tensor_tensor(st_r[:, bs:bs + 1], st_ssq[:, bs:bs + 1], 1.0 / DM,
                                           st_r[:, bs:bs + 1], op0=ALU.mult, op1=ALU.subtract)
                    v.tensor_scalar_add(st_r[:, bs:bs + 1], st_r[:, bs:bs + 1], LN_EPS)
                    rec_done[r] = cV.n

            if r == 2:
                emit_rope_tables()

            if 1 <= r <= NT:
                t = r - 1
                bb = t % 2

                @block.gpsimd
                def _(g, t=t, bb=bb):
                    g.wait_ge(cS.h, xn_sca[t])
                    dXn[bb].inc(g.dma_start(out=xn_dram.ap()[128 * t:128 * (t + 1), :],
                                            in_=xnstage[bb]))
                xn_st_d[t] = dXn[bb].n

                @block.tensor
                def _(tn, t=t, bb=bb):
                    tn.wait_ge(cS.h, xn_sca[t])
                    if t == 0:
                        tn.wait_ge(cP.h, p0_done_p)
                    if t >= 2:
                        tn.wait_ge(cV.h, xn_done_v[t - 2])
                    pc = PC.bitcast(BF16)[:, 1024 * bb:1024 * (bb + 1)]
                    for c in range(ND):
                        ins = tn.transpose(pc[:, 128 * c:128 * (c + 1)],
                                           xnstage[bb][:, 128 * c:128 * (c + 1)],
                                           idbf[:, :])
                    cT.inc(ins)
                trp_t[t] = cT.n

            if 2 <= r <= NT + 1:
                t = r - 2
                bb = t % 2

                @block.vector
                def _(v, t=t, bb=bb):
                    v.wait_ge(cT.h, trp_t[t])
                    pc = PC.bitcast(BF16)[:, 1024 * bb:1024 * (bb + 1)]
                    v.tensor_copy(
                        xnT[:, :, 128 * t:128 * (t + 1)],
                        pc.rearrange("p (c t) -> p c t", c=ND))
                    xn_done_v[t] = cV.n

                kvps = kv_slots[t % 2]
                lgps = lg_slots[t % 2]

                @block.tensor
                def _(tn, t=t, kvps=kvps, lgps=lgps):
                    tn.wait_ge(dWa.h, w_kv_done[0])
                    tn.wait_ge(cV.h, xn_done_v[t])
                    if t >= 2:
                        tn.wait_ge(cV.h, kv_copy[t - 2])
                        tn.wait_ge(cS.h, lg_sca[t - 2])
                    ts = slice(128 * t, 128 * (t + 1))
                    tn.matmul(kvps, ones_1x128_bf[:, :], bkv_row[:, :],
                              start=True, stop=False)
                    for d in range(ND):
                        ins = tn.matmul(kvps, xnT[:, d, ts], wkv[:, d, :],
                                        start=False, stop=(d == ND - 1))
                    cT.inc(ins)
                    for d in range(ND):
                        ins = tn.matmul(lgps, xnT[:, d, ts], wg_sb[:, d, :],
                                        start=(d == 0), stop=(d == ND - 1))
                    cT.inc(ins)
                lg_mm_t[t] = cT.n

            if 3 <= r <= NT + 2:
                t = r - 3
                kvps = kv_slots[t % 2]
                lgps = lg_slots[t % 2]

                @block.vector
                def _(v, t=t, kvps=kvps, lgps=lgps):
                    v.wait_ge(cT.h, lg_mm_t[t])
                    v.tensor_copy(kvtm[:, t, :], kvps)
                    v.reduce_max(lmax[:, t:t + 1], lgps, axis=AX.X, negate=True)
                neg_done_v[t] = cV.n
                kv_copy[t] = cV.n

                @block.gpsimd
                def _(g, t=t):
                    g.wait_ge(cV.h, kv_copy[t])
                    k2 = kvtm[:, t, 0:64]
                    cA, cB = costab[:, t, 0:16], costab[:, t, 16:32]
                    sA, sB = sintab[:, t, 0:16], sintab[:, t, 16:32]
                    g.tensor_tensor(krs[:, 32:48], k2[:, 16:32], sA, ALU.mult)
                    g.tensor_tensor(krs[:, 0:16], k2[:, 0:16], cA, ALU.mult)
                    g.tensor_sub(krs[:, 0:16], krs[:, 0:16], krs[:, 32:48])
                    g.tensor_tensor(krs[:, 48:64], k2[:, 0:16], sB, ALU.mult)
                    g.tensor_tensor(krs[:, 16:32], k2[:, 16:32], cB, ALU.mult)
                    g.tensor_add(krs[:, 16:32], krs[:, 16:32], krs[:, 48:64])
                    g.tensor_copy(k2[:, 0:32], krs[:, 0:32])
                krope_p = cP.n

                @block.tensor
                def _(tn, t=t, krope_p=krope_p):
                    tn.wait_ge(cP.h, krope_p)
                    if t >= 2:
                        tn.wait_ge(cV.h, kt_v[t - 2])
                    ktp = PD[0:64, 0:128] if t % 2 == 0 else PD[0:64, 512:640]
                    cT.inc(tn.transpose(ktp, kvtm[:, t, 0:64], idf32[:, :]))
                ktr_t[t] = cT.n

            if 4 <= r <= NT + 3:
                t = r - 4

                @block.vector
                def _(v, t=t):
                    v.wait_ge(cT.h, ktr_t[t])
                    ktp = PD[0:64, 0:128] if t % 2 == 0 else PD[0:64, 512:640]
                    v.tensor_copy(kT[0:64, 128 * t:128 * (t + 1)], ktp)
                    kt_v[t] = cV.n

        ln_last_v = cV.n

        @block.sync
        def _(sy):
            for bq4 in range(4):
                sy.wait_ge(dLx[bq4].h, dLx[bq4].n)
            sy.wait_ge(dWq.h, dWq.n)
            dStash.inc(sy.dma_start(out=wq_dram.ap(), in_=regR[:, 0:8192]))
            sy.wait_ge(dWo.h, dWo.n)
            dStash.inc(sy.dma_start(out=wo_dram.ap(), in_=regR[:, 8192:16384]))
            for k in range(E * 512 // 128):
                dPre.inc(sy.dma_start(
                    out=xg_dram.ap()[128 * k:128 * (k + 1), :],
                    in_=regR[:, 0:1024]))

        if "xnT" in DBG:
            @block.sync
            def _(sy):
                sy.wait_ge(cV.h, ln_last_v)
                dDbg.inc(sy.dma_start(out=dbg_hbm["xnT"], in_=xnT[:, :, :]))

        def emit_rope(v, view3, nh, t):
            cosA = costab_bf[:, t:t + 1, 0:16].broadcast_to([128, nh, 16])
            cosB = costab_bf[:, t:t + 1, 16:32].broadcast_to([128, nh, 16])
            sinA = sintab_bf[:, t:t + 1, 0:16].broadcast_to([128, nh, 16])
            sinB = sintab_bf[:, t:t + 1, 16:32].broadcast_to([128, nh, 16])
            scr = rope_scr[:, 0:nh, :]
            v.tensor_tensor(scr[:, :, 32:48], view3[:, :, 16:32], sinA, ALU.mult)
            v.tensor_tensor(scr[:, :, 0:16], view3[:, :, 0:16], cosA, ALU.mult)
            v.tensor_sub(scr[:, :, 0:16], scr[:, :, 0:16], scr[:, :, 32:48])
            v.tensor_tensor(scr[:, :, 48:64], view3[:, :, 0:16], sinB, ALU.mult)
            v.tensor_tensor(scr[:, :, 16:32], view3[:, :, 16:32], cosB, ALU.mult)
            v.tensor_add(scr[:, :, 16:32], scr[:, :, 16:32], scr[:, :, 48:64])
            v.tensor_copy(view3[:, :, 0:32], scr[:, :, 0:32])

        @block.vector
        def _(v):
            v.wait_ge(cS.h, lg_sca[NT - 1])
            v.reciprocal(zrow[:, :], zrow[:, :])
            zB = zrow[:, :].rearrange("p (a e) -> p a e", e=1).broadcast_to([128, NT, E])
            v.tensor_tensor(probs[:, :, :], probs[:, :, :], zB, ALU.mult)
            v.tensor_copy(vstat[:, :, 0:64], kvtm[:, :, 64:128])

        @block.vector
        def _(v):
            v.wait_ge(cV.h, kt_v[NT - 1])
            v.tensor_copy(kT[64:128, :], kT[0:64, :])
            v.tensor_copy(probs_em[:, :, :].rearrange("p e a -> p a e"),
                          probs[:, :, :])

        kv_last_v = cV.n
        probs_last_s = cS.n

        @block.gpsimd
        def _(g):
            g.wait_ge(cV.h, kv_last_v)
            g.memset(stricL, 1.0)
            g.wait_ge(cP.h, cP.n)
            g.affine_select(out=stricL, in_=stricL, compare_op=ALU.is_gt,
                            fill=0.0, base=0, pattern=[[1, 128]], channel_multiplier=-1)
            g.iota(eidx8, [[1, E]], channel_multiplier=0,
                   allow_small_or_imprecise_dtypes=True)
        sidx_p = cP.n

        @block.sync
        def _(sy):
            sy.wait_ge(cV.h, kv_last_v)
            dPr.inc(sy.dma_start(out=ag_in.ap(), in_=probs_em[:, :, :]))

        @block.gpsimd
        def _(g):
            g.wait_ge(dPr.h, 16)
            ccc.inc(g.collective_compute(
                "AllGather", ALU.bypass,
                replica_groups=[list(range(NCORES))],
                ins=[ag_in.ap()], outs=[ag_out.ap()]))

        @block.sync
        def _(sy):
            sy.wait_ge(ccc.h, 1)
            for e in range(E):
                dPr.inc(sy.dma_start(
                    out=probs_all[:, e, :].rearrange("p (r a) -> p r a", r=NCORES),
                    in_=ag_out.ap()[:, :, e:e + 1, :]
                    .rearrange("r p e a -> p r (e a)")))
        pall_done = dPr.n

        for name, gate in (("kT", kv_last_v), ("vstat", kv_last_v),
                           ("kvtm", kv_last_v), ("probs", kv_last_v)):
            if name in DBG:
                src = {"kT": kT[:, :], "vstat": vstat[:, :, :],
                       "kvtm": kvtm[:, :, :], "probs": probs[:, :, :]}[name]

                @block.sync
                def _(sy, name=name, src=src, gate=gate):
                    sy.wait_ge(cV.h, gate)
                    dDbg.inc(sy.dma_start(out=dbg_hbm[name], in_=src))
        if "probs_all" in DBG:
            @block.sync
            def _(sy):
                sy.wait_ge(dPr.h, pall_done)
                dDbg.inc(sy.dma_start(out=dbg_hbm["probs_all"], in_=probs_all[:, :, :]))

        cnt_ps = PD[:, 512:520]

        class Bisect:
            def __init__(self):
                self.started = 0
                self.done = 0
                self.stage = "idle"
                self.red_cv = 0
                self.mm_ct = 0
                self.first = True
                self.enabled = False

            def pre(self, v):
                if not self.enabled:
                    return
                if self.stage != "idle" or self.started >= BIS_ITERS:
                    return
                if self.first:
                    v.wait_ge(dPr.h, pall_done)
                    self.first = False
                tauB = tau[:, :].rearrange("p (e a) -> p e a", a=1) \
                    .broadcast_to([128, E, 128])
                with nc.allow_low_precision(reason="counts <=128 exact in bf16"):
                    v.tensor_tensor(cmpb[:, :, :], probs_all[:, :, :], tauB,
                                    ALU.is_ge)
                    v.reduce_sum(cntpp[:, :], cmpb[:, :, :], axis=AX.X)
                self.red_cv = cV.n
                self.started += 1
                self.stage = "cmp"

            def mm(self, tn):
                if not self.enabled or self.stage != "cmp":
                    return
                tn.wait_ge(cV.h, self.red_cv)
                cT.inc(tn.matmul(cnt_ps, ones_128_bf[:, :], cntpp[:, :],
                                 start=True, stop=True))
                self.mm_ct = cT.n
                self.stage = "mm"

            def upd(self, v):
                if not self.enabled or self.stage != "mm":
                    return
                v.wait_ge(cT.h, self.mm_ct)
                v.tensor_single_scalar(geb[:, :], cnt_ps, CAP, ALU.is_ge)
                v.tensor_single_scalar(geb_inv[:, :], cnt_ps, CAP, ALU.is_lt)
                v.copy_predicated(lo[:, :], geb[:, :], tau[:, :])
                v.copy_predicated(hi[:, :], geb_inv[:, :], tau[:, :])
                v.tensor_add(tau[:, :], lo[:, :], hi[:, :])
                v.tensor_scalar_mul(tau[:, :], tau[:, :], 0.5)
                self.done += 1
                self.stage = "idle"

        bis = Bisect()
        gates_done_v = [0]

        def emit_gates(v):
            loB = lo[:, :].rearrange("p (a e) -> p a e", a=1).broadcast_to([128, NT, E])
            v.tensor_tensor(gsel[:, :, :], probs[:, :, :], loB, ALU.is_ge)
            v.scalar_tensor_tensor(gsel[:, :, :], gsel[:, :, :], 2.0, probs[:, :, :],
                                   op0=ALU.mult, op1=ALU.add)
            v.reduce_max(gmax[:, :], gsel[:, :, :], axis=AX.X)
            gmB = gmax[:, :].rearrange("p (a e) -> p a e", e=1).broadcast_to([128, NT, E])
            v.tensor_tensor(gsel[:, :, :], gsel[:, :, :], gmB, ALU.is_equal)
            v.tensor_mul(gates[:, :, :], gsel[:, :, :], probs[:, :, :])
            gates_done_v[0] = cV.n

        qev_s = {}
        qtr_t = {}
        last_exp_pair = [0]
        qevq_v = {}
        oev_v = {}
        last_S_mm = [0]
        attn_tt_v = {}
        zi_evac_v = {}
        zi_mm_last = [0]
        out_store_d = {}
        w2_arr = {}
        w1_last_ct = {}
        w2_last_ct = {}
        stt_v = {}

        wq_view = [None]

        def emit_qproj(w):
            wqv = wq_view[0] if wq_view[0] is not None else wq_sb
            for t in range(4 * w, 4 * w + 4):
                b = t % 2
                qps = PA if b == 0 else PB
                tps = PC if b == 0 else PD

                @block.tensor
                def _(tn, t=t, qps=qps, w=w):
                    tn.wait_ge(w_q_done[0][0].h, w_q_done[0][1])
                    if w == 0 and t < 2:
                        tn.wait_ge(cS.h, probs_last_s)
                    prev = qev_s.get(t - 2)
                    if prev is not None:
                        tn.wait_ge(cS.h, prev)
                    prevo = oev_v.get(t - 2)
                    if prevo is not None:
                        tn.wait_ge(cV.h, prevo)
                    ts = slice(128 * t, 128 * (t + 1))
                    tn.matmul(qps[:, 0:512], ones_1x128_bf[:, :], bq_row[:, 0:512],
                              start=True, stop=False)
                    tn.matmul(qps[:, 512:1024], ones_1x128_bf[:, :], bq_row[:, 512:1024],
                              start=True, stop=False)
                    for d in range(ND):
                        i0 = tn.matmul(qps[:, 0:512], xnT[:, d, ts], wqv[:, d, 0:512],
                                       start=False, stop=(d == ND - 1))
                        i1 = tn.matmul(qps[:, 512:1024], xnT[:, d, ts],
                                       wqv[:, d, 512:1024],
                                       start=False, stop=(d == ND - 1))
                    cT.inc(i0)
                    cT.inc(i1)
                qmm = cT.n

                @block.scalar
                def _(s, t=t, b=b, qps=qps, qmm=qmm, w=w):
                    s.wait_ge(cT.h, qmm)
                    prev = qtr_t.get(t - 2)
                    if prev is not None:
                        s.wait_ge(cT.h, prev)
                    if w == 0 and t < 2:
                        s.wait_ge(cV.h, ln_last_v)
                    s.activation(q_tm[b], qps[:, :], AF.Identity)
                    qev_s[t] = cS.n

                @block.vector
                def _(v, t=t, b=b, w=w):
                    v.wait_ge(cS.h, qev_s[t])
                    emit_rope(v, q_tm3[b], 16, t)
                    v.engine_nop()
                qrope = cV.n

                @block.tensor
                def _(tn, t=t, b=b, tps=tps, qrope=qrope, w=w):
                    tn.wait_ge(cV.h, qrope)
                    prev = qevq_v.get(t - 2)
                    if prev is not None:
                        tn.wait_ge(cV.h, prev)
                    if w == 0 and t < 2:
                        tn.wait_ge(cV.h, kt_v[NT - 1])
                    tpsb = tps[:, :].bitcast(BF16)[:, 0:1024]
                    for c in range(ND):
                        ins = tn.transpose(tpsb[:, 128 * c:128 * (c + 1)],
                                           q_tm[b][:, 128 * c:128 * (c + 1)],
                                           idbf[:, :])
                    cT.inc(ins)
                    qtr_t[t] = cT.n

                @block.vector
                def _(v, t=t, b=b, tps=tps, w=w):
                    v.wait_ge(cT.h, qtr_t[t])
                    if last_S_mm[0]:
                        v.wait_ge(cT.h, last_S_mm[0])
                    toff = 128 * (t - 4 * w)
                    tpsb = tps[:, :].bitcast(BF16)[:, 0:1024]
                    v.tensor_copy(
                        qTq[:, :].rearrange("p (c u) -> p c u", c=ND)[:, :, toff:toff + 128],
                        tpsb.rearrange("p (c u) -> p c u", c=ND))
                    qevq_v[t] = cV.n

        def emit_heads(w, pair_hook=None):
            nj = 4 * w + 4
            pend_mm = []
            pend_fin = []
            rz_free_ct = [0]

            def c0_of(j):
                k = j - 4 * w
                return 128 * k if k > 0 else 0

            def inject_znorm(tn):
                if pend_mm:
                    pw, ph, hh, rzc = pend_mm.pop(0)
                    tn.wait_ge(cS.h, rzc)
                    prev_zi = zi_evac_v.get(("z", pw, ph - 1)) or zi_evac_v.get("last")
                    if prev_zi:
                        tn.wait_ge(cV.h, prev_zi)
                    cT.inc(tn.matmul(PD[0:64, 0:512], ones_1x128_bf[:, 0:64],
                                     rz_pair[:, 512 * hh:512 * (hh + 1)],
                                     start=True, stop=True))
                    pend_fin.append((pw, ph, hh, cT.n))
                    rz_free_ct[0] = cT.n

            def finish_znorm(v):
                if pend_fin:
                    pw, ph, hh, zic = pend_fin.pop(0)
                    v.wait_ge(cT.h, zic)
                    v.tensor_copy(zi_sb[:, :], PD[0:64, 0:512])
                    po = PC[0:65, 512 * hh:512 * (hh + 1)]
                    v.tensor_tensor(
                        attnTq[(ph % 2) * 64:(ph % 2) * 64 + 64,
                               512 * (ph // 2):512 * (ph // 2) + 512],
                        po[0:64, :], zi_sb[:, :], ALU.mult)
                    attn_tt_v[(pw, ph)] = cV.n
                    zi_evac_v["last"] = cV.n

            jlist = list(range(nj - 1, -1, -1))

            prev_pvl = [0]
            for hp in range(H // 2):
                h0, h1 = 2 * hp, 2 * hp + 1
                ops = [PC[0:65, 0:512], PC[0:65, 512:1024]]
                spair = [(PA[:, 0:512], PA[:, 512:1024]),
                         (PB[:, 0:512], PB[:, 512:1024])]
                exp_cs = {}
                mask_cv = {}
                pv_ct = {}

                @block.vector
                def _(v, hp=hp, w=w):
                    if w == 0 and hp == 0:
                        for bb in range(2):
                            v.wait_ge(dXn[bb].h, dXn[bb].n)
                    if prev_pvl[0]:
                        v.wait_ge(cT.h, prev_pvl[0])
                    for si, cz in ((0, 384), (1, 256), (2, 128)):
                        v.memset(pbuf[si][:, :].rearrange(
                            "p (h c) -> p h c", h=2)[:, :, 0:cz], 0.0)
                    finish_znorm(v)
                    bis.upd(v)
                    bis.pre(v)
                pb_zero = cV.n

                planned = {}
                pn = cS.n
                for i in range(nj):
                    j = jlist[i]
                    pn += 1 if c0_of(j) == 0 else 2
                    planned[j] = pn
                rzc_planned = pn + 4

                for i0 in range(0, nj, 2):
                    @block.tensor
                    def _(tn, i0=i0, hp=hp, h0=h0, h1=h1, w=w):
                        tn.wait_ge(cV.h, qevq_v[4 * w + 3])
                        if hp == 0 and i0 == 0:
                            tn.wait_ge(cS.h, qev_s[4 * w + 3])
                        if i0 >= 4:
                            for i in (i0 - 4, i0 - 3):
                                jp = jlist[i]
                                wg_ = mask_cv.get(jp)
                                if wg_ is not None:
                                    tn.wait_ge(cV.h, wg_)
                                else:
                                    tn.wait_ge(cS.h, planned[jp])
                                if i == 0 and hp > 0:
                                    pa = attn_tt_v.get((w, 2 * hp - 1))
                                    if pa is not None:
                                        tn.wait_ge(cV.h, pa)
                                for hh in (0, 1):
                                    cT.inc(tn.matmul(
                                        ops[hh],
                                        vstat[:, jp, :],
                                        pbuf[i % 4][:, 512 * hh:512 * (hh + 1)],
                                        start=(jp == jlist[0]), stop=(jp == 0)))
                                pv_ct[jp] = cT.n
                        for i in (i0, i0 + 1):
                            j = jlist[i]
                            if i >= 2:
                                tn.wait_ge(cS.h, planned[jlist[i - 2]])
                            elif hp > 0:
                                tn.wait_ge(cS.h, last_exp_pair[0])
                            c0 = c0_of(j)
                            for hh, h in ((0, h0), (1, h1)):
                                kbase = (h % 2) * 64
                                mv = qTq[(h % 2) * 64:(h % 2) * 64 + 64,
                                         512 * (h // 2):512 * (h // 2) + 512]
                                cT.inc(tn.matmul(
                                    spair[j % 2][hh][:, c0:512] if c0 else spair[j % 2][hh],
                                    kT[kbase:kbase + 64, 128 * j:128 * (j + 1)],
                                    mv[:, c0:512] if c0 else mv,
                                    start=True, stop=True))
                            exp_cs[("smm", j)] = cT.n
                        inject_znorm(tn)
                        bis.mm(tn)

                    @block.vector
                    def _(v, i0=i0, hp=hp, w=w):
                        finish_znorm(v)
                        bis.upd(v)
                        bis.pre(v)
                        for i in (i0, i0 + 1):
                            j = jlist[i]
                            if j - 4 * w >= 0:
                                c0 = c0_of(j)
                                v.wait_ge(cS.h, planned[j])
                                pv2 = pbuf[i % 4][:, :].rearrange(
                                    "p (h c) -> p h c", h=2)[:, :, c0:c0 + 128]
                                tm = trimask[:, :].rearrange(
                                    "p (a c) -> p a c", a=1).broadcast_to([128, 2, 128])
                                v.tensor_tensor(pv2, pv2, tm, ALU.mult)
                                mask_cv[j] = cV.n

                for t0 in (nj, nj + 2):
                    @block.tensor
                    def _(tn, t0=t0, hp=hp, w=w):
                        for i in (t0 - 4, t0 - 3):
                            jp = jlist[i]
                            wg_ = mask_cv.get(jp)
                            if wg_ is not None:
                                tn.wait_ge(cV.h, wg_)
                            else:
                                tn.wait_ge(cS.h, planned[jp])
                            if i == 0 and hp > 0:
                                pa = attn_tt_v.get((w, 2 * hp - 1))
                                if pa is not None:
                                    tn.wait_ge(cV.h, pa)
                            for hh in (0, 1):
                                cT.inc(tn.matmul(
                                    ops[hh],
                                    vstat[:, jp, :],
                                    pbuf[i % 4][:, 512 * hh:512 * (hh + 1)],
                                    start=(jp == jlist[0]), stop=(jp == 0)))
                            pv_ct[jp] = cT.n
                pvl = cT.n
                prev_pvl[0] = cT.n
                last_exp_pair[0] = planned[0]

                @block.scalar
                def _(s, hp=hp, w=w, pb_zero=pb_zero, pvl=pvl, nj=nj):
                    for i in range(nj):
                        j = jlist[i]
                        c0 = c0_of(j)
                        if i < 2:
                            s.wait_ge(cV.h, pb_zero)
                        s.wait_ge(cT.h, exp_cs[("smm", j)])
                        if i >= 4:
                            pc = pv_ct.get(jlist[i - 4])
                            if pc is not None:
                                s.wait_ge(cT.h, pc)
                        if c0 == 0:
                            s.activation(pbuf[i % 4][:, :],
                                         PA[:, :] if j % 2 == 0 else PB[:, :],
                                         AF.Exp, scale=0.125, bias=negc[:, :])
                        else:
                            for hh in (0, 1):
                                s.activation(pbuf[i % 4][:, 512 * hh + c0:512 * (hh + 1)],
                                             spair[j % 2][hh][:, c0:512],
                                             AF.Exp, scale=0.125, bias=negc[:, :])
                    s.wait_ge(cT.h, pvl)
                    if rz_free_ct[0]:
                        s.wait_ge(cT.h, rz_free_ct[0])
                    for hh in (0, 1):
                        s.activation(zln_sb[:, :], PC[64:65, 512 * hh:512 * (hh + 1)],
                                     AF.Ln)
                        with nc.allow_low_precision(reason="1/Z bf16; attnT is bf16"):
                            s.activation(rz_pair[:, 512 * hh:512 * (hh + 1)],
                                         zln_sb[:, :], AF.Exp, scale=-1.0)
                assert cS.n == rzc_planned, (cS.n, rzc_planned)
                rzc = cS.n
                pend_mm.append((w, h0, 0, rzc))
                pend_mm.append((w, h1, 1, rzc))
                if pair_hook:
                    pair_hook()

            for _ in range(2):
                @block.tensor
                def _(tn):
                    inject_znorm(tn)

                @block.vector
                def _(v):
                    finish_znorm(v)
            last_S_mm[0] = cT.n

        def emit_oproj(w):
            half = w // 2
            att_done = cV.n
            for t in range(4 * w, 4 * w + 4):
                b = t % 2
                ops = PA if b == 0 else PB
                th = t - 8 * half

                @block.tensor
                def _(tn, t=t, ops=ops, w=w, att_done=att_done):
                    tn.wait_ge(w_o_done[0][0].h, w_o_done[0][1])
                    tn.wait_ge(cV.h, att_done)
                    prev = oev_v.get(t - 2)
                    if prev is not None:
                        tn.wait_ge(cV.h, prev)
                    toff = 128 * (t - 4 * w)
                    tn.matmul(ops[:, 0:512], ones_1x128_bf[:, :], bo_row[:, 0:512],
                              start=True, stop=False)
                    tn.matmul(ops[:, 512:1024], ones_1x128_bf[:, :], bo_row[:, 512:1024],
                              start=True, stop=False)
                    for c in range(ND):
                        stat = attnTq[:, 512 * c + toff:512 * c + toff + 128]
                        i0 = tn.matmul(ops[:, 0:512], stat, wo_sb[:, c, 0:512],
                                       start=False, stop=(c == ND - 1))
                        i1 = tn.matmul(ops[:, 512:1024], stat, wo_sb[:, c, 512:1024],
                                       start=False, stop=(c == ND - 1))
                    cT.inc(i0)
                    cT.inc(i1)
                omm = cT.n

                @block.vector
                def _(v, t=t, ops=ops, omm=omm, th=th, half=half):
                    v.wait_ge(cT.h, omm)
                    if half == 1:
                        prev_store = out_store_d.get((0, th))
                        if prev_store is not None:
                            v.wait_ge(dOutT[th].h, prev_store)
                    v.tensor_copy(acc[:, th, :], ops[:, :])
                    oev_v[t] = cV.n

        sp = {}

        def emit_indices_and_scatter():
            @block.vector
            def _(v):
                v.tensor_copy(flagsF.rearrange("p (e a) -> p e a", e=E),
                              gsel[:, :, :].rearrange("p a e -> p e a"))
                with nc.allow_low_precision(reason="0/1 flags exact in bf16"):
                    v.tensor_copy(flagsE, flagsF)
                v.tensor_scalar_mul(e512[:, :], eidx8, 512.0)
            fl_v = cV.n

            @block.tensor
            def _(tn):
                tn.wait_ge(cV.h, fl_v)
                tn.wait_ge(cP.h, sidx_p)
                tn.wait_ge(cS.h, last_exp_pair[0])
                fE = flagsE.rearrange("p (e a) -> p e a", e=E)
                for e in range(E):
                    cT.inc(tn.matmul(PA[:, 16 * e:16 * (e + 1)], stricL,
                                     fE[:, e, :], start=True, stop=True))
                cT.inc(tn.matmul(PB[:, 0:128], ones_128_bf[:, :], flagsE,
                                 start=True, stop=True))
            idx_mm = cT.n

            @block.vector
            def _(v):
                v.wait_ge(cT.h, idx_mm)
                v.tensor_copy(slotsel, PB[:, 0:128])
                cp3 = colpfx_sb.rearrange("p (e a) -> p e a", e=E)
                cs3 = slotsel.rearrange("p (e a) -> p e a", e=E)
                for e in range(E):
                    v.tensor_tensor_scan(cp3[:, e, :], cs3[:, e, :], cs3[:, e, :],
                                         0.0, op0=ALU.add, op1=ALU.bypass)
                v.tensor_sub(colpfx_sb, colpfx_sb, slotsel)
                v.tensor_add(slotsel, PA[:, 0:128], colpfx_sb)
                v.tensor_mul(slotsel, slotsel, flagsF)
                sv = slotsel.rearrange("p (e a) -> p a e", e=E)
                v.reduce_sum(slot_f, sv, axis=AX.X)
                v.tensor_scalar_min(slot_f, slot_f, 511.0)
                e5B = e512[:, :].rearrange("p (e a) -> p e a", a=1).broadcast_to([128, E, 16])
                v.tensor_tensor(slotsel.rearrange("p (e a) -> p e a", e=E),
                                flagsF.rearrange("p (e a) -> p e a", e=E),
                                e5B, ALU.mult)
                v.reduce_sum(e_f, sv, axis=AX.X)
                v.tensor_add(inv_f, slot_f, e_f)
                v.tensor_copy(inv_i32, inv_f)
                v.reduce_max(g_tok, gates[:, :, :], axis=AX.X)
            sp["idx_v"] = cV.n

            for t in range(NT):
                b = t % 2

                @block.sync
                def _(sy, t=t, b=b):
                    if t < 2:
                        for bb in range(2):
                            sy.wait_ge(dXn[bb].h, dXn[bb].n)
                    if t >= 2:
                        sy.wait_ge(dSc[b].h, 16 * (t // 2))
                    dXs[b].inc(sy.dma_start(out=xsc_sb[b],
                                            in_=xn_dram.ap()[128 * t:128 * (t + 1), :]))

                @block.gpsimd
                def _(g, t=t, b=b):
                    g.wait_ge(dXs[b].h, dXs[b].n)
                    if t == 0:
                        g.wait_ge(cV.h, sp["idx_v"])
                        g.wait_ge(dPre.h, dPre.n)
                    dSc[b].inc(g.indirect_dma_start(
                        out=xg_dram.ap(),
                        out_offset=IndirectOffsetOnAxis(inv_i32[:, t:t + 1], 0),
                        in_=xsc_sb[b],
                        in_offset=None))
            sp["sc0"] = dSc[0].n
            sp["sc1"] = dSc[1].n

        def emit_sparse_ffn():
            tr_v = {}
            last_fg = {}
            gelu_s = {}
            w2_v = {}
            st_d = {}
            hps4 = [PA[:, 0:512], PA[:, 512:1024], PB[:, 0:512], PB[:, 512:1024]]
            xgT3 = xgT.rearrange("p (d u) -> p d u", d=ND)
            def emit_xg_load(e):
                @block.sync
                def _(sy, e=e):
                    if e == 0:
                        sy.wait_ge(dSc[0].h, sp["sc0"])
                        sy.wait_ge(dSc[1].h, sp["sc1"])
                    for a in range(4):
                        if e > 0:
                            sy.wait_ge(cT.h, tr_v[("mm", e - 1, a)])
                        dXgA[a].inc(sy.dma_start(
                            out=xg_sb[:, 1024 * a:1024 * (a + 1)],
                            in_=xg_dram.ap()[512 * e + 128 * a:
                                             512 * e + 128 * (a + 1), :]))
                        sp[("xg", e, a)] = dXgA[a].n

            emit_xg_load(0)
            for e in range(E):
                for a in range(4):
                    @block.tensor
                    def _(tn, e=e, a=a):
                        tn.wait_ge(dXgA[a].h, sp[("xg", e, a)])
                        if a == 0 and e == 0:
                            tn.wait_ge(cV.h, sp["h1_znorm_cv"])
                        pcb = ((PC, PD)[a % 2])[:, :].bitcast(BF16)[:, 0:1024]
                        prev = tr_v.get((e, a - 2))
                        if prev is not None:
                            tn.wait_ge(cV.h, prev)
                        if a < 2 and e > 0:
                            prev = w2_v.get((e - 1, 2 + a))
                            if prev is not None:
                                tn.wait_ge(cV.h, prev)
                        for c in range(ND):
                            ins = tn.transpose(pcb[:, 128 * c:128 * (c + 1)],
                                               xg_sb[:, 1024 * a + 128 * c:
                                                     1024 * a + 128 * (c + 1)],
                                               idbf[:, :])
                        cT.inc(ins)
                        tr_v[("mm", e, a)] = cT.n

                    @block.vector
                    def _(v, e=e, a=a):
                        v.wait_ge(cT.h, tr_v[("mm", e, a)])
                        pcb = ((PC, PD)[a % 2])[:, :].bitcast(BF16)[:, 0:1024]
                        v.tensor_copy(xgT3[:, :, 128 * a:128 * (a + 1)],
                                      pcb.rearrange("p (c u) -> p c u", c=ND))
                        tr_v[(e, a)] = cV.n
                sp[("tr", e)] = cT.n
                tr_all = cV.n

                g_base = cS.n
                for f in range(NF):
                    gelu_s[(e, f)] = g_base + f + 1
                for f4 in range(0, NF, 4):
                    @block.tensor
                    def _(tn, e=e, f4=f4, tr_all=tr_all):
                        tn.wait_ge(dW1.h, w1_arr[(0, e)])
                        tn.wait_ge(cV.h, tr_all)
                        for f in range(f4, f4 + 4):
                            if f - 4 >= 0:
                                tn.wait_ge(cS.h, gelu_s[(e, f - 4)])
                            for d in range(ND):
                                ins = tn.matmul(hps4[f % 4],
                                                w1_sb[:, d, 128 * f:128 * (f + 1)],
                                                xgT3[:, d, :],
                                                start=(d == 0), stop=(d == ND - 1))
                            cT.inc(ins)
                            gelu_s[("mm", e, f)] = cT.n
                sp[("w1", e)] = cT.n
                if e + 1 < E:
                    emit_xg_load(e + 1)

                @block.scalar
                def _(s, e=e):
                    for f in range(NF):
                        s.wait_ge(cT.h, gelu_s[("mm", e, f)])
                        s.activation(hG[:, 512 * f:512 * (f + 1)],
                                     hps4[f % 4], AF.Gelu)
                assert cS.n == gelu_s[(e, NF - 1)], (cS.n, e)

                nxt = e + 1
                if nxt < E:
                    @block.gpsimd
                    def _(g, e=e, nxt=nxt):
                        g.wait_ge(cT.h, sp[("w1", e)])
                        for d in range(ND):
                            dW1.inc(g.dma_start(out=w1_sb[:, d, :],
                                                in_=w1_hbm[nxt, 128 * d:128 * (d + 1), :]))
                    w1_arr[(0, nxt)] = dW1.n

                glast = gelu_s[(e, NF - 1)]
                for a in range(4):
                    @block.tensor
                    def _(tn, e=e, a=a, glast=glast):
                        tn.wait_ge(cS.h, glast)
                        tn.wait_ge(dW2.h, w2_arr[(0, e)])
                        if e == 0 and a < 2:
                            tn.wait_ge(cV.h, sp["routing_cv"])
                        wps = PC if a % 2 == 0 else PD
                        prev = w2_v.get((e, a - 2))
                        if prev is not None:
                            tn.wait_ge(cV.h, prev)
                        for f in range(NF):
                            i0 = tn.matmul(wps[:, 0:512],
                                           hG[:, 512 * f + 128 * a:512 * f + 128 * (a + 1)],
                                           w2_sb[:, f, 0:512],
                                           start=(f == 0), stop=(f == NF - 1))
                            i1 = tn.matmul(wps[:, 512:1024],
                                           hG[:, 512 * f + 128 * a:512 * f + 128 * (a + 1)],
                                           w2_sb[:, f, 512:1024],
                                           start=(f == 0), stop=(f == NF - 1))
                        cT.inc(i0)
                        cT.inc(i1)
                        w2_v[("mm", e, a)] = cT.n

                    @block.vector
                    def _(v, e=e, a=a):
                        v.wait_ge(cT.h, w2_v[("mm", e, a)])
                        ls = last_fg.get(a % 2)
                        if ls is not None:
                            v.wait_ge(dFgP[a % 2].h, ls)
                        wps = PC if a % 2 == 0 else PD
                        with nc.allow_low_precision(reason="ffn out staged bf16 like other matmul operands"):
                            v.tensor_copy(ffg2[a % 2], wps[:, :])
                        w2_v[(e, a)] = cV.n

                    @block.sync
                    def _(sy, e=e, a=a):
                        sy.wait_ge(cV.h, w2_v[(e, a)])
                        dFgP[a % 2].inc(sy.dma_start(
                            out=fg_dram.ap()[512 * e + 128 * a:
                                             512 * e + 128 * (a + 1), :],
                            in_=ffg2[a % 2]))
                        last_fg[a % 2] = dFgP[a % 2].n
                sp[("w2", e)] = cT.n

                nxt = e + 1
                if nxt < E:
                    @block.gpsimd
                    def _(g, e=e, nxt=nxt):
                        g.wait_ge(cT.h, sp[("w2", e)])
                        for f in range(NF):
                            dW2.inc(g.dma_start(out=w2_sb[:, f, :],
                                                in_=w2_hbm[nxt, 128 * f:128 * (f + 1), :]))
                    w2_arr[(0, nxt)] = dW2.n
            sp["ffn_ct"] = cT.n
            sp["fg_d0"] = dFgP[0].n
            sp["fg_d1"] = dFgP[1].n

        epi_seq = []
        epi_state = {"pf": 0, "st": 0}
        epi_gath = {}
        add_done = {}

        def epi_push(half):
            for th in range(8):
                epi_seq.append((half, th))

        def epi_prefetch(k=1):
            while k > 0 and epi_state["pf"] < len(epi_seq):
                i = epi_state["pf"]
                if i >= 2 and (i - 2) not in add_done:
                    break
                half, th = epi_seq[i]
                tglob = 8 * half + th
                b = i % 2

                @block.gpsimd
                def _(g, i=i, tglob=tglob, b=b):
                    g.wait_ge(dFgP[0].h, sp["fg_d0"])
                    g.wait_ge(dFgP[1].h, sp["fg_d1"])
                    prev = add_done.get(i - 2)
                    if prev is not None:
                        g.wait_ge(cV.h, prev)
                    dFo[b].inc(g.indirect_dma_start(
                        out=fo_sb[b],
                        out_offset=None,
                        in_=fg_dram.ap(),
                        in_offset=IndirectOffsetOnAxis(inv_i32[:, tglob:tglob + 1], 0)))

                @block.sync
                def _(sy, i=i, tglob=tglob, b=b):
                    sy.wait_ge(cV.h, sp["routing_cv"])
                    sy.wait_ge(dSc[0].h, sp["sc0"])
                    sy.wait_ge(dSc[1].h, sp["sc1"])
                    prev = add_done.get(i - 2)
                    if prev is not None:
                        sy.wait_ge(cV.h, prev)
                    for q in range(4):
                        dXr[b].inc(sy.dma_start(
                            out=xrel[b][:, 256 * q:256 * (q + 1)],
                            in_=x_hbm[128 * tglob:128 * (tglob + 1),
                                      256 * q:256 * (q + 1)]))
                epi_gath[i] = (dFo[b].n, dXr[b].n)
                epi_state["pf"] += 1
                k -= 1

        def epi_step():
            i = epi_state["st"]
            if i >= len(epi_seq) or i >= epi_state["pf"]:
                return False
            half, th = epi_seq[i]
            tglob = 8 * half + th
            if oev_v.get(tglob) is None:
                return False
            b = i % 2
            fod, xrd = epi_gath[i]

            @block.vector
            def _(v, th=th, tglob=tglob, b=b, fod=fod, xrd=xrd):
                v.wait_ge(dFo[b].h, fod)
                v.wait_ge(dXr[b].h, xrd)
                v.wait_ge(cV.h, oev_v[tglob])
                v.scalar_tensor_tensor(acc[:, th, :], fo_sb[b],
                                       g_tok[:, tglob:tglob + 1], acc[:, th, :],
                                       op0=ALU.mult, op1=ALU.add)
                v.tensor_add(acc[:, th, :], acc[:, th, :], xrel[b])
            add_done[i] = cV.n

            @block.sync
            def _(sy, half=half, th=th, tglob=tglob, adv=add_done[i]):
                sy.wait_ge(cV.h, adv)
                dOutT[th].inc(sy.dma_start(
                    out=out_hbm[128 * tglob:128 * (tglob + 1), :],
                    in_=acc[:, th, :]))
                out_store_d[(half, th)] = dOutT[th].n
            epi_state["st"] += 1
            epi_prefetch(1)
            return True

        def epi_hook():
            epi_step()

        emit_qproj(0)
        bis_pair = [0]

        def bis_enable_hook():
            bis_pair[0] += 1
            if bis_pair[0] >= 3:
                bis.enabled = True

        emit_heads(0, pair_hook=bis_enable_hook)
        emit_oproj(0)
        bis.enabled = True
        emit_qproj(1)
        emit_heads(1)
        sp["h1_znorm_cv"] = cV.n

        while bis.done < BIS_ITERS:
            @block.vector
            def _(v):
                bis.upd(v)
                bis.pre(v)

            @block.tensor
            def _(tn):
                bis.mm(tn)

        @block.vector
        def _(v):
            bis.upd(v)
            emit_gates(v)
        sp["routing_cv"] = cV.n

        if "tau" in DBG:
            @block.sync
            def _(sy):
                sy.wait_ge(cV.h, gates_done_v[0])
                dDbg.inc(sy.dma_start(out=dbg_hbm["tau"], in_=lo[:, :]))
        if "gates" in DBG:
            @block.sync
            def _(sy):
                sy.wait_ge(cV.h, gates_done_v[0])
                dDbg.inc(sy.dma_start(out=dbg_hbm["gates"], in_=gates[:, :, :]))

        emit_indices_and_scatter()
        emit_oproj(1)

        oproj_h0_ct = cT.n

        @block.gpsimd
        def _(g):
            g.wait_ge(cT.h, oproj_h0_ct)
            g.wait_ge(dStash.h, dStash.n)
            for f in range(NF):
                dW2.inc(g.dma_start(out=w2_sb[:, f, :],
                                    in_=w2_hbm[0, 128 * f:128 * (f + 1), :]))
        w2_arr[(0, 0)] = dW2.n

        emit_sparse_ffn()

        @block.gpsimd
        def _(g):
            g.wait_ge(dStash.h, dStash.n)
            g.wait_ge(cT.h, sp[("w1", E - 1)])
            for k in range(4):
                dWq2.inc(g.dma_start(out=regB[:, 2048 * k:2048 * (k + 1)],
                                     in_=wq_dram.ap()[:, 2048 * k:2048 * (k + 1)]))

        @block.gpsimd
        def _(g):
            g.wait_ge(cT.h, sp["ffn_ct"])
            for k in range(4):
                dWo2.inc(g.dma_start(out=regR[:, 8192 + 2048 * k:
                                              8192 + 2048 * (k + 1)],
                                     in_=wo_dram.ap()[:, 2048 * k:2048 * (k + 1)]))
        w_q_done[0] = (dWq2, dWq2.n)
        w_o_done[0] = (dWo2, dWo2.n)
        wq_view[0] = wq_sb2

        epi_push(0)
        epi_prefetch(2)
        emit_qproj(2)
        emit_heads(2, pair_hook=epi_hook)
        while epi_state["st"] < 8:
            epi_prefetch(1)
            assert epi_step(), "half-0 epilogue pump stalled"
        emit_oproj(2)
        epi_push(1)
        epi_prefetch(1)
        emit_qproj(3)
        emit_heads(3, pair_hook=epi_hook)
        emit_oproj(3)
        guard = 0
        while epi_state["st"] < 16:
            epi_prefetch(1)
            epi_step()
            guard += 1
            assert guard < 200, "epilogue drain stalled"

        @block.sync
        def _(sy):
            for th in range(8):
                sy.wait_ge(dOutT[th].h, dOutT[th].n)
            if dDbg.n:
                sy.wait_ge(dDbg.h, dDbg.n)

    return nc


def kernel(**inputs):
    x = np.ascontiguousarray(np.asarray(inputs["L0_attn_x"], dtype=np.float32))
    wq = np.ascontiguousarray(np.asarray(inputs["L0_attn_Wq"], dtype=np.float32))
    bq = np.ascontiguousarray(np.asarray(inputs["L0_attn_bq"], dtype=np.float32)).reshape(1, DM)
    wk = np.ascontiguousarray(np.asarray(inputs["L0_attn_Wk"], dtype=np.float32))
    bk = np.ascontiguousarray(np.asarray(inputs["L0_attn_bk"], dtype=np.float32)).reshape(1, HD)
    wv = np.ascontiguousarray(np.asarray(inputs["L0_attn_Wv"], dtype=np.float32))
    bv = np.ascontiguousarray(np.asarray(inputs["L0_attn_bv"], dtype=np.float32)).reshape(1, HD)
    wo = np.ascontiguousarray(np.asarray(inputs["L0_attn_Wo"], dtype=np.float32))
    bo = np.ascontiguousarray(np.asarray(inputs["L0_attn_bo"], dtype=np.float32)).reshape(1, DM)
    wg = np.ascontiguousarray(np.asarray(inputs["L0_ffn_Wg"], dtype=np.float32))
    w1 = np.ascontiguousarray(np.asarray(inputs["L0_ffn_W1"], dtype=np.float32))
    w2 = np.ascontiguousarray(np.asarray(inputs["L0_ffn_W2"], dtype=np.float32))

    nc = build_nc()
    in_maps = []
    for c in range(NCORES):
        in_maps.append({
            "x": x[c], "wq": wq, "bq": bq, "wk": wk, "bk": bk, "wv": wv, "bv": bv,
            "wo": wo, "bo": bo, "wg": wg, "w1": w1, "w2": w2,
        })
    res = run_bass_kernel_spmd(nc, in_maps, core_ids=list(range(NCORES)))
    return np.stack([res.results[c]["out"] for c in range(NCORES)], axis=0)
